# version 01
# baseline (speedup 1.0000x reference)
import numpy as np
from contextlib import ExitStack

import concourse.bass as bass
import concourse.mybir as mybir
from concourse.bass_utils import run_bass_kernel_spmd

F32 = mybir.dt.float32
BF16 = mybir.dt.bfloat16
AF = mybir.ActivationFunctionType
ALU = mybir.AluOpType
AX = mybir.AxisListType

B, L, D, E, NS, DR, DCV = 4, 4096, 1024, 2048, 16, 64, 4
EXP, FI = 16, 512
TOK = 2048
C = 512
NCH = TOK // C
EJ = E // 128
DCH = D // 128
EPS = 1e-5
INV_D = 1.0 / 1024.0
CONST_VALS = (EPS, INV_D, -1.0, 1.0, 1.702)

ENGINES = ("sync", "scalar", "vector", "tensor", "gpsimd")


class Ring:

    def __init__(self, depth):
        self.depth = depth
        self.hist = []

    def acquire(self):
        i = len(self.hist)
        slot = i % self.depth
        waits = list(self.hist[i - self.depth]) if i >= self.depth else []
        self.hist.append([])
        return slot, waits

    def consumed(self, *miles):
        self.hist[-1].extend(m for m in miles if m)


class Prog:
    def __init__(self):
        self.ops = {e: [] for e in ENGINES}
        self.cnt = {s: 0 for s in ("qS", "qG", "sS", "sV", "sP", "sG")}
        self.ms = {}
        self.wm = {}
        self.esem = {"sync": "qS", "scalar": "sS", "vector": "sV",
                     "tensor": "sP", "gpsimd": "sG"}
        self.pending = {e: [] for e in ENGINES}
        self.pending_counts = {}

    def barrier(self, names):
        for e in ENGINES:
            self.pending[e].extend(names)

    def barrier_counts(self):
        snap = [(s, v) for s, v in self.cnt.items() if v > 0]
        for e in ENGINES:
            self.pending_counts.setdefault(e, []).extend(snap)

    def add(self, eng, fn, after=(), mile=None, dma=False, strict=False):
        names = [m for m in after if m] + self.pending[eng]
        self.pending[eng] = []
        waits = []
        for s, v in self.pending_counts.pop(eng, []):
            own = (s == self.esem[eng]) and s not in ("qS", "qG")
            if own:
                continue
            key = (eng, s)
            if self.wm.get(key, -1) < v:
                self.wm[key] = v
                waits.append((s, v))
        my_sem = "qG" if (eng == "gpsimd" and dma) else self.esem[eng]
        for m in names:
            assert m in self.ms, f"unknown milestone {m}"
            s, v = self.ms[m]
            own = (s == self.esem[eng]) and s not in ("qS", "qG")
            if own and not strict:
                continue
            key = (eng, s)
            if self.wm.get(key, -1) >= v:
                continue
            self.wm[key] = v
            waits.append((s, v))
        inc = None
        if mile is not None:
            amt = 16 if dma else 1
            self.cnt[my_sem] += amt
            self.ms[mile] = (my_sem, self.cnt[my_sem])
            inc = (my_sem, amt)
        self.ops[eng].append((waits, fn, inc))
        return mile

    def emit_all(self, block, semh):
        for name in ENGINES:
            def section(eng_obj, name=name):
                for waits, fn, inc in self.ops[name]:
                    for (s, v) in waits:
                        eng_obj.wait_ge(semh[s], v)
                    inst = fn(eng_obj)
                    if inc is not None:
                        inst.then_inc(semh[inc[0]], inc[1])
            getattr(block, name)(section)


def build(stages=5, dbg=False):
    nc = bass.Bass("TRN2", target_bir_lowering=False)

    def din(name, shape, dtype=F32):
        return nc.dram_tensor(name, shape, dtype, kind="ExternalInput").ap()

    resid_own = din("resid_own", [TOK, D])
    resid_pfx = din("resid_pfx", [TOK, D])
    in_wT = din("in_wT", [D, E])
    skip_wT = din("skip_wT", [D, E])
    normw8 = din("normw8", [128, DCH])
    wcatT = din("wcatT", [E, 96])
    wcatTp = din("wcatTp", [E, 96])
    wd2T_d = din("wd2T", [DR, E])
    wd2b_d = din("wd2b", [128, EJ])
    convw_d = din("convw", [128, EJ, DCV])
    convb_d = din("convb", [128, EJ])
    alog_d = din("alog", [128, EJ, NS])
    wd_d = din("wdvec", [128, EJ])
    out_wT = din("out_wT", [E, D])
    router_d = din("router", [D, EXP])
    gatew_d = din("gatew", [EXP, D, FI])
    upw_d = din("upw", [EXP, D, FI])
    gateb_d = din("gateb", [128, EXP, FI // 128])
    upb_d = din("upb", [128, EXP, FI // 128])
    downw_d = din("downw", [EXP, FI, D])
    downb_d = din("downb", [EXP, D])
    fconst_d = din("fconsts", [128, 8])

    in_bfd = nc.dram_tensor("in_bfd", [DCH, 128, E], BF16, kind="Internal").ap()
    skip_bfd = nc.dram_tensor("skip_bfd", [DCH, 128, E], BF16, kind="Internal").ap()
    out_bfd = nc.dram_tensor("out_bfd", [EJ, 128, D], BF16, kind="Internal").ap()
    x1td = nc.dram_tensor("x1td", [TOK, D], BF16, kind="Internal").ap()
    out_d = nc.dram_tensor("out", [TOK, D], F32, kind="ExternalOutput").ap()
    dbg_t = {}
    if dbg:
        for nm, shape, dty in (
                ("dbg_xbf", [128, EJ, C], BF16), ("dbg_dlr", [96, C], BF16),
                ("dbg_delta", [128, 2, C], F32), ("dbg_y", [128, EJ, C], BF16),
                ("dbg_hcar", [128, 256], F32), ("dbg_x1", [TOK, D], BF16),
                ("dbg_x1p", [128, D], F32), ("dbg_st", [128, 4], F32),
                ("dbg_rt", [128, 64], F32), ("dbg_oa0", [128, D], F32),
                ("dbg_oaf", [128, D], F32), ("dbg_act", [128, 4, 512], BF16),
                ("dbg_scores", [128, EXP], F32)):
            dbg_t[nm] = nc.dram_tensor(nm, shape, dty, kind="ExternalOutput").ap()

    p = Prog()

    with ExitStack() as top:
        semh = {s: top.enter_context(nc.semaphore(s))
                for s in ("qS", "qG", "sS", "sV", "sP", "sG")}
        fconst = top.enter_context(nc.sbuf_tensor("fconst", [128, 8], F32))
        for _i, _v in enumerate(CONST_VALS):
            nc.const_aps.register(F32, _v, fconst[:, _i:_i + 1])
        attn = ExitStack()

        def sbA(name, shape, dtype):
            return attn.enter_context(nc.sbuf_tensor(name, shape, dtype))

        wbuf = sbA("wbuf", [128, DCH, E], BF16)
        wbuf_o = wbuf.ap().rearrange("p a b -> p (a b)").rearrange(
            "p (a b) -> p a b", a=EJ)
        wcat_sb = sbA("wcat_sb", [128, EJ, 96], BF16)
        wcatp_sb = sbA("wcatp_sb", [128, EJ, 96], BF16)
        wd2_sb = sbA("wd2_sb", [DR, E], BF16)
        ident = sbA("ident", [128, 128], BF16)
        cdiag = sbA("cdiag", [128, EJ * DCV, 128], BF16)
        wddiag = sbA("wddiag", [128, EJ, 128], BF16)
        A_sb = sbA("A_sb", [128, EJ, NS], F32)
        negA_sb = sbA("negA_sb", [128, EJ, NS], F32)
        normw_sb = sbA("normw_sb", [128, DCH], F32)
        wd2b_sb = sbA("wd2b_sb", [128, EJ], F32)
        convb_sb = sbA("convb_sb", [128, EJ], F32)
        convw_sb = sbA("convw_sb", [128, EJ, DCV], F32)
        wd_sb = sbA("wd_sb", [128, EJ], F32)
        alog_sb = sbA("alog_sb", [128, EJ, NS], F32)
        selB = sbA("selB", [96, NS, 128], BF16)
        selC = sbA("selC", [96, NS, 128], BF16)
        rs_tile = sbA("rs_tile", [128, D], F32)
        sq_scr = sbA("sq_scr", [128, D], F32)
        ssq = sbA("ssq", [128, 4], F32)
        mcol = sbA("mcol", [128, 8], F32)
        xn_feat = sbA("xn_feat", [128, DCH, C], BF16)
        xpre_r = sbA("xpre_r", [128, 2, C + 3], BF16)
        halo = sbA("halo", [128, EJ, 3], BF16)
        x_bf = sbA("x_bf", [128, EJ, C], BF16)
        skip_s = sbA("skip_s", [128, EJ, C], BF16)
        dlrBC_bf = sbA("dlrBC_bf", [96, C], BF16)
        sp1 = sbA("sp1", [128, C], F32)
        sp2 = sbA("sp2", [128, C], F32)
        delta_r = sbA("delta_r", [128, 2, C], F32)
        dx_r = sbA("dx_r", [128, 2, C], BF16)
        ones_sb = sbA("ones_sb", [128, C], F32)
        a_r = sbA("a_r", [128, 2, C], F32)
        u_r = sbA("u_r", [128, 2, C], BF16)
        h_ring = sbA("h_ring", [128, 8, C], BF16)
        g_r = sbA("g_r", [128, 2, C], BF16)
        gdead = sbA("gdead", [128, C], BF16)
        Dm_r = sbA("Dm_r", [128, 2, C], F32)
        Gcols = sbA("Gcols", [128, NS], F32)
        Pfac = sbA("Pfac", [128, NS], F32)
        hcar = sbA("hcar", [128, 256], F32)
        Bmb_r = sbA("Bmb_r", [128, 2, C], BF16)
        Cmb_r = sbA("Cmb_r", [128, 2, C], BF16)
        y_bf = sbA("y_bf", [128, EJ, C], BF16)
        x1w = sbA("x1w", [128, E], F32)
        wcast_sb = sbA("wcast_sb", [128, E], BF16)
        x1b_t = sbA("x1b_t", [128, D], BF16)
        fmca = sbA("fmca", [128, 4], F32)

        p_tr = attn.enter_context(nc.psum_tensor("p_tr", [128, 1024], BF16))
        p_x = attn.enter_context(nc.psum_tensor("p_x", [128, 2, 512], F32))
        p_c = attn.enter_context(nc.psum_tensor("p_c", [128, 512], F32))
        p_bc = attn.enter_context(nc.psum_tensor("p_bc", [128, 512], F32))
        p_sel = attn.enter_context(nc.psum_tensor("p_sel", [128, 2, 512], F32))
        p_y = attn.enter_context(nc.psum_tensor("p_y", [128, 512], F32))

        p.add("sync", lambda e: e.dma_start(out=fconst[:], in_=fconst_d),
              mile="fcst", dma=True)
        csts = []
        for i, (sb, dr) in enumerate((
                (normw_sb, normw8), (wd2b_sb, wd2b_d), (convb_sb, convb_d),
                (convw_sb, convw_d), (wd_sb, wd_d), (alog_sb, alog_d))):
            p.add("sync", lambda e, sb=sb, dr=dr: e.dma_start(out=sb[:], in_=dr),
                  mile=f"cst{i}", dma=True)
            csts.append(f"cst{i}")
        prev = None
        for nm, src, dst in (("wc", wcatT, wcat_sb), ("wcp", wcatTp, wcatp_sb)):
            p.add("sync", lambda e, src=src: e.dma_start(
                out=x1w[:, 0:EJ * 96].rearrange("p (j k) -> p j k", j=EJ),
                in_=src.rearrange("(j p) k -> p j k", p=128)),
                after=(prev,), mile=f"{nm}L", dma=True)
            p.add("scalar", lambda e, dst=dst: e.copy(
                out=dst[:], in_=x1w[:, 0:EJ * 96].rearrange(
                    "p (j k) -> p j k", j=EJ)),
                after=(f"{nm}L",), mile=f"{nm}C")
            prev = f"{nm}C"
        p.add("sync", lambda e: e.dma_start(out=x1w[0:DR, 0:E], in_=wd2T_d),
              after=("wcpC",), mile="wd2L", dma=True)
        p.add("scalar", lambda e: e.copy(out=wd2_sb[:], in_=x1w[0:DR, 0:E]),
              after=("wd2L",), mile="wd2C")
        p.add("scalar", lambda e: e.activation(
            out=negA_sb[:], in_=alog_sb[:], func=AF.Exp), after=("cst5",),
            mile="negA")
        p.add("vector", lambda e: e.tensor_scalar(
            out=A_sb[:], in0=negA_sb[:], scalar1=-1.0, scalar2=None,
            op0=ALU.mult), after=("negA",), mile="A_")
        p.add("gpsimd", lambda e: e.memset(ident[:], 0.0))
        p.add("gpsimd", lambda e: e.affine_select(
            out=ident[:], in_=ident[:], compare_op=ALU.not_equal, fill=1.0,
            base=0, pattern=[[-1, 128]], channel_multiplier=1), mile="idI")
        p.add("gpsimd", lambda e: e.memset(selB[:], 0.0))
        p.add("gpsimd", lambda e: e.memset(selC[:], 0.0))
        for n in range(NS):
            p.add("gpsimd", lambda e, n=n: e.affine_select(
                out=selB[:, n, :], in_=selB[:, n, :],
                compare_op=ALU.not_equal, fill=1.0, base=-(64 + n),
                pattern=[[0, 128]], channel_multiplier=1),
                mile=f"selB{n}" if n == NS - 1 else None)
            p.add("gpsimd", lambda e, n=n: e.affine_select(
                out=selC[:, n, :], in_=selC[:, n, :],
                compare_op=ALU.not_equal, fill=1.0, base=-(80 + n),
                pattern=[[0, 128]], channel_multiplier=1),
                mile="selCZ" if n == NS - 1 else None)
        p.add("gpsimd", lambda e: e.memset(ones_sb[:], 1.0), mile="ones")
        p.add("gpsimd", lambda e: e.memset(hcar[:], 0.0), mile="hcar0")
        p.add("gpsimd", lambda e: e.memset(halo[:], 0.0), mile="halo0")
        for ej in range(EJ):
            for k in range(DCV):
                p.add("vector", lambda e, ej=ej, k=k: e.tensor_scalar(
                    out=cdiag[:, ej * DCV + k, :], in0=ident[:],
                    scalar1=convw_sb[:, ej, k:k + 1], scalar2=None,
                    op0=ALU.mult), after=("idI", "cst3"),
                    mile=f"cd{ej}" if k == DCV - 1 else None)
            p.add("vector", lambda e, ej=ej: e.tensor_scalar(
                out=wddiag[:, ej, :], in0=ident[:],
                scalar1=wd_sb[:, ej:ej + 1], scalar2=None, op0=ALU.mult),
                after=("idI", "cst4"), mile=f"wdd{ej}")
        prev = "wd2C"
        for nm, src, dst, fold in (("in", in_wT, in_bfd, True),
                                   ("sk", skip_wT, skip_bfd, True)):
            for dc in range(DCH):
                p.add("sync", lambda e, src=src, dc=dc: e.dma_start(
                    out=x1w[:], in_=src[dc * 128:(dc + 1) * 128, :]),
                    after=(prev,), mile=f"{nm}L{dc}", dma=True)
                p.add("vector", lambda e, dc=dc: e.tensor_scalar(
                    out=wcast_sb[:], in0=x1w[:],
                    scalar1=normw_sb[:, dc:dc + 1], scalar2=None,
                    op0=ALU.mult), after=(f"{nm}L{dc}", "cst0"),
                    mile=f"{nm}C{dc}")
                p.add("sync", lambda e, dst=dst, dc=dc: e.dma_start(
                    out=dst[dc], in_=wcast_sb[:]),
                    after=(f"{nm}C{dc}",), mile=f"{nm}S{dc}", dma=True)
                prev = f"{nm}C{dc}"
        for ec in range(EJ):
            p.add("sync", lambda e, ec=ec: e.dma_start(
                out=x1w[:, 0:D], in_=out_wT[ec * 128:(ec + 1) * 128, :]),
                after=(prev,), mile=f"oL{ec}", dma=True)
            p.add("scalar", lambda e: e.copy(
                out=wcast_sb[:, 0:D], in_=x1w[:, 0:D]),
                after=(f"oL{ec}", f"oS{ec-1}" if ec else None),
                mile=f"oC{ec}")
            p.add("sync", lambda e, ec=ec: e.dma_start(
                out=out_bfd[ec], in_=wcast_sb[:, 0:D]),
                after=(f"oC{ec}",), mile=f"oS{ec}", dma=True)
            prev = f"oC{ec}"

        rs_free = [f"oS{EJ-1}"]
        x1b_free = ["wd2C"]
        xnfeat_free = [None]
        wbuf_free = [f"skS{DCH-1}"]
        xpre = Ring(2)
        pxr = Ring(2)
        pc_r = Ring(1)
        pbc_r = Ring(1)
        psel_r = Ring(2)
        py_r = Ring(1)
        ptr_r = Ring(1)
        dlt = Ring(2)
        dxr = Ring(2)
        ar = Ring(2)
        ur = Ring(2)
        hr = Ring(8)
        gr = Ring(2)
        bmb = Ring(2)
        cmb = Ring(2)
        dmr = Ring(2)
        sp1r = Ring(1)
        sp2r = Ring(1)
        xbf_free = {ej: ["A_"] for ej in range(EJ)}
        skip_free = {ej: [None] for ej in range(EJ)}
        ybf_free = {ej: [None] for ej in range(EJ)}
        dlr_free = [None]
        halo_v = {ej: "halo0" for ej in range(EJ)}
        gcols_free = ["hcar0"]
        hcar_mile = {ej: None for ej in range(EJ)}

        for ch in range(8):
            pfx = ch < 4
            t0 = (ch % 4) * C
            rsrc = resid_pfx if pfx else resid_own

            for tt in range(4):
                p.add("sync", lambda e, tt=tt, rsrc=rsrc, t0=t0: e.dma_start(
                    out=rs_tile[:], in_=rsrc[t0 + tt * 128:t0 + (tt + 1) * 128, :]),
                    after=tuple(rs_free), mile=f"rs{ch}_{tt}", dma=True)
                p.add("scalar", lambda e, tt=tt: e.activation(
                    out=sq_scr[:], in_=rs_tile[:], func=AF.Square,
                    accum_out=ssq[:, tt:tt + 1]),
                    after=(f"rs{ch}_{tt}",), mile=f"sq{ch}_{tt}")
                p.add("scalar", lambda e, tt=tt: e.activation(
                    out=mcol[:, tt:tt + 1], in_=ssq[:, tt:tt + 1],
                    func=AF.Sqrt, scale=INV_D, bias=EPS),
                    after=(f"sq{ch}_{tt}", "fcst",
                           f"xnc{ch-1}_{tt}" if ch else None),
                    strict=True, mile=f"ms{ch}_{tt}")
                p.add("vector", lambda e, tt=tt: e.reciprocal(
                    out=mcol[:, 4 + tt:5 + tt], in_=mcol[:, tt:tt + 1]),
                    after=(f"ms{ch}_{tt}",), mile=f"rst{ch}_{tt}")
                p.add("scalar", lambda e, tt=tt: e.activation(
                    out=x1b_t[:], in_=rs_tile[:], func=AF.Copy,
                    scale=mcol[:, 4 + tt:5 + tt]),
                    after=tuple([f"rst{ch}_{tt}"] + x1b_free),
                    mile=f"xnc{ch}_{tt}")
                rs_free = [f"xnc{ch}_{tt}"]
                slot, w = ptr_r.acquire()

                def trf(e, tt=tt):
                    for j in range(DCH - 1):
                        e.transpose(out=p_tr[:, j * 128:(j + 1) * 128],
                                    in_=x1b_t[:, j * 128:(j + 1) * 128],
                                    identity=ident[:])
                    return e.transpose(out=p_tr[:, (DCH - 1) * 128:DCH * 128],
                                       in_=x1b_t[:, (DCH - 1) * 128:DCH * 128],
                                       identity=ident[:])
                p.add("tensor", trf,
                      after=tuple([f"xnc{ch}_{tt}", "idI"] + w),
                      mile=f"tr{ch}_{tt}")
                x1b_free = [f"tr{ch}_{tt}"]
                p.add("vector", lambda e, tt=tt: e.tensor_copy(
                    out=xn_feat[:, :, tt * 128:(tt + 1) * 128],
                    in_=p_tr[:].rearrange("p (a b) -> p a b", a=DCH)),
                    after=tuple([f"tr{ch}_{tt}"] + xnfeat_free),
                    mile=f"xnf{ch}_{tt}")
                ptr_r.consumed(f"xnf{ch}_{tt}")

            p.add("sync", lambda e: e.dma_start(
                out=wbuf[:], in_=in_bfd.rearrange("a p b -> p a b")),
                after=tuple(wbuf_free + [f"inS{DCH-1}"]),
                mile=f"inL{ch}", dma=True)
            for ej in range(EJ):
                slot, w = pxr.acquire()

                def inmm(e, ej=ej, slot=slot):
                    for dc in range(DCH - 1):
                        e.matmul(out=p_x[:, slot, :],
                                 lhsT=wbuf[:, dc, ej * 128:(ej + 1) * 128],
                                 rhs=xn_feat[:, dc, :],
                                 start=(dc == 0), stop=False)
                    return e.matmul(out=p_x[:, slot, :],
                                    lhsT=wbuf[:, DCH - 1, ej * 128:(ej + 1) * 128],
                                    rhs=xn_feat[:, DCH - 1, :],
                                    start=False, stop=True)
                p.add("tensor", inmm,
                      after=tuple([f"xnf{ch}_3", f"inL{ch}"] + w),
                      mile=f"Bx{ch}_{ej}")
                xslot, xw = xpre.acquire()
                p.add("scalar", lambda e, slot=slot, xslot=xslot: e.copy(
                    out=xpre_r[:, xslot, 3:3 + C], in_=p_x[:, slot, :]),
                    after=tuple([f"Bx{ch}_{ej}"] + xw), mile=f"evX{ch}_{ej}")
                pxr.consumed(f"evX{ch}_{ej}")
                p.add("vector", lambda e, ej=ej, xslot=xslot: e.tensor_copy(
                    out=xpre_r[:, xslot, 0:3], in_=halo[:, ej, :]),
                    after=(f"evX{ch}_{ej}", halo_v[ej]), mile=f"hin{ch}_{ej}")
                p.add("vector", lambda e, ej=ej, xslot=xslot: e.tensor_copy(
                    out=halo[:, ej, :], in_=xpre_r[:, xslot, C:C + 3]),
                    after=(f"evX{ch}_{ej}",), mile=f"hout{ch}_{ej}")
                halo_v[ej] = f"hout{ch}_{ej}"
                _, cw = pc_r.acquire()

                def convmm(e, ej=ej, xslot=xslot):
                    for k in range(DCV - 1):
                        e.matmul(out=p_c[:], lhsT=cdiag[:, ej * DCV + k, :],
                                 rhs=xpre_r[:, xslot, k:k + C],
                                 start=(k == 0), stop=False)
                    return e.matmul(out=p_c[:],
                                    lhsT=cdiag[:, ej * DCV + DCV - 1, :],
                                    rhs=xpre_r[:, xslot, DCV - 1:DCV - 1 + C],
                                    start=False, stop=True)
                p.add("tensor", convmm,
                      after=tuple([f"evX{ch}_{ej}", f"hin{ch}_{ej}",
                                   f"cd{ej}"] + cw),
                      mile=f"Bc{ch}_{ej}")
                xpre.consumed(f"Bc{ch}_{ej}", f"hout{ch}_{ej}")
                p.add("scalar", lambda e, ej=ej: e.activation(
                    out=x_bf[:, ej, :], in_=p_c[:], func=AF.Silu,
                    bias=convb_sb[:, ej:ej + 1]),
                    after=tuple([f"Bc{ch}_{ej}", "cst2"] + xbf_free[ej]),
                    mile=f"evC{ch}_{ej}")
                pc_r.consumed(f"evC{ch}_{ej}")
                xbf_free[ej] = []

            if not pfx:
                p.add("sync", lambda e: e.dma_start(
                    out=wbuf[:], in_=skip_bfd.rearrange("a p b -> p a b")),
                    after=(f"Bx{ch}_{EJ-1}", f"skS{DCH-1}"),
                    mile=f"skL{ch}", dma=True)
                for ej in range(EJ):
                    slot, w = pxr.acquire()

                    def skmm(e, ej=ej, slot=slot):
                        for dc in range(DCH - 1):
                            e.matmul(out=p_x[:, slot, :],
                                     lhsT=wbuf[:, dc, ej * 128:(ej + 1) * 128],
                                     rhs=xn_feat[:, dc, :],
                                     start=(dc == 0), stop=False)
                        return e.matmul(
                            out=p_x[:, slot, :],
                            lhsT=wbuf[:, DCH - 1, ej * 128:(ej + 1) * 128],
                            rhs=xn_feat[:, DCH - 1, :], start=False, stop=True)
                    p.add("tensor", skmm,
                          after=tuple([f"skL{ch}"] + w), mile=f"Bs{ch}_{ej}")
                    p.add("scalar", lambda e, ej=ej, slot=slot: e.activation(
                        out=skip_s[:, ej, :], in_=p_x[:, slot, :],
                        func=AF.Silu),
                        after=tuple([f"Bs{ch}_{ej}"] + skip_free[ej]),
                        mile=f"evS{ch}_{ej}")
                    pxr.consumed(f"evS{ch}_{ej}")
                    skip_free[ej] = []
                wbuf_free = [f"Bs{ch}_{EJ-1}"]
                xnfeat_free = [f"Bs{ch}_{EJ-1}"]
            else:
                wbuf_free = [f"Bx{ch}_{EJ-1}"]
                xnfeat_free = [f"Bx{ch}_{EJ-1}"]

            wsb = wcatp_sb if pfx else wcat_sb
            _, bw = pbc_r.acquire()

            def wcmm(e, wsb=wsb):
                for kj in range(EJ - 1):
                    e.matmul(out=p_bc[0:96, :], lhsT=wsb[:, kj, 0:96],
                             rhs=x_bf[:, kj, :], start=(kj == 0), stop=False)
                return e.matmul(out=p_bc[0:96, :], lhsT=wsb[:, EJ - 1, 0:96],
                                rhs=x_bf[:, EJ - 1, :], start=False, stop=True)
            p.add("tensor", wcmm,
                  after=tuple([f"evC{ch}_{ej}" for ej in range(EJ)]
                              + ["wcpC"] + bw),
                  mile=f"Cmm{ch}")
            p.add("vector", lambda e: e.tensor_copy(
                out=dlrBC_bf[:], in_=p_bc[0:96, :]),
                after=tuple([f"Cmm{ch}"] + dlr_free), mile=f"dlrC{ch}")
            pbc_r.consumed(f"dlrC{ch}")
            dlr_free = []
            if dbg and ch == 4:
                p.add("sync", lambda e: e.dma_start(
                    out=dbg_t["dbg_dlr"], in_=dlrBC_bf[:]),
                    after=(f"dlrC{ch}",), mile="DBGdlr", dma=True)
                p.add("sync", lambda e: e.dma_start(
                    out=dbg_t["dbg_xbf"], in_=x_bf[:]),
                    after=tuple(f"evC{ch}_{ej}" for ej in range(EJ)),
                    mile="DBGxbf", dma=True)

            if stages < 2:
                dlr_free = [f"dlrC{ch}"]
                for ej in range(EJ):
                    xbf_free[ej] = [f"Cmm{ch}"]
                    skip_free[ej] = [f"evS{ch}_{ej}"] if not pfx else [None]
                continue

            for ej in range(EJ):
                _, bw = pbc_r.acquire()

                def wd2mm(e, ej=ej):
                    return e.matmul(out=p_bc[:],
                                    lhsT=wd2_sb[:, ej * 128:(ej + 1) * 128],
                                    rhs=dlrBC_bf[0:DR, :],
                                    start=True, stop=True)
                p.add("tensor", wd2mm,
                      after=tuple([f"dlrC{ch}", "wd2C"] + bw),
                      mile=f"Dw{ch}_{ej}")
                _, s1w = sp1r.acquire()
                p.add("scalar", lambda e, ej=ej: e.activation(
                    out=sp1[:], in_=p_bc[:], func=AF.Abs,
                    bias=wd2b_sb[:, ej:ej + 1]),
                    after=tuple([f"Dw{ch}_{ej}", "cst1"] + s1w))
                p.add("scalar", lambda e: e.activation(
                    out=sp1[:], in_=sp1[:], func=AF.Exp, scale=-1.0),
                    after=("fcst",))
                p.add("scalar", lambda e: e.activation(
                    out=sp1[:], in_=sp1[:], func=AF.Ln, bias=1.0))
                _, s2w = sp2r.acquire()
                p.add("scalar", lambda e, ej=ej: e.activation(
                    out=sp2[:], in_=p_bc[:], func=AF.Relu,
                    bias=wd2b_sb[:, ej:ej + 1]),
                    after=tuple(s2w), mile=f"spR{ch}_{ej}")
                pbc_r.consumed(f"spR{ch}_{ej}")
                dslot, dw = dlt.acquire()
                p.add("vector", lambda e, dslot=dslot: e.tensor_add(
                    out=delta_r[:, dslot, :], in0=sp1[:], in1=sp2[:]),
                    after=tuple([f"spR{ch}_{ej}"] + dw), mile=f"dlt{ch}_{ej}")
                sp1r.consumed(f"dlt{ch}_{ej}")
                sp2r.consumed(f"dlt{ch}_{ej}")
                if dbg and ch == 4 and ej < 2:
                    p.add("sync", lambda e, dslot=dslot, ej=ej: e.dma_start(
                        out=dbg_t["dbg_delta"][:, ej, :],
                        in_=delta_r[:, dslot, :]),
                        after=(f"dlt{ch}_{ej}",), mile=f"DBGd{ej}", dma=True)
                    dlt.consumed(f"DBGd{ej}")
                xslot_d, dxw = dxr.acquire()
                p.add("gpsimd", lambda e, dslot=dslot, xslot_d=xslot_d, ej=ej:
                      e.tensor_tensor(out=dx_r[:, xslot_d, :],
                                      in0=delta_r[:, dslot, :],
                                      in1=x_bf[:, ej, :], op=ALU.mult),
                      after=tuple([f"dlt{ch}_{ej}"] + dxw),
                      mile=f"dx{ch}_{ej}")
                xbf_free[ej].append(f"dx{ch}_{ej}")

                if pfx:
                    aslot, aw = ar.acquire()
                    p.add("vector", lambda e, dslot=dslot, aslot=aslot:
                          e.tensor_tensor_scan(
                              out=a_r[:, aslot, :], data0=ones_sb[:],
                              data1=delta_r[:, dslot, :], initial=0.0,
                              op0=ALU.mult, op1=ALU.add),
                          after=tuple([f"dlt{ch}_{ej}", "ones"] + aw),
                          mile=f"S{ch}_{ej}")
                    dlt.consumed(f"S{ch}_{ej}", f"dx{ch}_{ej}")
                    mslot, mw = dmr.acquire()
                    p.add("gpsimd", lambda e, aslot=aslot, mslot=mslot:
                          e.tensor_scalar(
                              out=Dm_r[:, mslot, :], in0=a_r[:, aslot, :],
                              scalar1=a_r[:, aslot, C - 1:C], scalar2=None,
                              op0=ALU.subtract),
                          after=tuple([f"S{ch}_{ej}"] + mw),
                          mile=f"Dm{ch}_{ej}")
                    for n in range(NS):
                        sslot, sw = psel_r.acquire()
                        p.add("tensor", lambda e, n=n, sslot=sslot: e.matmul(
                            out=p_sel[:, sslot, :], lhsT=selB[:, n, :],
                            rhs=dlrBC_bf[:], start=True, stop=True),
                            after=tuple([f"dlrC{ch}", f"selB{NS-1}"] + sw),
                            mile=f"pB{ch}_{ej}_{n}")
                        bslot, bw2 = bmb.acquire()
                        p.add("scalar", lambda e, sslot=sslot, bslot=bslot:
                              e.copy(out=Bmb_r[:, bslot, :],
                                     in_=p_sel[:, sslot, :]),
                              after=tuple([f"pB{ch}_{ej}_{n}"] + bw2),
                              mile=f"bm{ch}_{ej}_{n}")
                        psel_r.consumed(f"bm{ch}_{ej}_{n}")
                        uslot, uw = ur.acquire()
                        p.add("scalar", lambda e, mslot=mslot, uslot=uslot,
                              ej=ej, n=n: e.activation(
                                  out=u_r[:, uslot, :],
                                  in_=Dm_r[:, mslot, :], func=AF.Exp,
                                  scale=negA_sb[:, ej, n:n + 1]),
                              after=tuple([f"Dm{ch}_{ej}", "A_"] + uw),
                              mile=f"wx{ch}_{ej}_{n}")
                        gslot, gw = gr.acquire()
                        p.add("gpsimd", lambda e, uslot=uslot, bslot=bslot,
                              gslot=gslot: e.tensor_tensor(
                                  out=g_r[:, gslot, :],
                                  in0=u_r[:, uslot, :],
                                  in1=Bmb_r[:, bslot, :], op=ALU.mult),
                              after=tuple([f"wx{ch}_{ej}_{n}",
                                           f"bm{ch}_{ej}_{n}"] + gw),
                              mile=f"t{ch}_{ej}_{n}")
                        ur.consumed(f"t{ch}_{ej}_{n}")
                        bmb.consumed(f"t{ch}_{ej}_{n}")
                        p.add("vector", lambda e, gslot=gslot, xslot_d=xslot_d,
                              n=n: e.scalar_tensor_tensor(
                                  out=gdead[:], in0=g_r[:, gslot, :],
                                  scalar=1.0, in1=dx_r[:, xslot_d, :],
                                  op0=ALU.mult, op1=ALU.mult,
                                  accum_out=Gcols[:, n:n + 1]),
                              after=tuple([f"t{ch}_{ej}_{n}", f"dx{ch}_{ej}"]
                                          + (gcols_free if n == 0 else [])),
                              mile=f"sg{ch}_{ej}_{n}")
                        gr.consumed(f"sg{ch}_{ej}_{n}")
                    dxr.consumed(f"sg{ch}_{ej}_{NS-1}")
                    dmr.consumed(f"wx{ch}_{ej}_{NS-1}")
                    p.add("scalar", lambda e, ej=ej, aslot=aslot:
                          e.activation(out=Pfac[:], in_=A_sb[:, ej, :],
                                       func=AF.Exp,
                                       scale=a_r[:, aslot, C - 1:C]),
                          after=(f"S{ch}_{ej}", "A_",
                                 f"hup{ch}_{ej-1}" if ej else
                                 (f"hup{ch-1}_{EJ-1}" if ch else None)),
                          mile=f"pf{ch}_{ej}")
                    ar.consumed(f"Dm{ch}_{ej}", f"pf{ch}_{ej}")
                    p.add("vector", lambda e, ej=ej: e.tensor_mul(
                        out=hcar[:, ej * NS:(ej + 1) * NS],
                        in0=hcar[:, ej * NS:(ej + 1) * NS], in1=Pfac[:]),
                        after=(f"pf{ch}_{ej}", "hcar0")
                              + ((hcar_mile[ej],) if hcar_mile[ej] else ()),
                        strict=True, mile=f"hu1{ch}_{ej}")
                    p.add("vector", lambda e, ej=ej: e.tensor_add(
                        out=hcar[:, ej * NS:(ej + 1) * NS],
                        in0=hcar[:, ej * NS:(ej + 1) * NS], in1=Gcols[:]),
                        after=(f"sg{ch}_{ej}_{NS-1}", f"hu1{ch}_{ej}"),
                        strict=True, mile=f"hup{ch}_{ej}")
                    hcar_mile[ej] = f"hup{ch}_{ej}"
                    gcols_free = [f"hup{ch}_{ej}"]
                else:
                    for n in range(NS):
                        sslot, sw = psel_r.acquire()
                        p.add("tensor", lambda e, n=n, sslot=sslot: e.matmul(
                            out=p_sel[:, sslot, :], lhsT=selB[:, n, :],
                            rhs=dlrBC_bf[:], start=True, stop=True),
                            after=tuple([f"dlrC{ch}", f"selB{NS-1}"] + sw),
                            mile=f"pB{ch}_{ej}_{n}")
                        bslot, bw2 = bmb.acquire()
                        p.add("scalar", lambda e, sslot=sslot, bslot=bslot:
                              e.copy(out=Bmb_r[:, bslot, :],
                                     in_=p_sel[:, sslot, :]),
                              after=tuple([f"pB{ch}_{ej}_{n}"] + bw2),
                              mile=f"bm{ch}_{ej}_{n}")
                        psel_r.consumed(f"bm{ch}_{ej}_{n}")
                        _, cw2 = pc_r.acquire()
                        p.add("tensor", lambda e, n=n: e.matmul(
                            out=p_c[:], lhsT=selC[:, n, :],
                            rhs=dlrBC_bf[:], start=True, stop=True),
                            after=tuple([f"dlrC{ch}", "selCZ"] + cw2),
                            mile=f"pC{ch}_{ej}_{n}")
                        cslot, cw3 = cmb.acquire()
                        p.add("scalar", lambda e, cslot=cslot: e.copy(
                            out=Cmb_r[:, cslot, :], in_=p_c[:]),
                            after=tuple([f"pC{ch}_{ej}_{n}"] + cw3),
                            mile=f"cm{ch}_{ej}_{n}")
                        pc_r.consumed(f"cm{ch}_{ej}_{n}")
                        aslot, aw = ar.acquire()
                        p.add("scalar", lambda e, dslot=dslot, aslot=aslot,
                              ej=ej, n=n: e.activation(
                                  out=a_r[:, aslot, :],
                                  in_=delta_r[:, dslot, :], func=AF.Exp,
                                  scale=A_sb[:, ej, n:n + 1]),
                              after=tuple([f"dlt{ch}_{ej}", "A_"] + aw),
                              mile=f"ax{ch}_{ej}_{n}")
                        uslot, uw = ur.acquire()
                        p.add("gpsimd", lambda e, xslot_d=xslot_d,
                              bslot=bslot, uslot=uslot: e.tensor_tensor(
                                  out=u_r[:, uslot, :],
                                  in0=dx_r[:, xslot_d, :],
                                  in1=Bmb_r[:, bslot, :], op=ALU.mult),
                              after=tuple([f"dx{ch}_{ej}",
                                           f"bm{ch}_{ej}_{n}"] + uw),
                              mile=f"u{ch}_{ej}_{n}")
                        bmb.consumed(f"u{ch}_{ej}_{n}")
                        hslot, hw = hr.acquire()
                        p.add("vector", lambda e, aslot=aslot, uslot=uslot,
                              hslot=hslot, ej=ej, n=n: e.tensor_tensor_scan(
                                  out=h_ring[:, hslot, :],
                                  data0=a_r[:, aslot, :],
                                  data1=u_r[:, uslot, :],
                                  initial=hcar[:, ej * NS + n:ej * NS + n + 1],
                                  op0=ALU.mult, op1=ALU.add),
                              after=tuple([f"ax{ch}_{ej}_{n}",
                                           f"u{ch}_{ej}_{n}"] + hw
                                          + ([hcar_mile[ej]]
                                             if n == 0 and hcar_mile[ej]
                                             else [])),
                              strict=(n == 0), mile=f"sc{ch}_{ej}_{n}")
                        ar.consumed(f"sc{ch}_{ej}_{n}")
                        ur.consumed(f"sc{ch}_{ej}_{n}")
                        if n == 7 or n == NS - 1:
                            base = ej * NS + (0 if n == 7 else 8)
                            p.add("vector", lambda e, base=base: e.tensor_copy(
                                out=hcar[:, base:base + 8],
                                in_=h_ring[:, 0:8, C - 1:C]),
                                after=(f"sc{ch}_{ej}_{n}",), strict=True,
                                mile=f"cy{ch}_{ej}_{n}")
                            if n == NS - 1:
                                hcar_mile[ej] = f"cy{ch}_{ej}_{n}"
                        gslot, gw = gr.acquire()
                        p.add("gpsimd", lambda e, hslot=hslot, cslot=cslot,
                              gslot=gslot: e.tensor_tensor(
                                  out=g_r[:, gslot, :],
                                  in0=h_ring[:, hslot, :],
                                  in1=Cmb_r[:, cslot, :], op=ALU.mult),
                              after=tuple([f"sc{ch}_{ej}_{n}",
                                           f"cm{ch}_{ej}_{n}"] + gw),
                              mile=f"g{ch}_{ej}_{n}")
                        cmb.consumed(f"g{ch}_{ej}_{n}")
                        hr.consumed(f"g{ch}_{ej}_{n}",
                                    f"cy{ch}_{ej}_{7 if n <= 7 else NS-1}"
                                    if (n == 7 or n == NS - 1) else None)
                        _, yw = (py_r.acquire() if n == 0 else (None, []))
                        p.add("tensor", lambda e, gslot=gslot, n=n: e.matmul(
                            out=p_y[:], lhsT=ident[:], rhs=g_r[:, gslot, :],
                            start=(n == 0), stop=False),
                            after=tuple([f"g{ch}_{ej}_{n}"] + yw),
                            mile=f"ya{ch}_{ej}_{n}")
                        gr.consumed(f"ya{ch}_{ej}_{n}")
                    dxr.consumed(f"u{ch}_{ej}_{NS-1}")
                    dlt.consumed(f"ax{ch}_{ej}_{NS-1}", f"dx{ch}_{ej}")
                    p.add("tensor", lambda e, ej=ej: e.matmul(
                        out=p_y[:], lhsT=wddiag[:, ej, :],
                        rhs=x_bf[:, ej, :], start=False, stop=True),
                        after=(f"wdd{ej}", f"evC{ch}_{ej}"),
                        mile=f"ywd{ch}_{ej}")
                    xbf_free[ej].append(f"ywd{ch}_{ej}")
                    p.add("vector", lambda e, ej=ej: e.tensor_tensor(
                        out=y_bf[:, ej, :], in0=p_y[:],
                        in1=skip_s[:, ej, :], op=ALU.mult),
                        after=tuple([f"ywd{ch}_{ej}", f"evS{ch}_{ej}"]
                                    + ybf_free[ej]),
                        mile=f"yev{ch}_{ej}")
                    py_r.consumed(f"yev{ch}_{ej}")
                    skip_free[ej] = [f"yev{ch}_{ej}"]
                    ybf_free[ej] = []
            dlr_free = ([f"pB{ch}_{EJ-1}_{NS-1}"]
                        if pfx else [f"ywd{ch}_{EJ-1}"])
            for _ej in range(EJ):
                xbf_free[_ej].append(f"Cmm{ch}")
            if dbg and ch == 3 and stages >= 2:
                p.add("sync", lambda e: e.dma_start(
                    out=dbg_t["dbg_hcar"], in_=hcar[:]),
                    after=(f"hup{ch}_{EJ-1}",), mile="DBGh", dma=True)
            if dbg and ch == 4 and stages >= 2:
                p.add("sync", lambda e: e.dma_start(
                    out=dbg_t["dbg_y"], in_=y_bf[:]),
                    after=tuple(f"yev{ch}_{ej}" for ej in range(EJ)),
                    mile="DBGy", dma=True)

            if pfx or stages < 3:
                if stages < 3 and not pfx:
                    for ej in range(EJ):
                        ybf_free[ej] = [f"yev{ch}_{ej}"]
                continue

            p.add("sync", lambda e: e.dma_start(
                out=wbuf[:], in_=out_bfd.rearrange("a p b -> p a b")),
                after=tuple(wbuf_free + [f"oS{EJ-1}"]),
                mile=f"oLd{ch}", dma=True)
            wbuf_free = []
            for lt in range(4):
                slot0, w0 = pxr.acquire()
                slot1, w1 = pxr.acquire()
                assert slot0 == 0 and slot1 == 1

                def outmm(e, lt=lt):
                    last = None
                    for half in range(2):
                        for ec in range(EJ):
                            last = e.matmul(
                                out=p_x[:, half, :],
                                lhsT=y_bf[:, ec, lt * 128:(lt + 1) * 128],
                                rhs=wbuf_o[:, ec, half * 512:(half + 1) * 512],
                                start=(ec == 0), stop=(ec == EJ - 1))
                    return last
                p.add("tensor", outmm,
                      after=tuple([f"yev{ch}_{EJ-1}", f"oLd{ch}"] + w0 + w1),
                      mile=f"Fmm{ch}_{lt}")
                p.add("sync", lambda e, lt=lt, t0=t0: e.dma_start(
                    out=rs_tile[:],
                    in_=resid_own[t0 + lt * 128:t0 + (lt + 1) * 128, :]),
                    after=tuple(rs_free), mile=f"rs2{ch}_{lt}", dma=True)
                p.add("vector", lambda e: e.scalar_tensor_tensor(
                    out=x1w[:, 0:D], in0=rs_tile[:], scalar=2.0,
                    in1=p_x[:].rearrange("p a b -> p (a b)"),
                    op0=ALU.mult, op1=ALU.add),
                    after=(f"Fmm{ch}_{lt}", f"rs2{ch}_{lt}"),
                    mile=f"x1p{ch}_{lt}")
                pxr.consumed(f"x1p{ch}_{lt}")
                pxr.consumed(f"x1p{ch}_{lt}")
                rs_free = [f"x1p{ch}_{lt}"]
                if dbg and ch == 4 and lt == 0:
                    p.add("sync", lambda e: e.dma_start(
                        out=dbg_t["dbg_x1p"], in_=x1w[:, 0:D]),
                        after=(f"x1p{ch}_{lt}",), mile="DBGxp", dma=True)
                p.add("vector", lambda e: e.reduce_sum(
                    out=fmca[:, 0:1], in_=x1w[:, 0:D], axis=AX.X),
                    mile=f"msm{ch}_{lt}")
                p.add("vector", lambda e: e.tensor_scalar(
                    out=fmca[:, 1:2], in0=fmca[:, 0:1], scalar1=INV_D,
                    scalar2=None, op0=ALU.mult),
                    after=(f"msm{ch}_{lt}",), strict=True,
                    mile=f"m2{ch}_{lt}")
                p.add("vector", lambda e: e.tensor_scalar(
                    out=x1w[:, 0:D], in0=x1w[:, 0:D],
                    scalar1=fmca[:, 1:2], scalar2=None, op0=ALU.subtract),
                    after=(f"m2{ch}_{lt}",) + (("DBGxp",)
                          if (dbg and ch == 4 and lt == 0) else ()),
                    strict=True,
                    mile=f"ctr{ch}_{lt}")
                p.add("scalar", lambda e: e.activation(
                    out=sq_scr[:], in_=x1w[:, 0:D], func=AF.Square,
                    accum_out=fmca[:, 2:3]),
                    after=(f"ctr{ch}_{lt}",), mile=f"vq{ch}_{lt}")
                p.add("scalar", lambda e: e.activation(
                    out=fmca[:, 3:4], in_=fmca[:, 2:3], func=AF.Sqrt,
                    scale=INV_D, bias=EPS),
                    after=(f"vq{ch}_{lt}",), strict=True, mile=f"vs{ch}_{lt}")
                p.add("vector", lambda e: e.reciprocal(
                    out=fmca[:, 2:3], in_=fmca[:, 3:4]),
                    after=(f"vs{ch}_{lt}",), mile=f"rc{ch}_{lt}")
                p.add("vector", lambda e: e.tensor_scalar(
                    out=x1b_t[:], in0=x1w[:, 0:D], scalar1=fmca[:, 2:3],
                    scalar2=None, op0=ALU.mult),
                    after=tuple([f"rc{ch}_{lt}"] + x1b_free), strict=True,
                    mile=f"x1c{ch}_{lt}")
                if dbg and ch == 4 and lt == 0:
                    p.add("sync", lambda e: e.dma_start(
                        out=dbg_t["dbg_st"], in_=fmca[:]),
                        after=(f"x1c{ch}_{lt}",), mile="DBGst", dma=True)
                p.add("sync", lambda e, lt=lt, t0=t0: e.dma_start(
                    out=x1td[t0 + lt * 128:t0 + (lt + 1) * 128, :],
                    in_=x1b_t[:]),
                    after=(f"x1c{ch}_{lt}",), mile=f"x1S{ch}_{lt}", dma=True)
                if dbg:
                    p.add("sync", lambda e, lt=lt, t0=t0: e.dma_start(
                        out=dbg_t["dbg_x1"][t0 + lt * 128:t0 + (lt + 1) * 128, :],
                        in_=x1b_t[:]),
                        after=(f"x1c{ch}_{lt}",), mile=f"DBGx1{ch}_{lt}",
                        dma=True)
                x1b_free = ([f"x1S{ch}_{lt}"]
                            + ([f"DBGx1{ch}_{lt}"] if dbg else []))
            for ej in range(EJ):
                ybf_free[ej] = [f"Fmm{ch}_3"]
            wbuf_free = [f"Fmm{ch}_3"]

        if stages >= 4:
            attn.close()
            p.barrier_counts()

            def sbF(name, shape, dtype):
                return top.enter_context(nc.sbuf_tensor(name, shape, dtype))

            x1f = sbF("x1f", [128, DCH, TOK], BF16)
            out_acc = sbF("out_acc", [128, 16, D], F32)
            land = sbF("land", [128, DCH * FI], F32)
            gwb = sbF("gwb", [128, 2, DCH, FI], BF16)
            uwb = sbF("uwb", [128, 2, DCH, FI], BF16)
            dwb = sbF("dwb", [128, 2, 4, D], BF16)
            acts = sbF("acts", [128, 4, 512], BF16)
            gc_bf = sbF("gc_bf", [128, 512], BF16)
            sg_bf = sbF("sg_bf", [128, 512], BF16)
            up1_bf = sbF("up1_bf", [128, 512], BF16)
            up2_bf = sbF("up2_bf", [128, 512], BF16)
            glu_bf = sbF("glu_bf", [128, 512], BF16)
            x1tok = sbF("x1tok", [128, D], BF16)
            lgt = sbF("lgt", [128, EXP], F32)
            fm1 = sbF("fm1", [128, EXP], F32)
            fl2 = sbF("fl2", [128, EXP], F32)
            fmk2 = sbF("fmk2", [128, EXP], F32)
            scores = sbF("scores", [128, 16, EXP], F32)
            sc_bf = sbF("sc_bf", [128, EXP], BF16)
            fcol = sbF("fcol", [128, 8], F32)
            srT = sbF("srT", [16, TOK], BF16)
            identF = sbF("identF", [128, 128], BF16)
            router_bf = sbF("router_bf", [128, DCH, EXP], BF16)
            downb_bf = sbF("downb_bf", [16, D], BF16)
            gb_sb = sbF("gb_sb", [128, EXP, 4], F32)
            ub_sb = sbF("ub_sb", [128, EXP, 4], F32)
            flw = sbF("flw", [128, D], F32)
            fmc = sbF("fmc", [128, 4], F32)
            fsq = sbF("fsq", [128, 512], F32)

            p_ftr = top.enter_context(nc.psum_tensor("p_ftr", [128, 1024], BF16))
            p_fg = top.enter_context(nc.psum_tensor("p_fg", [128, 512], F32))
            p_fu = top.enter_context(nc.psum_tensor("p_fu", [128, 512], F32))
            p_fd = top.enter_context(nc.psum_tensor("p_fd", [128, 1024], F32))
            p_fr = top.enter_context(nc.psum_tensor("p_fr", [128, 512], F32))

            p.add("gpsimd", lambda e: e.memset(identF[:], 0.0))
            p.add("gpsimd", lambda e: e.affine_select(
                out=identF[:], in_=identF[:], compare_op=ALU.not_equal,
                fill=1.0, base=0, pattern=[[-1, 128]], channel_multiplier=1),
                mile="FidI")
            p.add("sync", lambda e: e.dma_start(
                out=flw[:, 0:DCH * EXP].rearrange("p (a b) -> p a b", a=DCH),
                in_=router_d.rearrange("(a p) b -> p a b", p=128)),
                mile="rLd", dma=True)
            p.add("scalar", lambda e: e.copy(
                out=router_bf[:],
                in_=flw[:, 0:DCH * EXP].rearrange("p (a b) -> p a b", a=DCH)),
                after=("rLd",), mile="rC")
            p.add("sync", lambda e: e.dma_start(
                out=flw[0:16, 0:D], in_=downb_d), after=("rC",),
                mile="dbLd", dma=True)
            p.add("scalar", lambda e: e.copy(
                out=downb_bf[:], in_=flw[0:16, 0:D]), after=("dbLd",),
                mile="dbC")
            p.add("sync", lambda e: e.dma_start(out=gb_sb[:], in_=gateb_d),
                  mile="gbL", dma=True)
            p.add("sync", lambda e: e.dma_start(out=ub_sb[:], in_=upb_d),
                  mile="ubL", dma=True)

            x1tok_free = ["dbC"]
            for tt in range(16):
                p.add("sync", lambda e, tt=tt: e.dma_start(
                    out=x1tok[:], in_=x1td[tt * 128:(tt + 1) * 128, :]),
                    after=tuple(x1tok_free), mile=f"fxL{tt}", dma=True)

                def ftr(e, tt=tt):
                    for j in range(DCH - 1):
                        e.transpose(out=p_ftr[:, j * 128:(j + 1) * 128],
                                    in_=x1tok[:, j * 128:(j + 1) * 128],
                                    identity=identF[:])
                    return e.transpose(
                        out=p_ftr[:, (DCH - 1) * 128:DCH * 128],
                        in_=x1tok[:, (DCH - 1) * 128:DCH * 128],
                        identity=identF[:])
                p.add("tensor", ftr,
                      after=(f"fxL{tt}", "FidI",
                             f"fxE{tt-1}" if tt else None),
                      mile=f"fxT{tt}")
                x1tok_free = [f"fxT{tt}"]
                p.add("vector", lambda e, tt=tt: e.tensor_copy(
                    out=x1f[:, :, tt * 128:(tt + 1) * 128],
                    in_=p_ftr[:].rearrange("p (a b) -> p a b", a=DCH)),
                    after=(f"fxT{tt}",), mile=f"fxE{tt}")

            for tt in range(16):
                def rmm(e, tt=tt):
                    for dc in range(DCH - 1):
                        e.matmul(out=p_fr[:, 0:EXP],
                                 lhsT=x1f[:, dc, tt * 128:(tt + 1) * 128],
                                 rhs=router_bf[:, dc, :],
                                 start=(dc == 0), stop=False)
                    return e.matmul(
                        out=p_fr[:, 0:EXP],
                        lhsT=x1f[:, DCH - 1, tt * 128:(tt + 1) * 128],
                        rhs=router_bf[:, DCH - 1, :], start=False, stop=True)
                p.add("tensor", rmm,
                      after=(f"fxE{tt}", "rC",
                             f"rcp{tt-1}" if tt else None),
                      mile=f"rmm{tt}")
                p.add("vector", lambda e: e.tensor_copy(
                    out=lgt[:], in_=p_fr[:, 0:EXP]),
                    after=(f"rmm{tt}",), mile=f"rcp{tt}")
                p.add("vector", lambda e: e.reduce_max(
                    out=fcol[:, 0:1], in_=lgt[:], axis=AX.X),
                    after=(f"rcp{tt}",), strict=True, mile=f"rm1{tt}")
                p.add("vector", lambda e: e.tensor_scalar(
                    out=fm1[:], in0=lgt[:], scalar1=fcol[:, 0:1],
                    scalar2=None, op0=ALU.is_equal),
                    after=(f"rm1{tt}",), strict=True, mile=f"fm1m{tt}")

                p.add("vector", lambda e: e.scalar_tensor_tensor(
                    out=fl2[:], in0=fm1[:], scalar=-1e30, in1=lgt[:],
                    op0=ALU.mult, op1=ALU.add),
                    after=(f"fm1m{tt}",), strict=True, mile=f"fl2m{tt}")
                p.add("vector", lambda e: e.reduce_max(
                    out=fcol[:, 1:2], in_=fl2[:], axis=AX.X),
                    after=(f"fl2m{tt}",), strict=True, mile=f"rm2{tt}")
                p.add("vector", lambda e: e.tensor_scalar(
                    out=fmk2[:], in0=fl2[:], scalar1=fcol[:, 1:2],
                    scalar2=None, op0=ALU.is_equal),
                    after=(f"rm2{tt}",), strict=True, mile=f"mk2{tt}")
                p.add("vector", lambda e: e.tensor_scalar(
                    out=fcol[:, 2:3], in0=fcol[:, 1:2],
                    scalar1=fcol[:, 0:1], scalar2=None, op0=ALU.subtract),
                    after=(f"rm2{tt}",), strict=True, mile=f"d21{tt}")
                p.add("scalar", lambda e: e.activation(
                    out=fcol[:, 3:4], in_=fcol[:, 2:3], func=AF.Sigmoid),
                    after=(f"d21{tt}",), mile=f"w2{tt}")
                p.add("vector", lambda e: e.tensor_scalar(
                    out=fcol[:, 4:5], in0=fcol[:, 3:4], scalar1=-1.0,
                    scalar2=1.0, op0=ALU.mult, op1=ALU.add),
                    after=(f"w2{tt}", f"mk2{tt}"), strict=True,
                    mile=f"w1{tt}")
                p.add("vector", lambda e, tt=tt: e.tensor_scalar(
                    out=scores[:, tt, :], in0=fm1[:],
                    scalar1=fcol[:, 4:5], scalar2=None, op0=ALU.mult),
                    after=(f"w1{tt}",), strict=True, mile=f"sa{tt}")
                p.add("vector", lambda e, tt=tt: e.scalar_tensor_tensor(
                    out=scores[:, tt, :], in0=fmk2[:],
                    scalar=fcol[:, 3:4], in1=scores[:, tt, :],
                    op0=ALU.mult, op1=ALU.add),
                    after=(f"sa{tt}", f"mk2{tt}"), strict=True,
                    mile=f"scs{tt}")
                p.add("vector", lambda e, tt=tt: e.tensor_copy(
                    out=sc_bf[:], in_=scores[:, tt, :]),
                    after=(f"scs{tt}", f"sTe{tt-1}" if tt else None),
                    strict=True, mile=f"scb{tt}")
                p.add("tensor", lambda e: e.transpose(
                    out=p_ftr[0:EXP, 0:128], in_=sc_bf[:], identity=identF[:]),
                    after=(f"scb{tt}", f"fxE{15}",
                           f"sTe{tt-1}" if tt else None),
                    mile=f"sT{tt}")
                p.add("scalar", lambda e, tt=tt: e.copy(
                    out=srT[:, tt * 128:(tt + 1) * 128],
                    in_=p_ftr[0:EXP, 0:128]),
                    after=(f"sT{tt}",), mile=f"sTe{tt}")
                if dbg and tt == 0:
                    p.add("sync", lambda e: e.dma_start(
                        out=dbg_t["dbg_scores"], in_=scores[:, 0, :]),
                        after=("scs0",), mile="DBGsc", dma=True)
                    p.add("sync", lambda e: e.dma_start(
                        out=dbg_t["dbg_rt"][:, 0:16], in_=lgt[:]),
                        after=("scs0",), mile="DBGr1", dma=True)
                    p.add("sync", lambda e: e.dma_start(
                        out=dbg_t["dbg_rt"][:, 16:32], in_=fmk2[:]),
                        after=("scs0",), mile="DBGr2", dma=True)
                    p.add("sync", lambda e: e.dma_start(
                        out=dbg_t["dbg_rt"][:, 32:48], in_=fm1[:]),
                        after=("scs0",), mile="DBGr3", dma=True)
                    p.add("sync", lambda e: e.dma_start(
                        out=dbg_t["dbg_rt"][:, 48:56], in_=fcol[:]),
                        after=("scs0",), mile="DBGr4", dma=True)
                    p.add("vector", lambda e: e.engine_nop(),
                          after=("DBGr1", "DBGr2", "DBGr3", "DBGr4"),
                          mile="DBGrwait")

            for tt in range(16):
                def bmm(e, tt=tt):
                    e.matmul(out=p_fd[:, 0:512],
                             lhsT=srT[:, tt * 128:(tt + 1) * 128],
                             rhs=downb_bf[:, 0:512], start=True, stop=True)
                    return e.matmul(out=p_fd[:, 512:1024],
                                    lhsT=srT[:, tt * 128:(tt + 1) * 128],
                                    rhs=downb_bf[:, 512:1024],
                                    start=True, stop=True)
                p.add("tensor", bmm,
                      after=(f"sTe{tt}", "dbC", f"oaI{tt-1}" if tt else None),
                      mile=f"bmm{tt}")
                p.add("sync", lambda e, tt=tt: e.dma_start(
                    out=x1tok[:], in_=x1td[tt * 128:(tt + 1) * 128, :]),
                    after=tuple([f"bmm{tt}"] + x1tok_free),
                    mile=f"fx2L{tt}", dma=True)
                p.add("vector", lambda e, tt=tt: e.tensor_add(
                    out=out_acc[:, tt, :],
                    in0=p_fd[:], in1=x1tok[:]),
                    after=(f"bmm{tt}", f"fx2L{tt}"), mile=f"oaI{tt}")
                if dbg and tt == 0:
                    p.add("sync", lambda e: e.dma_start(
                        out=dbg_t["dbg_oa0"], in_=out_acc[:, 0, :]),
                        after=("oaI0",), mile="DBGoa0", dma=True)
                x1tok_free = [f"oaI{tt}"]

            prevload = "ubL"
            for ex in range(EXP):
                es = ex % 2
                p.add("sync", lambda e, ex=ex: e.dma_start(
                    out=land[:].rearrange("p (a b) -> p a b", a=DCH),
                    in_=gatew_d[ex].rearrange("(a p) b -> p a b", p=128)),
                    after=(prevload, f"gmmE{ex-2}" if ex >= 2 else None),
                    mile=f"gL{ex}", dma=True)
                p.add("scalar", lambda e, es=es: e.copy(
                    out=gwb[:, es, :, :],
                    in_=land[:].rearrange("p (a b) -> p a b", a=DCH)),
                    after=(f"gL{ex}",), mile=f"gC{ex}")
                p.add("sync", lambda e, ex=ex: e.dma_start(
                    out=land[:].rearrange("p (a b) -> p a b", a=DCH),
                    in_=upw_d[ex].rearrange("(a p) b -> p a b", p=128)),
                    after=(f"gC{ex}", f"ummE{ex-2}" if ex >= 2 else None),
                    mile=f"uL{ex}", dma=True)
                p.add("scalar", lambda e, es=es: e.copy(
                    out=uwb[:, es, :, :],
                    in_=land[:].rearrange("p (a b) -> p a b", a=DCH)),
                    after=(f"uL{ex}",), mile=f"uC{ex}")
                p.add("sync", lambda e, ex=ex: e.dma_start(
                    out=land[:, 0:4 * D].rearrange("p (a b) -> p a b", a=4),
                    in_=downw_d[ex].rearrange("(a p) b -> p a b", p=128)),
                    after=(f"uC{ex}", f"dmmE{ex-2}" if ex >= 2 else None),
                    mile=f"dL{ex}", dma=True)
                p.add("scalar", lambda e, es=es: e.copy(
                    out=dwb[:, es, :, :],
                    in_=land[:, 0:4 * D].rearrange("p (a b) -> p a b", a=4)),
                    after=(f"dL{ex}",), mile=f"dC{ex}")
                prevload = f"dC{ex}"
                for tsb in range(4):
                    for fc in range(4):
                        def gmm(e, es=es, tsb=tsb, fc=fc):
                            for dc in range(DCH - 1):
                                e.matmul(
                                    out=p_fg[:],
                                    lhsT=gwb[:, es, dc, fc * 128:(fc + 1) * 128],
                                    rhs=x1f[:, dc, tsb * 512:(tsb + 1) * 512],
                                    start=(dc == 0), stop=False)
                            return e.matmul(
                                out=p_fg[:],
                                lhsT=gwb[:, es, DCH - 1, fc * 128:(fc + 1) * 128],
                                rhs=x1f[:, DCH - 1, tsb * 512:(tsb + 1) * 512],
                                start=False, stop=True)
                        gk = f"{ex}_{tsb}_{fc}"
                        p.add("tensor", gmm,
                              after=(f"gC{ex}", f"fxE{15}",
                                     f"gev{ex}_{tsb}_{fc-1}" if fc else
                                     (f"gev{ex}_{tsb-1}_3" if tsb else
                                      f"gev{ex-1}_3_3" if ex else None)),
                              mile=f"gmm{gk}" + ("E" if tsb == 3 and fc == 3
                                                 else ""))
                        gmm_m = f"gmm{gk}" + ("E" if tsb == 3 and fc == 3 else "")
                        p.add("vector", lambda e, ex=ex, fc=fc: e.tensor_scalar(
                            out=gc_bf[:], in0=p_fg[:],
                            scalar1=gb_sb[:, ex, fc:fc + 1], scalar2=7.0,
                            op0=ALU.add, op1=ALU.min),
                            after=(gmm_m, "gbL",
                                   f"glu{ex}_{tsb}_{fc-1}" if fc else None),
                            mile=f"gev{gk}")
                        p.add("scalar", lambda e: e.activation(
                            out=sg_bf[:], in_=gc_bf[:], func=AF.Sigmoid,
                            scale=1.702),
                            after=(f"gev{gk}",
                                   f"glu{ex}_{tsb}_{fc-1}" if fc else None),
                            mile=f"sig{gk}")

                        def umm(e, es=es, tsb=tsb, fc=fc):
                            for dc in range(DCH - 1):
                                e.matmul(
                                    out=p_fu[:],
                                    lhsT=uwb[:, es, dc, fc * 128:(fc + 1) * 128],
                                    rhs=x1f[:, dc, tsb * 512:(tsb + 1) * 512],
                                    start=(dc == 0), stop=False)
                            return e.matmul(
                                out=p_fu[:],
                                lhsT=uwb[:, es, DCH - 1, fc * 128:(fc + 1) * 128],
                                rhs=x1f[:, DCH - 1, tsb * 512:(tsb + 1) * 512],
                                start=False, stop=True)
                        p.add("tensor", umm,
                              after=(f"uC{ex}",
                                     f"uev{ex}_{tsb}_{fc-1}" if fc else
                                     (f"uev{ex}_{tsb-1}_3" if tsb else
                                      f"uev{ex-1}_3_3" if ex else None)),
                              mile=f"umm{gk}" + ("E" if tsb == 3 and fc == 3
                                                 else ""))
                        umm_m = f"umm{gk}" + ("E" if tsb == 3 and fc == 3 else "")
                        p.add("vector", lambda e, ex=ex, fc=fc: e.tensor_scalar(
                            out=up1_bf[:], in0=p_fu[:],
                            scalar1=ub_sb[:, ex, fc:fc + 1], scalar2=7.0,
                            op0=ALU.add, op1=ALU.min),
                            after=(umm_m, "ubL",
                                   f"up2{ex}_{tsb}_{fc-1}" if fc else None),
                            mile=f"uev{gk}")
                        p.add("gpsimd", lambda e: e.tensor_scalar(
                            out=up2_bf[:], in0=up1_bf[:], scalar1=-7.0,
                            scalar2=1.0, op0=ALU.max, op1=ALU.add),
                            after=(f"uev{gk}",
                                   f"act{ex}_{tsb}_{fc-1}" if fc else None),
                            mile=f"up2{gk}")
                        p.add("gpsimd", lambda e: e.tensor_tensor(
                            out=glu_bf[:], in0=gc_bf[:], in1=sg_bf[:],
                            op=ALU.mult),
                            after=(f"sig{gk}", f"gev{gk}"), mile=f"glu{gk}")
                        p.add("gpsimd", lambda e, fc=fc: e.tensor_tensor(
                            out=acts[:, fc, :], in0=up2_bf[:],
                            in1=glu_bf[:], op=ALU.mult),
                            after=(f"dmm{ex}_{tsb-1}_3" if fc == 0 and tsb
                                   else (f"dmm{ex-1}_3_3" if fc == 0 and ex
                                         else None),
                                   "DBGact" if dbg and ex == 0 and tsb == 1
                                   and fc == 0 else None),
                            mile=f"act{gk}")
                    if dbg and ex == 0 and tsb == 0:
                        p.add("sync", lambda e: e.dma_start(
                            out=dbg_t["dbg_act"], in_=acts[:]),
                            after=("act0_0_3",), mile="DBGact", dma=True)
                    for tt2 in range(4):
                        def dmm(e, es=es, tsb=tsb, tt2=tt2):
                            last = None
                            for half in range(2):
                                for ic in range(4):
                                    last = e.matmul(
                                        out=p_fd[:, half * 512:(half + 1) * 512],
                                        lhsT=acts[:, ic,
                                                  tt2 * 128:(tt2 + 1) * 128],
                                        rhs=dwb[:, es, ic,
                                                half * 512:(half + 1) * 512],
                                        start=(ic == 0), stop=(ic == 3))
                            return last
                        dk = f"{ex}_{tsb}_{tt2}"
                        p.add("tensor", dmm,
                              after=(f"act{ex}_{tsb}_3", f"dC{ex}",
                                     f"oaI{15}",
                                     f"dev{ex}_{tsb}_{tt2-1}" if tt2 else
                                     (f"dev{ex}_{tsb-1}_3" if tsb else
                                      (f"dev{ex-1}_3_3" if ex else None))),
                              mile=f"dmm{dk}")
                        tg = tsb * 4 + tt2
                        p.add("vector", lambda e, ex=ex, tg=tg: (
                            e.scalar_tensor_tensor(
                                out=out_acc[:, tg, 0:512],
                                in0=p_fd[:, 0:512],
                                scalar=scores[:, tg, ex:ex + 1],
                                in1=out_acc[:, tg, 0:512],
                                op0=ALU.mult, op1=ALU.add),
                            e.scalar_tensor_tensor(
                                out=out_acc[:, tg, 512:1024],
                                in0=p_fd[:, 512:1024],
                                scalar=scores[:, tg, ex:ex + 1],
                                in1=out_acc[:, tg, 512:1024],
                                op0=ALU.mult, op1=ALU.add))[-1],
                            after=(f"dmm{dk}", f"oaI{tg}", f"scs{tg}"),
                            strict=True, mile=f"dev{ex}_{tsb}_{tt2}")
                p.ms[f"gmmE{ex}"] = p.ms[f"gmm{ex}_3_3E"]
                p.ms[f"ummE{ex}"] = p.ms[f"umm{ex}_3_3E"]
                p.ms[f"dmmE{ex}"] = p.ms[f"dmm{ex}_3_3"]

            if dbg:
                p.add("sync", lambda e: e.dma_start(
                    out=dbg_t["dbg_oaf"], in_=out_acc[:, 0, :]),
                    after=(f"dev{EXP-1}_0_0",), mile="DBGoaf", dma=True)
            outw = []
            for tt in range(16):
                p.add("vector", lambda e, tt=tt: e.reduce_sum(
                    out=fmc[:, 0:1], in_=out_acc[:, tt, :], axis=AX.X),
                    after=(f"dev{EXP-1}_{tt//4}_{tt%4}",), mile=f"Fms{tt}")
                p.add("vector", lambda e: e.tensor_scalar(
                    out=fmc[:, 1:2], in0=fmc[:, 0:1], scalar1=INV_D,
                    scalar2=None, op0=ALU.mult),
                    after=(f"Fms{tt}",), strict=True, mile=f"Fm2{tt}")
                p.add("vector", lambda e, tt=tt: e.tensor_scalar(
                    out=flw[:], in0=out_acc[:, tt, :], scalar1=fmc[:, 1:2],
                    scalar2=None, op0=ALU.subtract),
                    after=(f"Fm2{tt}", f"OUT{tt-1}" if tt else "dbC"),
                    strict=True, mile=f"Fctr{tt}")
                p.add("scalar", lambda e: e.activation(
                    out=fsq[:], in_=flw[:, 0:512], func=AF.Square,
                    accum_out=fmc[:, 2:3]),
                    after=(f"Fctr{tt}",))
                p.add("scalar", lambda e: e.activation(
                    out=fsq[:], in_=flw[:, 512:1024], func=AF.Square,
                    accum_out=fmc[:, 3:4]), mile=f"Fvq{tt}")
                p.add("vector", lambda e: e.tensor_add(
                    out=fmc[:, 2:3], in0=fmc[:, 2:3], in1=fmc[:, 3:4]),
                    after=(f"Fvq{tt}",), strict=True, mile=f"Fva{tt}")
                p.add("vector", lambda e: e.tensor_scalar(
                    out=fmc[:, 2:3], in0=fmc[:, 2:3], scalar1=INV_D,
                    scalar2=EPS, op0=ALU.mult, op1=ALU.add),
                    after=(f"Fva{tt}",), strict=True, mile=f"Fv{tt}")
                p.add("scalar", lambda e: e.activation(
                    out=fmc[:, 3:4], in_=fmc[:, 2:3], func=AF.Sqrt),
                    after=(f"Fv{tt}",), mile=f"Fsd{tt}")
                p.add("vector", lambda e: e.reciprocal(
                    out=fmc[:, 2:3], in_=fmc[:, 3:4]),
                    after=(f"Fsd{tt}",), mile=f"Frc{tt}")
                p.add("vector", lambda e: e.tensor_scalar(
                    out=flw[:], in0=flw[:], scalar1=fmc[:, 2:3],
                    scalar2=None, op0=ALU.mult),
                    after=(f"Frc{tt}",), strict=True, mile=f"Fsc{tt}")
                p.add("sync", lambda e, tt=tt: e.dma_start(
                    out=out_d[tt * 128:(tt + 1) * 128, :], in_=flw[:]),
                    after=(f"Fsc{tt}",), mile=f"OUT{tt}", dma=True)
                outw.append(f"OUT{tt}")

        if stages < 4:
            attn.close()

        qs_total = p.cnt["qS"]
        qg_total = p.cnt["qG"]
        p.ops["sync"].append(([("qS", qs_total)], lambda e: e.nop(), None))
        if qg_total:
            p.ops["gpsimd"].append(
                ([("qG", qg_total)], lambda e: e.nop(), None))

        with nc.Block() as block:
            p.emit_all(block, semh)

    return nc


_NC_CACHE = {}


def _get_nc(stages=5, dbg=False):
    key = (stages, dbg)
    if key not in _NC_CACHE:
        _NC_CACHE[key] = build(stages, dbg)
    return _NC_CACHE[key]


def make_in_maps(inp):
    f32 = np.float32
    resid = np.asarray(inp["L0_attn_resid"], f32)
    in_wT = np.ascontiguousarray(np.asarray(inp["L0_attn_in_w"], f32).T)
    skip_wT = np.ascontiguousarray(np.asarray(inp["L0_attn_skip_w"], f32).T)
    normw8 = np.ascontiguousarray(
        np.asarray(inp["L0_attn_norm_w"], f32).reshape(DCH, 128).T)
    wd1T = np.asarray(inp["L0_attn_wd1"], f32).T
    wBT = np.asarray(inp["L0_attn_wB"], f32).T
    wCT = np.asarray(inp["L0_attn_wC"], f32).T
    wcat = np.ascontiguousarray(np.concatenate([wd1T, wBT, wCT], axis=1))
    wcatp0 = wcat.copy()
    wcatp0[:, 64:80] = 0.0
    wd2T = np.ascontiguousarray(np.asarray(inp["L0_attn_wd2"], f32).T)
    wd2b = np.ascontiguousarray(
        np.asarray(inp["L0_attn_wd2_b"], f32).reshape(EJ, 128).T)
    convw = np.ascontiguousarray(
        np.asarray(inp["L0_attn_conv_w"], f32).reshape(EJ, 128, DCV)
        .transpose(1, 0, 2))
    convb = np.ascontiguousarray(
        np.asarray(inp["L0_attn_conv_b"], f32).reshape(EJ, 128).T)
    alog = np.ascontiguousarray(
        np.asarray(inp["L0_attn_A_log"], f32).reshape(EJ, 128, NS)
        .transpose(1, 0, 2))
    wdv = np.ascontiguousarray(
        np.asarray(inp["L0_attn_W_D"], f32).reshape(EJ, 128).T)
    out_wT = np.ascontiguousarray(np.asarray(inp["L0_attn_out_w"], f32).T)
    router = np.ascontiguousarray(np.asarray(inp["L0_ffn_router_weight"], f32))
    gup = np.asarray(inp["L0_ffn_gate_up_proj"], f32)
    gatew = np.ascontiguousarray(gup[:, :, 0::2])
    upw = np.ascontiguousarray(gup[:, :, 1::2])
    gupb = np.asarray(inp["L0_ffn_gate_up_proj_bias"], f32)
    gateb = np.ascontiguousarray(
        gupb[:, 0::2].reshape(EXP, 4, 128).transpose(2, 0, 1))
    upb = np.ascontiguousarray(
        gupb[:, 1::2].reshape(EXP, 4, 128).transpose(2, 0, 1))
    downw = np.ascontiguousarray(np.asarray(inp["L0_ffn_down_proj"], f32))
    downb = np.ascontiguousarray(np.asarray(inp["L0_ffn_down_proj_bias"], f32))
    zeros = np.zeros((TOK, D), f32)
    fconsts = np.tile(np.array(CONST_VALS + (0.0,) * (8 - len(CONST_VALS)),
                               f32), (128, 1))

    shared = dict(in_wT=in_wT, skip_wT=skip_wT, normw8=normw8, wcatT=wcat,
                  wd2T=wd2T, wd2b=wd2b, convw=convw, convb=convb, alog=alog,
                  wdvec=wdv, out_wT=out_wT, router=router, gatew=gatew,
                  fconsts=fconsts,
                  upw=upw, gateb=gateb, upb=upb, downw=downw, downb=downb)
    in_maps = []
    for c in range(8):
        b, r = c // 2, c % 2
        m = dict(shared)
        m["resid_own"] = np.ascontiguousarray(resid[b, r * TOK:(r + 1) * TOK])
        m["resid_pfx"] = zeros if r == 0 else np.ascontiguousarray(
            resid[b, 0:TOK])
        m["wcatTp"] = wcatp0 if r == 0 else wcat
        in_maps.append(m)
    return in_maps


def kernel(**inp):
    nc = _get_nc(5, False)
    in_maps = make_in_maps(inp)
    res = run_bass_kernel_spmd(nc, in_maps, core_ids=list(range(8)))
    out = np.empty((B, L, D), np.float32)
    for c in range(8):
        b, r = c // 2, c % 2
        out[b, r * TOK:(r + 1) * TOK] = res.results[c]["out"]
    return out


# version 04
# speedup vs baseline: 1.0898x; 1.0898x over previous
import numpy as np
from contextlib import ExitStack

import concourse.bass as bass
import concourse.mybir as mybir
from concourse.bass_utils import run_bass_kernel_spmd

F32 = mybir.dt.float32
BF16 = mybir.dt.bfloat16
AF = mybir.ActivationFunctionType
ALU = mybir.AluOpType
AX = mybir.AxisListType

B, L, D, E, NS, DR, DCV = 4, 4096, 1024, 2048, 16, 64, 4
NV = 6
EXP, FI = 16, 512
TOK = 2048
C = 512
NCH = TOK // C
EJ = E // 128
DCH = D // 128
EPS = 1e-5
INV_D = 1.0 / 1024.0
CONST_VALS = (EPS, INV_D, -1.0, 1.0, 1.702)

ENGINES = ("sync", "scalar", "vector", "tensor", "gpsimd")


class Ring:

    def __init__(self, depth):
        self.depth = depth
        self.hist = []

    def acquire(self):
        i = len(self.hist)
        slot = i % self.depth
        waits = list(self.hist[i - self.depth]) if i >= self.depth else []
        self.hist.append([])
        return slot, waits

    def consumed(self, *miles):
        self.hist[-1].extend(m for m in miles if m)


class Prog:
    def __init__(self):
        self.ops = {e: [] for e in ENGINES}
        self.cnt = {s: 0 for s in ("qS", "qG", "sS", "sV", "sP", "sG")}
        self.ms = {}
        self.wm = {}
        self.esem = {"sync": "qS", "scalar": "sS", "vector": "sV",
                     "tensor": "sP", "gpsimd": "sG"}
        self.pending = {e: [] for e in ENGINES}
        self.pending_counts = {}

    def barrier(self, names):
        for e in ENGINES:
            self.pending[e].extend(names)

    def barrier_counts(self):
        snap = [(s, v) for s, v in self.cnt.items() if v > 0]
        for e in ENGINES:
            self.pending_counts.setdefault(e, []).extend(snap)

    def add(self, eng, fn, after=(), mile=None, dma=False, strict=False):
        names = [m for m in after if m] + self.pending[eng]
        self.pending[eng] = []
        waits = []
        for s, v in self.pending_counts.pop(eng, []):
            own = (s == self.esem[eng]) and s not in ("qS", "qG")
            if own:
                continue
            key = (eng, s)
            if self.wm.get(key, -1) < v:
                self.wm[key] = v
                waits.append((s, v))
        my_sem = "qG" if (eng == "gpsimd" and dma) else self.esem[eng]
        for m in names:
            assert m in self.ms, f"unknown milestone {m}"
            s, v = self.ms[m]
            own = (s == self.esem[eng]) and s not in ("qS", "qG")
            if own and not strict:
                continue
            key = (eng, s)
            if self.wm.get(key, -1) >= v:
                continue
            self.wm[key] = v
            waits.append((s, v))
        inc = None
        if mile is not None:
            amt = 16 if dma else 1
            self.cnt[my_sem] += amt
            self.ms[mile] = (my_sem, self.cnt[my_sem])
            inc = (my_sem, amt)
        self.ops[eng].append((waits, fn, inc))
        return mile

    def emit_all(self, block, semh):
        for name in ENGINES:
            def section(eng_obj, name=name):
                for waits, fn, inc in self.ops[name]:
                    for (s, v) in waits:
                        eng_obj.wait_ge(semh[s], v)
                    inst = fn(eng_obj)
                    if inc is not None:
                        inst.then_inc(semh[inc[0]], inc[1])
            getattr(block, name)(section)


def build(stages=5, dbg=False):
    nc = bass.Bass("TRN2", target_bir_lowering=False)

    def din(name, shape, dtype=F32):
        return nc.dram_tensor(name, shape, dtype, kind="ExternalInput").ap()

    resid_own = din("resid_own", [TOK, D])
    resid_pfx = din("resid_pfx", [TOK, D])
    in_wT = din("in_wT", [D, E])
    skip_wT = din("skip_wT", [D, E])
    normw8 = din("normw8", [128, DCH])
    wcatT = din("wcatT", [E, 96])
    wcatTp = din("wcatTp", [E, 96])
    wd2T_d = din("wd2T", [DR, E])
    wd2b_d = din("wd2b", [128, EJ])
    convw_d = din("convw", [128, EJ, DCV])
    convb_d = din("convb", [128, EJ])
    alog_d = din("alog", [128, EJ, NS])
    wd_d = din("wdvec", [128, EJ])
    out_wT = din("out_wT", [E, D])
    router_d = din("router", [D, EXP])
    gatew_d = din("gatew", [EXP, D, FI])
    upw_d = din("upw", [EXP, D, FI])
    gateb_d = din("gateb", [128, EXP, FI // 128])
    upb_d = din("upb", [128, EXP, FI // 128])
    downw_d = din("downw", [EXP, FI, D])
    downb_d = din("downb", [EXP, D])
    fconst_d = din("fconsts", [128, 8])

    in_bfd = nc.dram_tensor("in_bfd", [DCH, 128, E], BF16, kind="Internal").ap()
    skip_bfd = nc.dram_tensor("skip_bfd", [DCH, 128, E], BF16, kind="Internal").ap()
    out_bfd = nc.dram_tensor("out_bfd", [EJ, 128, D], BF16, kind="Internal").ap()
    x1td = nc.dram_tensor("x1td", [TOK, D], BF16, kind="Internal").ap()
    out_d = nc.dram_tensor("out", [TOK, D], F32, kind="ExternalOutput").ap()
    dbg_t = {}
    if dbg:
        for nm, shape, dty in (
                ("dbg_xbf", [128, EJ, C], BF16), ("dbg_dlr", [96, C], BF16),
                ("dbg_delta", [128, 2, C], F32), ("dbg_y", [128, EJ, C], BF16),
                ("dbg_hcar", [128, 256], F32), ("dbg_x1", [TOK, D], BF16),
                ("dbg_x1p", [128, D], F32), ("dbg_st", [128, 4], F32),
                ("dbg_rt", [128, 64], F32), ("dbg_oa0", [128, D], F32),
                ("dbg_oaf", [128, D], F32), ("dbg_act", [128, 4, 512], BF16),
                ("dbg_scores", [128, EXP], F32)):
            dbg_t[nm] = nc.dram_tensor(nm, shape, dty, kind="ExternalOutput").ap()

    p = Prog()

    with ExitStack() as top:
        semh = {s: top.enter_context(nc.semaphore(s))
                for s in ("qS", "qG", "sS", "sV", "sP", "sG")}
        fconst = top.enter_context(nc.sbuf_tensor("fconst", [128, 8], F32))
        for _i, _v in enumerate(CONST_VALS):
            nc.const_aps.register(F32, _v, fconst[:, _i:_i + 1])
        attn = ExitStack()

        def sbA(name, shape, dtype):
            return attn.enter_context(nc.sbuf_tensor(name, shape, dtype))

        wbuf = sbA("wbuf", [128, DCH, E], BF16)
        wbuf_o = wbuf.ap().rearrange("p a b -> p (a b)").rearrange(
            "p (a b) -> p a b", a=EJ)
        wcat_sb = sbA("wcat_sb", [128, EJ, 96], BF16)
        wcatp_sb = sbA("wcatp_sb", [128, EJ, 96], BF16)
        wd2_sb = sbA("wd2_sb", [DR, E], BF16)
        ident = sbA("ident", [128, 128], BF16)
        cdiag = sbA("cdiag", [128, EJ * DCV, 128], BF16)
        wddiag = sbA("wddiag", [128, EJ, 128], BF16)
        A_sb = sbA("A_sb", [128, EJ, NS], F32)
        negA_sb = sbA("negA_sb", [128, EJ, NS], F32)
        normw_sb = sbA("normw_sb", [128, DCH], F32)
        wd2b_sb = sbA("wd2b_sb", [128, EJ], F32)
        convb_sb = sbA("convb_sb", [128, EJ], F32)
        convw_sb = sbA("convw_sb", [128, EJ, DCV], F32)
        wd_sb = sbA("wd_sb", [128, EJ], F32)
        alog_sb = sbA("alog_sb", [128, EJ, NS], F32)
        selB = sbA("selB", [96, NS, 128], BF16)
        selC = sbA("selC", [96, NS, 128], BF16)
        rs_tile = sbA("rs_tile", [128, D], F32)
        sq_scr = sbA("sq_scr", [128, D], F32)
        ssq = sbA("ssq", [128, 4], F32)
        mcol = sbA("mcol", [128, 8], F32)
        xn_feat = sbA("xn_feat", [128, DCH, C], BF16)
        xpre_r = sbA("xpre_r", [128, 2, C + 3], BF16)
        halo = sbA("halo", [128, EJ, 3], BF16)
        x_bf = sbA("x_bf", [128, EJ, C], BF16)
        skip_s = sbA("skip_s", [128, EJ, C], BF16)
        dlrBC_bf = sbA("dlrBC_bf", [96, C], BF16)
        sp1 = sbA("sp1", [128, C], F32)
        sp2 = sbA("sp2", [128, C], F32)
        delta_r = sbA("delta_r", [128, 2, C], F32)
        dx_r = sbA("dx_r", [128, 2, C], BF16)
        ones_sb = sbA("ones_sb", [128, C], F32)
        a_r = sbA("a_r", [128, 4, C], F32)
        u_r = sbA("u_r", [128, 4, C], BF16)
        h_ring = sbA("h_ring", [128, 8, C], BF16)
        g_r = sbA("g_r", [128, 4, C], BF16)
        gdead = sbA("gdead", [128, C], BF16)
        x1w_full = None
        Gcols = sbA("Gcols", [128, NS], F32)
        Pfac = sbA("Pfac", [128, NS], F32)
        hcar = sbA("hcar", [128, 256], F32)
        Bmb_r = sbA("Bmb_r", [128, 4, C], BF16)
        Cmb_r = sbA("Cmb_r", [128, 4, C], BF16)
        y_bf = sbA("y_bf", [128, EJ, C], BF16)
        x1w = sbA("x1w", [128, E], F32)
        Dm_r = x1w.ap()[:, D:E].rearrange("p (a b) -> p a b", a=2)
        wcast_sb = sbA("wcast_sb", [128, E], BF16)
        x1b_t = sbA("x1b_t", [128, D], BF16)
        fmca = sbA("fmca", [128, 4], F32)

        p_tr = attn.enter_context(nc.psum_tensor("p_tr", [128, 1024], BF16))
        p_x = attn.enter_context(nc.psum_tensor("p_x", [128, 2, 512], F32))
        p_c = attn.enter_context(nc.psum_tensor("p_c", [128, 512], F32))
        p_bc = attn.enter_context(nc.psum_tensor("p_bc", [128, 512], F32))
        p_sel = attn.enter_context(nc.psum_tensor("p_sel", [128, 2, 512], F32))
        p_y = attn.enter_context(nc.psum_tensor("p_y", [128, 512], F32))

        p.add("sync", lambda e: e.dma_start(out=fconst[:], in_=fconst_d),
              mile="fcst", dma=True)
        csts = []
        for i, (sb, dr) in enumerate((
                (normw_sb, normw8), (wd2b_sb, wd2b_d), (convb_sb, convb_d),
                (convw_sb, convw_d), (wd_sb, wd_d), (alog_sb, alog_d))):
            p.add("sync", lambda e, sb=sb, dr=dr: e.dma_start(out=sb[:], in_=dr),
                  mile=f"cst{i}", dma=True)
            csts.append(f"cst{i}")
        prev = None
        for nm, src, dst in (("wc", wcatT, wcat_sb), ("wcp", wcatTp, wcatp_sb)):
            p.add("sync", lambda e, src=src: e.dma_start(
                out=x1w[:, 0:EJ * 96].rearrange("p (j k) -> p j k", j=EJ),
                in_=src.rearrange("(j p) k -> p j k", p=128)),
                after=(prev,), mile=f"{nm}L", dma=True)
            p.add("scalar", lambda e, dst=dst: e.copy(
                out=dst[:], in_=x1w[:, 0:EJ * 96].rearrange(
                    "p (j k) -> p j k", j=EJ)),
                after=(f"{nm}L",), mile=f"{nm}C")
            prev = f"{nm}C"
        p.add("sync", lambda e: e.dma_start(out=x1w[0:DR, 0:E], in_=wd2T_d),
              after=("wcpC",), mile="wd2L", dma=True)
        p.add("scalar", lambda e: e.copy(out=wd2_sb[:], in_=x1w[0:DR, 0:E]),
              after=("wd2L",), mile="wd2C")
        p.add("scalar", lambda e: e.activation(
            out=negA_sb[:], in_=alog_sb[:], func=AF.Exp), after=("cst5",),
            mile="negA")
        p.add("vector", lambda e: e.tensor_scalar(
            out=A_sb[:], in0=negA_sb[:], scalar1=-1.0, scalar2=None,
            op0=ALU.mult), after=("negA",), mile="A_")
        p.add("gpsimd", lambda e: e.memset(ident[:], 0.0))
        p.add("gpsimd", lambda e: e.affine_select(
            out=ident[:], in_=ident[:], compare_op=ALU.not_equal, fill=1.0,
            base=0, pattern=[[-1, 128]], channel_multiplier=1), mile="idI")
        p.add("gpsimd", lambda e: e.memset(selB[:], 0.0))
        p.add("gpsimd", lambda e: e.memset(selC[:], 0.0))
        for n in range(NS):
            p.add("gpsimd", lambda e, n=n: e.affine_select(
                out=selB[:, n, :], in_=selB[:, n, :],
                compare_op=ALU.not_equal, fill=1.0, base=-(64 + n),
                pattern=[[0, 128]], channel_multiplier=1),
                mile=f"selB{n}" if n == NS - 1 else None)
            p.add("gpsimd", lambda e, n=n: e.affine_select(
                out=selC[:, n, :], in_=selC[:, n, :],
                compare_op=ALU.not_equal, fill=1.0, base=-(80 + n),
                pattern=[[0, 128]], channel_multiplier=1),
                mile="selCZ" if n == NS - 1 else None)
        p.add("gpsimd", lambda e: e.memset(ones_sb[:], 1.0), mile="ones")
        p.add("gpsimd", lambda e: e.memset(hcar[:], 0.0), mile="hcar0")
        p.add("gpsimd", lambda e: e.memset(halo[:], 0.0), mile="halo0")
        for ej in range(EJ):
            for k in range(DCV):
                p.add("vector", lambda e, ej=ej, k=k: e.tensor_scalar(
                    out=cdiag[:, ej * DCV + k, :], in0=ident[:],
                    scalar1=convw_sb[:, ej, k:k + 1], scalar2=None,
                    op0=ALU.mult), after=("idI", "cst3"),
                    mile=f"cd{ej}" if k == DCV - 1 else None)
            p.add("vector", lambda e, ej=ej: e.tensor_scalar(
                out=wddiag[:, ej, :], in0=ident[:],
                scalar1=wd_sb[:, ej:ej + 1], scalar2=None, op0=ALU.mult),
                after=("idI", "cst4"), mile=f"wdd{ej}")
        prev = "wd2C"
        for nm, src, dst, fold in (("in", in_wT, in_bfd, True),
                                   ("sk", skip_wT, skip_bfd, True)):
            for dc in range(DCH):
                p.add("sync", lambda e, src=src, dc=dc: e.dma_start(
                    out=x1w[:], in_=src[dc * 128:(dc + 1) * 128, :]),
                    after=(prev,), mile=f"{nm}L{dc}", dma=True)
                p.add("vector", lambda e, dc=dc: e.tensor_scalar(
                    out=wcast_sb[:], in0=x1w[:],
                    scalar1=normw_sb[:, dc:dc + 1], scalar2=None,
                    op0=ALU.mult), after=(f"{nm}L{dc}", "cst0"),
                    mile=f"{nm}C{dc}")
                p.add("sync", lambda e, dst=dst, dc=dc: e.dma_start(
                    out=dst[dc], in_=wcast_sb[:]),
                    after=(f"{nm}C{dc}",), mile=f"{nm}S{dc}", dma=True)
                prev = f"{nm}C{dc}"
        for ec in range(EJ):
            p.add("sync", lambda e, ec=ec: e.dma_start(
                out=x1w[:, 0:D], in_=out_wT[ec * 128:(ec + 1) * 128, :]),
                after=(prev,), mile=f"oL{ec}", dma=True)
            p.add("scalar", lambda e: e.copy(
                out=wcast_sb[:, 0:D], in_=x1w[:, 0:D]),
                after=(f"oL{ec}", f"oS{ec-1}" if ec else None),
                mile=f"oC{ec}")
            p.add("sync", lambda e, ec=ec: e.dma_start(
                out=out_bfd[ec], in_=wcast_sb[:, 0:D]),
                after=(f"oC{ec}",), mile=f"oS{ec}", dma=True)
            prev = f"oC{ec}"

        rs_free = [f"oS{EJ-1}"]
        x1b_free = ["wd2C"]
        xnfeat_free = [None]
        wbuf_free = [f"skS{DCH-1}"]
        xpre = Ring(2)
        pxr = Ring(2)
        pc_r = Ring(1)
        pbc_r = Ring(1)
        psel_r = Ring(2)
        py_r = Ring(1)
        ptr_r = Ring(1)
        dlt = Ring(2)
        dxr = Ring(2)
        ar = Ring(4)
        ur = Ring(4)
        hr = Ring(8)
        gr = Ring(4)
        bmb = Ring(4)
        cmb = Ring(4)
        dmr = Ring(2)
        sp1r = Ring(1)
        sp2r = Ring(1)
        xbf_free = {ej: ["A_"] for ej in range(EJ)}
        skip_free = {ej: [None] for ej in range(EJ)}
        ybf_free = {ej: [None] for ej in range(EJ)}
        dlr_free = [None]
        halo_v = {ej: "halo0" for ej in range(EJ)}
        gcols_free = ["hcar0"]
        hcar_mile = {ej: None for ej in range(EJ)}

        for ch in range(8):
            pfx = ch < 4
            t0 = (ch % 4) * C
            rsrc = resid_pfx if pfx else resid_own

            for tt in range(4):
                p.add("sync", lambda e, tt=tt, rsrc=rsrc, t0=t0: e.dma_start(
                    out=rs_tile[:], in_=rsrc[t0 + tt * 128:t0 + (tt + 1) * 128, :]),
                    after=tuple(rs_free), mile=f"rs{ch}_{tt}", dma=True)
                p.add("scalar", lambda e, tt=tt: e.activation(
                    out=sq_scr[:], in_=rs_tile[:], func=AF.Square,
                    accum_out=ssq[:, tt:tt + 1]),
                    after=(f"rs{ch}_{tt}",), mile=f"sq{ch}_{tt}")
                p.add("scalar", lambda e, tt=tt: e.activation(
                    out=mcol[:, tt:tt + 1], in_=ssq[:, tt:tt + 1],
                    func=AF.Sqrt, scale=INV_D, bias=EPS),
                    after=(f"sq{ch}_{tt}", "fcst",
                           f"xnc{ch-1}_{tt}" if ch else None),
                    strict=True, mile=f"ms{ch}_{tt}")
                p.add("vector", lambda e, tt=tt: e.reciprocal(
                    out=mcol[:, 4 + tt:5 + tt], in_=mcol[:, tt:tt + 1]),
                    after=(f"ms{ch}_{tt}",), mile=f"rst{ch}_{tt}")
                p.add("scalar", lambda e, tt=tt: e.activation(
                    out=x1b_t[:], in_=rs_tile[:], func=AF.Copy,
                    scale=mcol[:, 4 + tt:5 + tt]),
                    after=tuple([f"rst{ch}_{tt}"] + x1b_free),
                    mile=f"xnc{ch}_{tt}")
                rs_free = [f"xnc{ch}_{tt}"]
                slot, w = ptr_r.acquire()

                def trf(e, tt=tt):
                    for j in range(DCH - 1):
                        e.transpose(out=p_tr[:, j * 128:(j + 1) * 128],
                                    in_=x1b_t[:, j * 128:(j + 1) * 128],
                                    identity=ident[:])
                    return e.transpose(out=p_tr[:, (DCH - 1) * 128:DCH * 128],
                                       in_=x1b_t[:, (DCH - 1) * 128:DCH * 128],
                                       identity=ident[:])
                p.add("tensor", trf,
                      after=tuple([f"xnc{ch}_{tt}", "idI"] + w),
                      mile=f"tr{ch}_{tt}")
                x1b_free = [f"tr{ch}_{tt}"]
                p.add("vector", lambda e, tt=tt: e.tensor_copy(
                    out=xn_feat[:, :, tt * 128:(tt + 1) * 128],
                    in_=p_tr[:].rearrange("p (a b) -> p a b", a=DCH)),
                    after=tuple([f"tr{ch}_{tt}"] + xnfeat_free),
                    mile=f"xnf{ch}_{tt}")
                ptr_r.consumed(f"xnf{ch}_{tt}")

            p.add("sync", lambda e: e.dma_start(
                out=wbuf[:], in_=in_bfd.rearrange("a p b -> p a b")),
                after=tuple(wbuf_free + [f"inS{DCH-1}"]),
                mile=f"inL{ch}", dma=True)
            for ej in range(EJ):
                slot, w = pxr.acquire()

                def inmm(e, ej=ej, slot=slot):
                    for dc in range(DCH - 1):
                        e.matmul(out=p_x[:, slot, :],
                                 lhsT=wbuf[:, dc, ej * 128:(ej + 1) * 128],
                                 rhs=xn_feat[:, dc, :],
                                 start=(dc == 0), stop=False)
                    return e.matmul(out=p_x[:, slot, :],
                                    lhsT=wbuf[:, DCH - 1, ej * 128:(ej + 1) * 128],
                                    rhs=xn_feat[:, DCH - 1, :],
                                    start=False, stop=True)
                p.add("tensor", inmm,
                      after=tuple([f"xnf{ch}_3", f"inL{ch}"] + w),
                      mile=f"Bx{ch}_{ej}")
                xslot, xw = xpre.acquire()
                p.add("scalar", lambda e, slot=slot, xslot=xslot: e.copy(
                    out=xpre_r[:, xslot, 3:3 + C], in_=p_x[:, slot, :]),
                    after=tuple([f"Bx{ch}_{ej}"] + xw), mile=f"evX{ch}_{ej}")
                pxr.consumed(f"evX{ch}_{ej}")
                p.add("vector", lambda e, ej=ej, xslot=xslot: e.tensor_copy(
                    out=xpre_r[:, xslot, 0:3], in_=halo[:, ej, :]),
                    after=(f"evX{ch}_{ej}", halo_v[ej]), mile=f"hin{ch}_{ej}")
                p.add("vector", lambda e, ej=ej, xslot=xslot: e.tensor_copy(
                    out=halo[:, ej, :], in_=xpre_r[:, xslot, C:C + 3]),
                    after=(f"evX{ch}_{ej}",), mile=f"hout{ch}_{ej}")
                halo_v[ej] = f"hout{ch}_{ej}"
                _, cw = pc_r.acquire()

                def convmm(e, ej=ej, xslot=xslot):
                    for k in range(DCV - 1):
                        e.matmul(out=p_c[:], lhsT=cdiag[:, ej * DCV + k, :],
                                 rhs=xpre_r[:, xslot, k:k + C],
                                 start=(k == 0), stop=False)
                    return e.matmul(out=p_c[:],
                                    lhsT=cdiag[:, ej * DCV + DCV - 1, :],
                                    rhs=xpre_r[:, xslot, DCV - 1:DCV - 1 + C],
                                    start=False, stop=True)
                p.add("tensor", convmm,
                      after=tuple([f"evX{ch}_{ej}", f"hin{ch}_{ej}",
                                   f"cd{ej}"] + cw),
                      mile=f"Bc{ch}_{ej}")
                xpre.consumed(f"Bc{ch}_{ej}", f"hout{ch}_{ej}")
                p.add("scalar", lambda e, ej=ej: e.activation(
                    out=x_bf[:, ej, :], in_=p_c[:], func=AF.Silu,
                    bias=convb_sb[:, ej:ej + 1]),
                    after=tuple([f"Bc{ch}_{ej}", "cst2"] + xbf_free[ej]),
                    mile=f"evC{ch}_{ej}")
                pc_r.consumed(f"evC{ch}_{ej}")
                xbf_free[ej] = []

            if not pfx:
                p.add("sync", lambda e: e.dma_start(
                    out=wbuf[:], in_=skip_bfd.rearrange("a p b -> p a b")),
                    after=(f"Bx{ch}_{EJ-1}", f"skS{DCH-1}"),
                    mile=f"skL{ch}", dma=True)
                for ej in range(EJ):
                    slot, w = pxr.acquire()

                    def skmm(e, ej=ej, slot=slot):
                        for dc in range(DCH - 1):
                            e.matmul(out=p_x[:, slot, :],
                                     lhsT=wbuf[:, dc, ej * 128:(ej + 1) * 128],
                                     rhs=xn_feat[:, dc, :],
                                     start=(dc == 0), stop=False)
                        return e.matmul(
                            out=p_x[:, slot, :],
                            lhsT=wbuf[:, DCH - 1, ej * 128:(ej + 1) * 128],
                            rhs=xn_feat[:, DCH - 1, :], start=False, stop=True)
                    p.add("tensor", skmm,
                          after=tuple([f"skL{ch}"] + w), mile=f"Bs{ch}_{ej}")
                    p.add("scalar", lambda e, ej=ej, slot=slot: e.activation(
                        out=skip_s[:, ej, :], in_=p_x[:, slot, :],
                        func=AF.Silu),
                        after=tuple([f"Bs{ch}_{ej}"] + skip_free[ej]),
                        mile=f"evS{ch}_{ej}")
                    pxr.consumed(f"evS{ch}_{ej}")
                    skip_free[ej] = []
                wbuf_free = [f"Bs{ch}_{EJ-1}"]
                xnfeat_free = [f"Bs{ch}_{EJ-1}"]
            else:
                wbuf_free = [f"Bx{ch}_{EJ-1}"]
                xnfeat_free = [f"Bx{ch}_{EJ-1}"]

            wsb = wcatp_sb if pfx else wcat_sb
            _, bw = pbc_r.acquire()

            def wcmm(e, wsb=wsb):
                for kj in range(EJ - 1):
                    e.matmul(out=p_bc[0:96, :], lhsT=wsb[:, kj, 0:96],
                             rhs=x_bf[:, kj, :], start=(kj == 0), stop=False)
                return e.matmul(out=p_bc[0:96, :], lhsT=wsb[:, EJ - 1, 0:96],
                                rhs=x_bf[:, EJ - 1, :], start=False, stop=True)
            p.add("tensor", wcmm,
                  after=tuple([f"evC{ch}_{ej}" for ej in range(EJ)]
                              + ["wcpC"] + bw),
                  mile=f"Cmm{ch}")
            p.add("vector", lambda e: e.tensor_copy(
                out=dlrBC_bf[:], in_=p_bc[0:96, :]),
                after=tuple([f"Cmm{ch}"] + dlr_free), mile=f"dlrC{ch}")
            pbc_r.consumed(f"dlrC{ch}")
            dlr_free = []
            if dbg and ch == 4:
                p.add("sync", lambda e: e.dma_start(
                    out=dbg_t["dbg_dlr"], in_=dlrBC_bf[:]),
                    after=(f"dlrC{ch}",), mile="DBGdlr", dma=True)
                p.add("sync", lambda e: e.dma_start(
                    out=dbg_t["dbg_xbf"], in_=x_bf[:]),
                    after=tuple(f"evC{ch}_{ej}" for ej in range(EJ)),
                    mile="DBGxbf", dma=True)

            if stages < 2:
                dlr_free = [f"dlrC{ch}"]
                for ej in range(EJ):
                    xbf_free[ej] = [f"Cmm{ch}"]
                    skip_free[ej] = [f"evS{ch}_{ej}"] if not pfx else [None]
                continue

            for ej in range(EJ):
                _, bw = pbc_r.acquire()

                def wd2mm(e, ej=ej):
                    return e.matmul(out=p_bc[:],
                                    lhsT=wd2_sb[:, ej * 128:(ej + 1) * 128],
                                    rhs=dlrBC_bf[0:DR, :],
                                    start=True, stop=True)
                p.add("tensor", wd2mm,
                      after=tuple([f"dlrC{ch}", "wd2C"] + bw),
                      mile=f"Dw{ch}_{ej}")
                _, s1w = sp1r.acquire()
                p.add("scalar", lambda e, ej=ej: e.activation(
                    out=sp1[:], in_=p_bc[:], func=AF.Abs,
                    bias=wd2b_sb[:, ej:ej + 1]),
                    after=tuple([f"Dw{ch}_{ej}", "cst1"] + s1w))
                p.add("scalar", lambda e: e.activation(
                    out=sp1[:], in_=sp1[:], func=AF.Exp, scale=-1.0),
                    after=("fcst",))
                p.add("scalar", lambda e: e.activation(
                    out=sp1[:], in_=sp1[:], func=AF.Ln, bias=1.0))
                _, s2w = sp2r.acquire()
                p.add("scalar", lambda e, ej=ej: e.activation(
                    out=sp2[:], in_=p_bc[:], func=AF.Relu,
                    bias=wd2b_sb[:, ej:ej + 1]),
                    after=tuple(s2w), mile=f"spR{ch}_{ej}")
                pbc_r.consumed(f"spR{ch}_{ej}")
                dslot, dw = dlt.acquire()
                p.add("vector", lambda e, dslot=dslot: e.tensor_add(
                    out=delta_r[:, dslot, :], in0=sp1[:], in1=sp2[:]),
                    after=tuple([f"spR{ch}_{ej}"] + dw), mile=f"dlt{ch}_{ej}")
                sp1r.consumed(f"dlt{ch}_{ej}")
                sp2r.consumed(f"dlt{ch}_{ej}")
                if dbg and ch == 4 and ej < 2:
                    p.add("sync", lambda e, dslot=dslot, ej=ej: e.dma_start(
                        out=dbg_t["dbg_delta"][:, ej, :],
                        in_=delta_r[:, dslot, :]),
                        after=(f"dlt{ch}_{ej}",), mile=f"DBGd{ej}", dma=True)
                    dlt.consumed(f"DBGd{ej}")
                xslot_d, dxw = dxr.acquire()
                p.add("gpsimd", lambda e, dslot=dslot, xslot_d=xslot_d, ej=ej:
                      e.tensor_tensor(out=dx_r[:, xslot_d, :],
                                      in0=delta_r[:, dslot, :],
                                      in1=x_bf[:, ej, :], op=ALU.mult),
                      after=tuple([f"dlt{ch}_{ej}"] + dxw),
                      mile=f"dx{ch}_{ej}")
                xbf_free[ej].append(f"dx{ch}_{ej}")

                if pfx:
                    aslot, aw = ar.acquire()
                    p.add("vector", lambda e, dslot=dslot, aslot=aslot:
                          e.tensor_tensor_scan(
                              out=a_r[:, aslot, :], data0=ones_sb[:],
                              data1=delta_r[:, dslot, :], initial=0.0,
                              op0=ALU.mult, op1=ALU.add),
                          after=tuple([f"dlt{ch}_{ej}", "ones"] + aw),
                          mile=f"S{ch}_{ej}")
                    dlt.consumed(f"S{ch}_{ej}", f"dx{ch}_{ej}")
                    mslot, mw = dmr.acquire()
                    p.add("gpsimd", lambda e, aslot=aslot, mslot=mslot:
                          e.tensor_scalar(
                              out=Dm_r[:, mslot, :], in0=a_r[:, aslot, :],
                              scalar1=a_r[:, aslot, C - 1:C], scalar2=None,
                              op0=ALU.subtract),
                          after=tuple([f"S{ch}_{ej}"] + mw),
                          mile=f"Dm{ch}_{ej}")
                    for n in range(NS):
                        sslot, sw = psel_r.acquire()
                        p.add("tensor", lambda e, n=n, sslot=sslot: e.matmul(
                            out=p_sel[:, sslot, :], lhsT=selB[:, n, :],
                            rhs=dlrBC_bf[:], start=True, stop=True),
                            after=tuple([f"dlrC{ch}", f"selB{NS-1}"] + sw),
                            mile=f"pB{ch}_{ej}_{n}")
                        bslot, bw2 = bmb.acquire()
                        p.add("scalar", lambda e, sslot=sslot, bslot=bslot:
                              e.copy(out=Bmb_r[:, bslot, :],
                                     in_=p_sel[:, sslot, :]),
                              after=tuple([f"pB{ch}_{ej}_{n}"] + bw2),
                              mile=f"bm{ch}_{ej}_{n}")
                        psel_r.consumed(f"bm{ch}_{ej}_{n}")
                        uslot, uw = ur.acquire()
                        p.add("scalar", lambda e, mslot=mslot, uslot=uslot,
                              ej=ej, n=n: e.activation(
                                  out=u_r[:, uslot, :],
                                  in_=Dm_r[:, mslot, :], func=AF.Exp,
                                  scale=negA_sb[:, ej, n:n + 1]),
                              after=tuple([f"Dm{ch}_{ej}", "A_"] + uw),
                              mile=f"wx{ch}_{ej}_{n}")
                        gslot, gw = gr.acquire()
                        p.add("gpsimd" if n < 11 else "vector",
                              lambda e, uslot=uslot, bslot=bslot,
                              gslot=gslot: e.tensor_tensor(
                                  out=g_r[:, gslot, :],
                                  in0=u_r[:, uslot, :],
                                  in1=Bmb_r[:, bslot, :], op=ALU.mult),
                              after=tuple([f"wx{ch}_{ej}_{n}",
                                           f"bm{ch}_{ej}_{n}"] + gw),
                              mile=f"t{ch}_{ej}_{n}")
                        ur.consumed(f"t{ch}_{ej}_{n}")
                        bmb.consumed(f"t{ch}_{ej}_{n}")
                        p.add("vector",
                              lambda e, gslot=gslot, xslot_d=xslot_d,
                              n=n: e.scalar_tensor_tensor(
                                  out=gdead[:], in0=g_r[:, gslot, :],
                                  scalar=1.0, in1=dx_r[:, xslot_d, :],
                                  op0=ALU.mult, op1=ALU.mult,
                                  accum_out=Gcols[:, n:n + 1]),
                              after=tuple([f"t{ch}_{ej}_{n}", f"dx{ch}_{ej}"]
                                          + (gcols_free if n == 0 else [])),
                              mile=f"sg{ch}_{ej}_{n}")
                        gr.consumed(f"sg{ch}_{ej}_{n}")
                    dxr.consumed(f"sg{ch}_{ej}_{NS-1}")
                    dmr.consumed(f"wx{ch}_{ej}_{NS-1}")
                    p.add("scalar", lambda e, ej=ej, aslot=aslot:
                          e.activation(out=Pfac[:], in_=A_sb[:, ej, :],
                                       func=AF.Exp,
                                       scale=a_r[:, aslot, C - 1:C]),
                          after=(f"S{ch}_{ej}", "A_",
                                 f"hup{ch}_{ej-1}" if ej else
                                 (f"hup{ch-1}_{EJ-1}" if ch else None)),
                          mile=f"pf{ch}_{ej}")
                    ar.consumed(f"Dm{ch}_{ej}", f"pf{ch}_{ej}")
                    p.add("vector", lambda e, ej=ej: e.tensor_mul(
                        out=hcar[:, ej * NS:(ej + 1) * NS],
                        in0=hcar[:, ej * NS:(ej + 1) * NS], in1=Pfac[:]),
                        after=(f"pf{ch}_{ej}", "hcar0")
                              + ((hcar_mile[ej],) if hcar_mile[ej] else ()),
                        strict=True, mile=f"hu1{ch}_{ej}")
                    p.add("vector", lambda e, ej=ej: e.tensor_add(
                        out=hcar[:, ej * NS:(ej + 1) * NS],
                        in0=hcar[:, ej * NS:(ej + 1) * NS], in1=Gcols[:]),
                        after=(f"sg{ch}_{ej}_{NS-1}", f"hu1{ch}_{ej}"),
                        strict=True, mile=f"hup{ch}_{ej}")
                    hcar_mile[ej] = f"hup{ch}_{ej}"
                    gcols_free = [f"hup{ch}_{ej}"]
                else:
                    for n in range(NS):
                        sslot, sw = psel_r.acquire()
                        p.add("tensor", lambda e, n=n, sslot=sslot: e.matmul(
                            out=p_sel[:, sslot, :], lhsT=selB[:, n, :],
                            rhs=dlrBC_bf[:], start=True, stop=True),
                            after=tuple([f"dlrC{ch}", f"selB{NS-1}"] + sw),
                            mile=f"pB{ch}_{ej}_{n}")
                        bslot, bw2 = bmb.acquire()
                        p.add("scalar", lambda e, sslot=sslot, bslot=bslot:
                              e.copy(out=Bmb_r[:, bslot, :],
                                     in_=p_sel[:, sslot, :]),
                              after=tuple([f"pB{ch}_{ej}_{n}"] + bw2),
                              mile=f"bm{ch}_{ej}_{n}")
                        psel_r.consumed(f"bm{ch}_{ej}_{n}")
                        _, cw2 = pc_r.acquire()
                        p.add("tensor", lambda e, n=n: e.matmul(
                            out=p_c[:], lhsT=selC[:, n, :],
                            rhs=dlrBC_bf[:], start=True, stop=True),
                            after=tuple([f"dlrC{ch}", "selCZ"] + cw2),
                            mile=f"pC{ch}_{ej}_{n}")
                        cslot, cw3 = cmb.acquire()
                        p.add("scalar", lambda e, cslot=cslot: e.copy(
                            out=Cmb_r[:, cslot, :], in_=p_c[:]),
                            after=tuple([f"pC{ch}_{ej}_{n}"] + cw3),
                            mile=f"cm{ch}_{ej}_{n}")
                        pc_r.consumed(f"cm{ch}_{ej}_{n}")
                        aslot, aw = ar.acquire()
                        p.add("scalar", lambda e, dslot=dslot, aslot=aslot,
                              ej=ej, n=n: e.activation(
                                  out=a_r[:, aslot, :],
                                  in_=delta_r[:, dslot, :], func=AF.Exp,
                                  scale=A_sb[:, ej, n:n + 1]),
                              after=tuple([f"dlt{ch}_{ej}", "A_"] + aw),
                              mile=f"ax{ch}_{ej}_{n}")
                        uslot, uw = ur.acquire()
                        p.add("gpsimd" if n < NV else "vector",
                              lambda e, xslot_d=xslot_d,
                              bslot=bslot, uslot=uslot: e.tensor_tensor(
                                  out=u_r[:, uslot, :],
                                  in0=dx_r[:, xslot_d, :],
                                  in1=Bmb_r[:, bslot, :], op=ALU.mult),
                              after=tuple([f"dx{ch}_{ej}",
                                           f"bm{ch}_{ej}_{n}"] + uw),
                              mile=f"u{ch}_{ej}_{n}")
                        bmb.consumed(f"u{ch}_{ej}_{n}")
                        hslot, hw = hr.acquire()
                        p.add("vector", lambda e, aslot=aslot, uslot=uslot,
                              hslot=hslot, ej=ej, n=n: e.tensor_tensor_scan(
                                  out=h_ring[:, hslot, :],
                                  data0=a_r[:, aslot, :],
                                  data1=u_r[:, uslot, :],
                                  initial=hcar[:, ej * NS + n:ej * NS + n + 1],
                                  op0=ALU.mult, op1=ALU.add),
                              after=tuple([f"ax{ch}_{ej}_{n}",
                                           f"u{ch}_{ej}_{n}"] + hw
                                          + ([hcar_mile[ej]]
                                             if n == 0 and hcar_mile[ej]
                                             else [])),
                              strict=(n == 0), mile=f"sc{ch}_{ej}_{n}")
                        ar.consumed(f"sc{ch}_{ej}_{n}")
                        ur.consumed(f"sc{ch}_{ej}_{n}")
                        if n == 7 or n == NS - 1:
                            base = ej * NS + (0 if n == 7 else 8)
                            p.add("vector", lambda e, base=base: e.tensor_copy(
                                out=hcar[:, base:base + 8],
                                in_=h_ring[:, 0:8, C - 1:C]),
                                after=(f"sc{ch}_{ej}_{n}",), strict=True,
                                mile=f"cy{ch}_{ej}_{n}")
                            if n == NS - 1:
                                hcar_mile[ej] = f"cy{ch}_{ej}_{n}"
                        gslot, gw = gr.acquire()
                        p.add("gpsimd" if n < NV else "vector",
                              lambda e, hslot=hslot, cslot=cslot,
                              gslot=gslot: e.tensor_tensor(
                                  out=g_r[:, gslot, :],
                                  in0=h_ring[:, hslot, :],
                                  in1=Cmb_r[:, cslot, :], op=ALU.mult),
                              after=tuple([f"sc{ch}_{ej}_{n}",
                                           f"cm{ch}_{ej}_{n}"] + gw),
                              mile=f"g{ch}_{ej}_{n}")
                        cmb.consumed(f"g{ch}_{ej}_{n}")
                        hr.consumed(f"g{ch}_{ej}_{n}",
                                    f"cy{ch}_{ej}_{7 if n <= 7 else NS-1}"
                                    if (n == 7 or n == NS - 1) else None)
                        _, yw = (py_r.acquire() if n == 0 else (None, []))
                        p.add("tensor", lambda e, gslot=gslot, n=n: e.matmul(
                            out=p_y[:], lhsT=ident[:], rhs=g_r[:, gslot, :],
                            start=(n == 0), stop=False),
                            after=tuple([f"g{ch}_{ej}_{n}"] + yw),
                            mile=f"ya{ch}_{ej}_{n}")
                        gr.consumed(f"ya{ch}_{ej}_{n}")
                    dxr.consumed(f"u{ch}_{ej}_{NS-1}")
                    dlt.consumed(f"ax{ch}_{ej}_{NS-1}", f"dx{ch}_{ej}")
                    p.add("tensor", lambda e, ej=ej: e.matmul(
                        out=p_y[:], lhsT=wddiag[:, ej, :],
                        rhs=x_bf[:, ej, :], start=False, stop=True),
                        after=(f"wdd{ej}", f"evC{ch}_{ej}"),
                        mile=f"ywd{ch}_{ej}")
                    xbf_free[ej].append(f"ywd{ch}_{ej}")
                    p.add("vector", lambda e, ej=ej: e.tensor_tensor(
                        out=y_bf[:, ej, :], in0=p_y[:],
                        in1=skip_s[:, ej, :], op=ALU.mult),
                        after=tuple([f"ywd{ch}_{ej}", f"evS{ch}_{ej}"]
                                    + ybf_free[ej]),
                        mile=f"yev{ch}_{ej}")
                    py_r.consumed(f"yev{ch}_{ej}")
                    skip_free[ej] = [f"yev{ch}_{ej}"]
                    ybf_free[ej] = []
            dlr_free = ([f"pB{ch}_{EJ-1}_{NS-1}"]
                        if pfx else [f"ywd{ch}_{EJ-1}"])
            for _ej in range(EJ):
                xbf_free[_ej].append(f"Cmm{ch}")
            if dbg and ch == 3 and stages >= 2:
                p.add("sync", lambda e: e.dma_start(
                    out=dbg_t["dbg_hcar"], in_=hcar[:]),
                    after=(f"hup{ch}_{EJ-1}",), mile="DBGh", dma=True)
            if dbg and ch == 4 and stages >= 2:
                p.add("sync", lambda e: e.dma_start(
                    out=dbg_t["dbg_y"], in_=y_bf[:]),
                    after=tuple(f"yev{ch}_{ej}" for ej in range(EJ)),
                    mile="DBGy", dma=True)

            if pfx or stages < 3:
                if stages < 3 and not pfx:
                    for ej in range(EJ):
                        ybf_free[ej] = [f"yev{ch}_{ej}"]
                continue

            p.add("sync", lambda e: e.dma_start(
                out=wbuf[:], in_=out_bfd.rearrange("a p b -> p a b")),
                after=tuple(wbuf_free + [f"oS{EJ-1}"]),
                mile=f"oLd{ch}", dma=True)
            wbuf_free = []
            for lt in range(4):
                slot0, w0 = pxr.acquire()
                slot1, w1 = pxr.acquire()
                assert slot0 == 0 and slot1 == 1

                def outmm(e, lt=lt):
                    last = None
                    for half in range(2):
                        for ec in range(EJ):
                            last = e.matmul(
                                out=p_x[:, half, :],
                                lhsT=y_bf[:, ec, lt * 128:(lt + 1) * 128],
                                rhs=wbuf_o[:, ec, half * 512:(half + 1) * 512],
                                start=(ec == 0), stop=(ec == EJ - 1))
                    return last
                p.add("tensor", outmm,
                      after=tuple([f"yev{ch}_{EJ-1}", f"oLd{ch}"] + w0 + w1),
                      mile=f"Fmm{ch}_{lt}")
                p.add("sync", lambda e, lt=lt, t0=t0: e.dma_start(
                    out=rs_tile[:],
                    in_=resid_own[t0 + lt * 128:t0 + (lt + 1) * 128, :]),
                    after=tuple(rs_free), mile=f"rs2{ch}_{lt}", dma=True)
                p.add("vector", lambda e: e.scalar_tensor_tensor(
                    out=x1w[:, 0:D], in0=rs_tile[:], scalar=2.0,
                    in1=p_x[:].rearrange("p a b -> p (a b)"),
                    op0=ALU.mult, op1=ALU.add),
                    after=(f"Fmm{ch}_{lt}", f"rs2{ch}_{lt}"),
                    mile=f"x1p{ch}_{lt}")
                pxr.consumed(f"x1p{ch}_{lt}")
                pxr.consumed(f"x1p{ch}_{lt}")
                rs_free = [f"x1p{ch}_{lt}"]
                if dbg and ch == 4 and lt == 0:
                    p.add("sync", lambda e: e.dma_start(
                        out=dbg_t["dbg_x1p"], in_=x1w[:, 0:D]),
                        after=(f"x1p{ch}_{lt}",), mile="DBGxp", dma=True)
                p.add("vector", lambda e: e.reduce_sum(
                    out=fmca[:, 0:1], in_=x1w[:, 0:D], axis=AX.X),
                    mile=f"msm{ch}_{lt}")
                p.add("vector", lambda e: e.tensor_scalar(
                    out=fmca[:, 1:2], in0=fmca[:, 0:1], scalar1=INV_D,
                    scalar2=None, op0=ALU.mult),
                    after=(f"msm{ch}_{lt}",), strict=True,
                    mile=f"m2{ch}_{lt}")
                p.add("vector", lambda e: e.tensor_scalar(
                    out=x1w[:, 0:D], in0=x1w[:, 0:D],
                    scalar1=fmca[:, 1:2], scalar2=None, op0=ALU.subtract),
                    after=(f"m2{ch}_{lt}",) + (("DBGxp",)
                          if (dbg and ch == 4 and lt == 0) else ()),
                    strict=True,
                    mile=f"ctr{ch}_{lt}")
                p.add("scalar", lambda e: e.activation(
                    out=sq_scr[:], in_=x1w[:, 0:D], func=AF.Square,
                    accum_out=fmca[:, 2:3]),
                    after=(f"ctr{ch}_{lt}",), mile=f"vq{ch}_{lt}")
                p.add("scalar", lambda e: e.activation(
                    out=fmca[:, 3:4], in_=fmca[:, 2:3], func=AF.Sqrt,
                    scale=INV_D, bias=EPS),
                    after=(f"vq{ch}_{lt}",), strict=True, mile=f"vs{ch}_{lt}")
                p.add("vector", lambda e: e.reciprocal(
                    out=fmca[:, 2:3], in_=fmca[:, 3:4]),
                    after=(f"vs{ch}_{lt}",), mile=f"rc{ch}_{lt}")
                p.add("vector", lambda e: e.tensor_scalar(
                    out=x1b_t[:], in0=x1w[:, 0:D], scalar1=fmca[:, 2:3],
                    scalar2=None, op0=ALU.mult),
                    after=tuple([f"rc{ch}_{lt}"] + x1b_free), strict=True,
                    mile=f"x1c{ch}_{lt}")
                if dbg and ch == 4 and lt == 0:
                    p.add("sync", lambda e: e.dma_start(
                        out=dbg_t["dbg_st"], in_=fmca[:]),
                        after=(f"x1c{ch}_{lt}",), mile="DBGst", dma=True)
                p.add("sync", lambda e, lt=lt, t0=t0: e.dma_start(
                    out=x1td[t0 + lt * 128:t0 + (lt + 1) * 128, :],
                    in_=x1b_t[:]),
                    after=(f"x1c{ch}_{lt}",), mile=f"x1S{ch}_{lt}", dma=True)
                if dbg:
                    p.add("sync", lambda e, lt=lt, t0=t0: e.dma_start(
                        out=dbg_t["dbg_x1"][t0 + lt * 128:t0 + (lt + 1) * 128, :],
                        in_=x1b_t[:]),
                        after=(f"x1c{ch}_{lt}",), mile=f"DBGx1{ch}_{lt}",
                        dma=True)
                x1b_free = ([f"x1S{ch}_{lt}"]
                            + ([f"DBGx1{ch}_{lt}"] if dbg else []))
            for ej in range(EJ):
                ybf_free[ej] = [f"Fmm{ch}_3"]
            wbuf_free = [f"Fmm{ch}_3"]

        if stages >= 4:
            attn.close()
            p.barrier_counts()

            def sbF(name, shape, dtype):
                return top.enter_context(nc.sbuf_tensor(name, shape, dtype))

            x1f = sbF("x1f", [128, DCH, TOK], BF16)
            out_acc = sbF("out_acc", [128, 16, D], F32)
            land = sbF("land", [128, DCH * FI], F32)
            gwb = sbF("gwb", [128, 2, DCH, FI], BF16)
            uwb = sbF("uwb", [128, 2, DCH, FI], BF16)
            dwb = sbF("dwb", [128, 2, 4, D], BF16)
            acts = sbF("acts", [128, 2, 4, 512], BF16)
            gc_r = sbF("gc_r", [128, 2, 512], BF16)
            sg_r = sbF("sg_r", [128, 2, 512], BF16)
            up1_r = sbF("up1_r", [128, 2, 512], BF16)
            up2_r = sbF("up2_r", [128, 2, 512], BF16)
            glu_r = sbF("glu_r", [128, 2, 512], BF16)
            gc_bf = sbF("gc_bf", [128, 512], BF16)
            sg_bf = sbF("sg_bf", [128, 512], BF16)
            up1_bf = sbF("up1_bf", [128, 512], BF16)
            up2_bf = sbF("up2_bf", [128, 512], BF16)
            glu_bf = sbF("glu_bf", [128, 512], BF16)
            x1tok = sbF("x1tok", [128, D], BF16)
            lgt = sbF("lgt", [128, EXP], F32)
            fm1 = sbF("fm1", [128, EXP], F32)
            fl2 = sbF("fl2", [128, EXP], F32)
            fmk2 = sbF("fmk2", [128, EXP], F32)
            scores = sbF("scores", [128, 16, EXP], F32)
            sc_bf = sbF("sc_bf", [128, EXP], BF16)
            fcol = sbF("fcol", [128, 8], F32)
            srT = sbF("srT", [16, TOK], BF16)
            identF = sbF("identF", [128, 128], BF16)
            router_bf = sbF("router_bf", [128, DCH, EXP], BF16)
            downb_bf = sbF("downb_bf", [16, D], BF16)
            gb_sb = sbF("gb_sb", [128, EXP, 4], F32)
            ub_sb = sbF("ub_sb", [128, EXP, 4], F32)
            flw = sbF("flw", [128, D], F32)
            fmc = sbF("fmc", [128, 4], F32)
            fsq = sbF("fsq", [128, 512], F32)

            p_ftr = top.enter_context(nc.psum_tensor("p_ftr", [128, 1024], BF16))
            p_ftr2 = top.enter_context(nc.psum_tensor("p_ftr2", [128, 512], F32))
            p_fg = top.enter_context(nc.psum_tensor("p_fg", [128, 2, 512], F32))
            p_fu = top.enter_context(nc.psum_tensor("p_fu", [128, 2, 512], F32))
            p_fd = top.enter_context(nc.psum_tensor("p_fd", [128, 1024], F32))


            p.add("gpsimd", lambda e: e.memset(identF[:], 0.0))
            p.add("gpsimd", lambda e: e.affine_select(
                out=identF[:], in_=identF[:], compare_op=ALU.not_equal,
                fill=1.0, base=0, pattern=[[-1, 128]], channel_multiplier=1),
                mile="FidI")
            p.add("sync", lambda e: e.dma_start(
                out=flw[:, 0:DCH * EXP].rearrange("p (a b) -> p a b", a=DCH),
                in_=router_d.rearrange("(a p) b -> p a b", p=128)),
                mile="rLd", dma=True)
            p.add("scalar", lambda e: e.copy(
                out=router_bf[:],
                in_=flw[:, 0:DCH * EXP].rearrange("p (a b) -> p a b", a=DCH)),
                after=("rLd",), mile="rC")
            p.add("sync", lambda e: e.dma_start(
                out=flw[0:16, 0:D], in_=downb_d), after=("rC",),
                mile="dbLd", dma=True)
            p.add("scalar", lambda e: e.copy(
                out=downb_bf[:], in_=flw[0:16, 0:D]), after=("dbLd",),
                mile="dbC")
            p.add("sync", lambda e: e.dma_start(out=gb_sb[:], in_=gateb_d),
                  mile="gbL", dma=True)
            p.add("sync", lambda e: e.dma_start(out=ub_sb[:], in_=upb_d),
                  mile="ubL", dma=True)

            x1tok_free = ["dbC"]
            for tt in range(16):
                p.add("sync", lambda e, tt=tt: e.dma_start(
                    out=x1tok[:], in_=x1td[tt * 128:(tt + 1) * 128, :]),
                    after=tuple(x1tok_free), mile=f"fxL{tt}", dma=True)

                def ftr(e, tt=tt):
                    for j in range(DCH - 1):
                        e.transpose(out=p_ftr[:, j * 128:(j + 1) * 128],
                                    in_=x1tok[:, j * 128:(j + 1) * 128],
                                    identity=identF[:])
                    return e.transpose(
                        out=p_ftr[:, (DCH - 1) * 128:DCH * 128],
                        in_=x1tok[:, (DCH - 1) * 128:DCH * 128],
                        identity=identF[:])
                p.add("tensor", ftr,
                      after=(f"fxL{tt}", "FidI",
                             f"fxE{tt-1}" if tt else None),
                      mile=f"fxT{tt}")
                x1tok_free = [f"fxT{tt}"]
                p.add("vector", lambda e, tt=tt: e.tensor_copy(
                    out=x1f[:, :, tt * 128:(tt + 1) * 128],
                    in_=p_ftr[:].rearrange("p (a b) -> p a b", a=DCH)),
                    after=(f"fxT{tt}",), mile=f"fxE{tt}")

            for tt in range(16):
                def rmm(e, tt=tt):
                    for dc in range(DCH - 1):
                        e.matmul(out=p_ftr2[:, 0:EXP],
                                 lhsT=x1f[:, dc, tt * 128:(tt + 1) * 128],
                                 rhs=router_bf[:, dc, :],
                                 start=(dc == 0), stop=False)
                    return e.matmul(
                        out=p_ftr2[:, 0:EXP],
                        lhsT=x1f[:, DCH - 1, tt * 128:(tt + 1) * 128],
                        rhs=router_bf[:, DCH - 1, :], start=False, stop=True)
                p.add("tensor", rmm,
                      after=(f"fxE{tt}", "rC",
                             f"rcp{tt-1}" if tt else None),
                      mile=f"rmm{tt}")
                p.add("vector", lambda e: e.tensor_copy(
                    out=lgt[:], in_=p_ftr2[:, 0:EXP]),
                    after=(f"rmm{tt}",), mile=f"rcp{tt}")
                p.add("vector", lambda e: e.reduce_max(
                    out=fcol[:, 0:1], in_=lgt[:], axis=AX.X),
                    after=(f"rcp{tt}",), strict=True, mile=f"rm1{tt}")
                p.add("vector", lambda e: e.tensor_scalar(
                    out=fm1[:], in0=lgt[:], scalar1=fcol[:, 0:1],
                    scalar2=None, op0=ALU.is_equal),
                    after=(f"rm1{tt}",), strict=True, mile=f"fm1m{tt}")

                p.add("vector", lambda e: e.scalar_tensor_tensor(
                    out=fl2[:], in0=fm1[:], scalar=-1e30, in1=lgt[:],
                    op0=ALU.mult, op1=ALU.add),
                    after=(f"fm1m{tt}",), strict=True, mile=f"fl2m{tt}")
                p.add("vector", lambda e: e.reduce_max(
                    out=fcol[:, 1:2], in_=fl2[:], axis=AX.X),
                    after=(f"fl2m{tt}",), strict=True, mile=f"rm2{tt}")
                p.add("vector", lambda e: e.tensor_scalar(
                    out=fmk2[:], in0=fl2[:], scalar1=fcol[:, 1:2],
                    scalar2=None, op0=ALU.is_equal),
                    after=(f"rm2{tt}",), strict=True, mile=f"mk2{tt}")
                p.add("vector", lambda e: e.tensor_scalar(
                    out=fcol[:, 2:3], in0=fcol[:, 1:2],
                    scalar1=fcol[:, 0:1], scalar2=None, op0=ALU.subtract),
                    after=(f"rm2{tt}",), strict=True, mile=f"d21{tt}")
                p.add("scalar", lambda e: e.activation(
                    out=fcol[:, 3:4], in_=fcol[:, 2:3], func=AF.Sigmoid),
                    after=(f"d21{tt}",), mile=f"w2{tt}")
                p.add("vector", lambda e: e.tensor_scalar(
                    out=fcol[:, 4:5], in0=fcol[:, 3:4], scalar1=-1.0,
                    scalar2=1.0, op0=ALU.mult, op1=ALU.add),
                    after=(f"w2{tt}", f"mk2{tt}"), strict=True,
                    mile=f"w1{tt}")
                p.add("vector", lambda e, tt=tt: e.tensor_scalar(
                    out=scores[:, tt, :], in0=fm1[:],
                    scalar1=fcol[:, 4:5], scalar2=None, op0=ALU.mult),
                    after=(f"w1{tt}",), strict=True, mile=f"sa{tt}")
                p.add("vector", lambda e, tt=tt: e.scalar_tensor_tensor(
                    out=scores[:, tt, :], in0=fmk2[:],
                    scalar=fcol[:, 3:4], in1=scores[:, tt, :],
                    op0=ALU.mult, op1=ALU.add),
                    after=(f"sa{tt}", f"mk2{tt}"), strict=True,
                    mile=f"scs{tt}")
                p.add("vector", lambda e, tt=tt: e.tensor_copy(
                    out=sc_bf[:], in_=scores[:, tt, :]),
                    after=(f"scs{tt}", f"sTe{tt-1}" if tt else None),
                    strict=True, mile=f"scb{tt}")
                p.add("tensor", lambda e: e.transpose(
                    out=p_ftr[0:EXP, 0:128], in_=sc_bf[:], identity=identF[:]),
                    after=(f"scb{tt}", f"fxE{15}",
                           f"sTe{tt-1}" if tt else None),
                    mile=f"sT{tt}")
                p.add("scalar", lambda e, tt=tt: e.copy(
                    out=srT[:, tt * 128:(tt + 1) * 128],
                    in_=p_ftr[0:EXP, 0:128]),
                    after=(f"sT{tt}",), mile=f"sTe{tt}")
                if dbg and tt == 0:
                    p.add("sync", lambda e: e.dma_start(
                        out=dbg_t["dbg_scores"], in_=scores[:, 0, :]),
                        after=("scs0",), mile="DBGsc", dma=True)
                    p.add("sync", lambda e: e.dma_start(
                        out=dbg_t["dbg_rt"][:, 0:16], in_=lgt[:]),
                        after=("scs0",), mile="DBGr1", dma=True)
                    p.add("sync", lambda e: e.dma_start(
                        out=dbg_t["dbg_rt"][:, 16:32], in_=fmk2[:]),
                        after=("scs0",), mile="DBGr2", dma=True)
                    p.add("sync", lambda e: e.dma_start(
                        out=dbg_t["dbg_rt"][:, 32:48], in_=fm1[:]),
                        after=("scs0",), mile="DBGr3", dma=True)
                    p.add("sync", lambda e: e.dma_start(
                        out=dbg_t["dbg_rt"][:, 48:56], in_=fcol[:]),
                        after=("scs0",), mile="DBGr4", dma=True)
                    p.add("vector", lambda e: e.engine_nop(),
                          after=("DBGr1", "DBGr2", "DBGr3", "DBGr4"),
                          mile="DBGrwait")

            for tt in range(16):
                def bmm(e, tt=tt):
                    e.matmul(out=p_fd[:, 0:512],
                             lhsT=srT[:, tt * 128:(tt + 1) * 128],
                             rhs=downb_bf[:, 0:512], start=True, stop=True)
                    return e.matmul(out=p_fd[:, 512:1024],
                                    lhsT=srT[:, tt * 128:(tt + 1) * 128],
                                    rhs=downb_bf[:, 512:1024],
                                    start=True, stop=True)
                p.add("tensor", bmm,
                      after=(f"sTe{tt}", "dbC", f"oaI{tt-1}" if tt else None),
                      mile=f"bmm{tt}")
                p.add("sync", lambda e, tt=tt: e.dma_start(
                    out=x1tok[:], in_=x1td[tt * 128:(tt + 1) * 128, :]),
                    after=tuple([f"bmm{tt}"] + x1tok_free),
                    mile=f"fx2L{tt}", dma=True)
                p.add("vector", lambda e, tt=tt: e.tensor_add(
                    out=out_acc[:, tt, :],
                    in0=p_fd[:], in1=x1tok[:]),
                    after=(f"bmm{tt}", f"fx2L{tt}"), mile=f"oaI{tt}")
                if dbg and tt == 0:
                    p.add("sync", lambda e: e.dma_start(
                        out=dbg_t["dbg_oa0"], in_=out_acc[:, 0, :]),
                        after=("oaI0",), mile="DBGoa0", dma=True)
                x1tok_free = [f"oaI{tt}"]

            pfg_r = Ring(2)
            pfu_r = Ring(2)
            gcr = Ring(2)
            sgr = Ring(2)
            u1r = Ring(2)
            u2r = Ring(2)
            glr = Ring(2)
            actr = Ring(2)
            acts_slot = {}
            prevload = "ubL"
            for ex in range(EXP):
                es = ex % 2
                p.add("sync", lambda e, ex=ex: e.dma_start(
                    out=land[:].rearrange("p (a b) -> p a b", a=DCH),
                    in_=gatew_d[ex].rearrange("(a p) b -> p a b", p=128)),
                    after=(prevload, f"gmmE{ex-2}" if ex >= 2 else None),
                    mile=f"gL{ex}", dma=True)
                p.add("scalar", lambda e, es=es: e.copy(
                    out=gwb[:, es, :, :],
                    in_=land[:].rearrange("p (a b) -> p a b", a=DCH)),
                    after=(f"gL{ex}",), mile=f"gC{ex}")
                p.add("sync", lambda e, ex=ex: e.dma_start(
                    out=land[:].rearrange("p (a b) -> p a b", a=DCH),
                    in_=upw_d[ex].rearrange("(a p) b -> p a b", p=128)),
                    after=(f"gC{ex}", f"ummE{ex-2}" if ex >= 2 else None),
                    mile=f"uL{ex}", dma=True)
                p.add("scalar", lambda e, es=es: e.copy(
                    out=uwb[:, es, :, :],
                    in_=land[:].rearrange("p (a b) -> p a b", a=DCH)),
                    after=(f"uL{ex}",), mile=f"uC{ex}")
                p.add("sync", lambda e, ex=ex: e.dma_start(
                    out=land[:, 0:4 * D].rearrange("p (a b) -> p a b", a=4),
                    in_=downw_d[ex].rearrange("(a p) b -> p a b", p=128)),
                    after=(f"uC{ex}", f"dmmE{ex-2}" if ex >= 2 else None),
                    mile=f"dL{ex}", dma=True)
                p.add("scalar", lambda e, es=es: e.copy(
                    out=dwb[:, es, :, :],
                    in_=land[:, 0:4 * D].rearrange("p (a b) -> p a b", a=4)),
                    after=(f"dL{ex}",), mile=f"dC{ex}")
                prevload = f"dC{ex}"
                for tsb in range(4):
                    acs_, acw_ = actr.acquire()
                    acts_slot[(ex, tsb)] = acs_
                    for fc in range(4):
                        gk = f"{ex}_{tsb}_{fc}"
                        gslot_, gw_ = pfg_r.acquire()

                        def gmm(e, es=es, tsb=tsb, fc=fc, s=gslot_):
                            last = None
                            for dc in range(DCH):
                                last = e.matmul(
                                    out=p_fg[:, s, :],
                                    lhsT=gwb[:, es, dc, fc * 128:(fc + 1) * 128],
                                    rhs=x1f[:, dc, tsb * 512:(tsb + 1) * 512],
                                    start=(dc == 0), stop=(dc == DCH - 1))
                            return last
                        p.add("tensor", gmm,
                              after=tuple([f"gC{ex}", f"fxE{15}"] + gw_),
                              mile=f"gmm{gk}" + ("E" if tsb == 3 and fc == 3
                                                 else ""))
                        gmm_m = f"gmm{gk}" + ("E" if tsb == 3 and fc == 3 else "")
                        gcs_, gcw_ = gcr.acquire()
                        p.add("vector", lambda e, ex=ex, fc=fc, s=gslot_,
                              o=gcs_: e.tensor_scalar(
                                  out=gc_r[:, o, :], in0=p_fg[:, s, :],
                                  scalar1=gb_sb[:, ex, fc:fc + 1],
                                  scalar2=7.0, op0=ALU.add, op1=ALU.min),
                              after=tuple([gmm_m, "gbL"] + gcw_),
                              mile=f"gev{gk}")
                        pfg_r.consumed(f"gev{gk}")
                        sgs_, sgw_ = sgr.acquire()
                        p.add("scalar", lambda e, i=gcs_, o=sgs_: e.activation(
                            out=sg_r[:, o, :], in_=gc_r[:, i, :],
                            func=AF.Sigmoid, scale=1.702),
                            after=tuple([f"gev{gk}"] + sgw_),
                            mile=f"sig{gk}")
                        uslot_, uw_ = pfu_r.acquire()

                        def umm(e, es=es, tsb=tsb, fc=fc, s=uslot_):
                            last = None
                            for dc in range(DCH):
                                last = e.matmul(
                                    out=p_fu[:, s, :],
                                    lhsT=uwb[:, es, dc, fc * 128:(fc + 1) * 128],
                                    rhs=x1f[:, dc, tsb * 512:(tsb + 1) * 512],
                                    start=(dc == 0), stop=(dc == DCH - 1))
                            return last
                        p.add("tensor", umm,
                              after=tuple([f"uC{ex}"] + uw_),
                              mile=f"umm{gk}" + ("E" if tsb == 3 and fc == 3
                                                 else ""))
                        umm_m = f"umm{gk}" + ("E" if tsb == 3 and fc == 3 else "")
                        u1s_, u1w_ = u1r.acquire()
                        p.add("vector", lambda e, ex=ex, fc=fc, s=uslot_,
                              o=u1s_: e.tensor_scalar(
                                  out=up1_r[:, o, :], in0=p_fu[:, s, :],
                                  scalar1=ub_sb[:, ex, fc:fc + 1],
                                  scalar2=7.0, op0=ALU.add, op1=ALU.min),
                              after=tuple([umm_m, "ubL"] + u1w_),
                              mile=f"uev{gk}")
                        pfu_r.consumed(f"uev{gk}")
                        u2s_, u2w_ = u2r.acquire()
                        p.add("gpsimd", lambda e, i=u1s_, o=u2s_: e.tensor_scalar(
                            out=up2_r[:, o, :], in0=up1_r[:, i, :],
                            scalar1=-7.0, scalar2=1.0, op0=ALU.max,
                            op1=ALU.add),
                            after=tuple([f"uev{gk}"] + u2w_),
                            mile=f"up2{gk}")
                        u1r.consumed(f"up2{gk}")
                        gls_, glw_ = glr.acquire()
                        p.add("gpsimd", lambda e, i1=gcs_, i2=sgs_, o=gls_:
                              e.tensor_tensor(
                                  out=glu_r[:, o, :], in0=gc_r[:, i1, :],
                                  in1=sg_r[:, i2, :], op=ALU.mult),
                              after=tuple([f"sig{gk}", f"gev{gk}"] + glw_),
                              mile=f"glu{gk}")
                        gcr.consumed(f"glu{gk}")
                        sgr.consumed(f"glu{gk}")
                        p.add("gpsimd", lambda e, i1=u2s_, i2=gls_, o=acs_,
                              fc=fc: e.tensor_tensor(
                                  out=acts[:, o, fc, :],
                                  in0=up2_r[:, i1, :],
                                  in1=glu_r[:, i2, :], op=ALU.mult),
                              after=tuple(
                                  [f"up2{gk}", f"glu{gk}",
                                   "DBGact" if dbg and ex == 0 and tsb == 1
                                   and fc == 0 else None]
                                  + (acw_ if fc == 0 else [])),
                              mile=f"act{gk}")
                        u2r.consumed(f"act{gk}")
                        glr.consumed(f"act{gk}")
                    if dbg and ex == 0 and tsb == 0:
                        p.add("sync", lambda e: e.dma_start(
                            out=dbg_t["dbg_act"],
                            in_=acts[:, acts_slot[(0, 0)], :, :]),
                            after=("act0_0_3",), mile="DBGact", dma=True)
                    for tt2 in range(4):
                        aslot2 = acts_slot[(ex, tsb)]

                        def dmm(e, es=es, tsb=tsb, tt2=tt2, s=aslot2):
                            last = None
                            for half in range(2):
                                for ic in range(4):
                                    last = e.matmul(
                                        out=p_fd[:, half * 512:(half + 1) * 512],
                                        lhsT=acts[:, s, ic,
                                                  tt2 * 128:(tt2 + 1) * 128],
                                        rhs=dwb[:, es, ic,
                                                half * 512:(half + 1) * 512],
                                        start=(ic == 0), stop=(ic == 3))
                            return last
                        dk = f"{ex}_{tsb}_{tt2}"
                        p.add("tensor", dmm,
                              after=(f"act{ex}_{tsb}_3", f"dC{ex}",
                                     f"oaI{15}",
                                     f"dev{ex}_{tsb}_{tt2-1}" if tt2 else
                                     (f"dev{ex}_{tsb-1}_3" if tsb else
                                      (f"dev{ex-1}_3_3" if ex else None))),
                              mile=f"dmm{dk}")
                        if tt2 == 3:
                            actr.consumed(f"dmm{dk}")
                        tg = tsb * 4 + tt2
                        p.add("vector", lambda e, ex=ex, tg=tg: (
                            e.scalar_tensor_tensor(
                                out=out_acc[:, tg, 0:512],
                                in0=p_fd[:, 0:512],
                                scalar=scores[:, tg, ex:ex + 1],
                                in1=out_acc[:, tg, 0:512],
                                op0=ALU.mult, op1=ALU.add),
                            e.scalar_tensor_tensor(
                                out=out_acc[:, tg, 512:1024],
                                in0=p_fd[:, 512:1024],
                                scalar=scores[:, tg, ex:ex + 1],
                                in1=out_acc[:, tg, 512:1024],
                                op0=ALU.mult, op1=ALU.add))[-1],
                            after=(f"dmm{dk}", f"oaI{tg}", f"scs{tg}"),
                            strict=True, mile=f"dev{ex}_{tsb}_{tt2}")
                p.ms[f"gmmE{ex}"] = p.ms[f"gmm{ex}_3_3E"]
                p.ms[f"ummE{ex}"] = p.ms[f"umm{ex}_3_3E"]
                p.ms[f"dmmE{ex}"] = p.ms[f"dmm{ex}_3_3"]

            if dbg:
                p.add("sync", lambda e: e.dma_start(
                    out=dbg_t["dbg_oaf"], in_=out_acc[:, 0, :]),
                    after=(f"dev{EXP-1}_0_0",), mile="DBGoaf", dma=True)
            outw = []
            for tt in range(16):
                p.add("vector", lambda e, tt=tt: e.reduce_sum(
                    out=fmc[:, 0:1], in_=out_acc[:, tt, :], axis=AX.X),
                    after=(f"dev{EXP-1}_{tt//4}_{tt%4}",), mile=f"Fms{tt}")
                p.add("vector", lambda e: e.tensor_scalar(
                    out=fmc[:, 1:2], in0=fmc[:, 0:1], scalar1=INV_D,
                    scalar2=None, op0=ALU.mult),
                    after=(f"Fms{tt}",), strict=True, mile=f"Fm2{tt}")
                p.add("vector", lambda e, tt=tt: e.tensor_scalar(
                    out=flw[:], in0=out_acc[:, tt, :], scalar1=fmc[:, 1:2],
                    scalar2=None, op0=ALU.subtract),
                    after=(f"Fm2{tt}", f"OUT{tt-1}" if tt else "dbC"),
                    strict=True, mile=f"Fctr{tt}")
                p.add("scalar", lambda e: e.activation(
                    out=fsq[:], in_=flw[:, 0:512], func=AF.Square,
                    accum_out=fmc[:, 2:3]),
                    after=(f"Fctr{tt}",))
                p.add("scalar", lambda e: e.activation(
                    out=fsq[:], in_=flw[:, 512:1024], func=AF.Square,
                    accum_out=fmc[:, 3:4]), mile=f"Fvq{tt}")
                p.add("vector", lambda e: e.tensor_add(
                    out=fmc[:, 2:3], in0=fmc[:, 2:3], in1=fmc[:, 3:4]),
                    after=(f"Fvq{tt}",), strict=True, mile=f"Fva{tt}")
                p.add("vector", lambda e: e.tensor_scalar(
                    out=fmc[:, 2:3], in0=fmc[:, 2:3], scalar1=INV_D,
                    scalar2=EPS, op0=ALU.mult, op1=ALU.add),
                    after=(f"Fva{tt}",), strict=True, mile=f"Fv{tt}")
                p.add("scalar", lambda e: e.activation(
                    out=fmc[:, 3:4], in_=fmc[:, 2:3], func=AF.Sqrt),
                    after=(f"Fv{tt}",), mile=f"Fsd{tt}")
                p.add("vector", lambda e: e.reciprocal(
                    out=fmc[:, 2:3], in_=fmc[:, 3:4]),
                    after=(f"Fsd{tt}",), mile=f"Frc{tt}")
                p.add("vector", lambda e: e.tensor_scalar(
                    out=flw[:], in0=flw[:], scalar1=fmc[:, 2:3],
                    scalar2=None, op0=ALU.mult),
                    after=(f"Frc{tt}",), strict=True, mile=f"Fsc{tt}")
                p.add("sync", lambda e, tt=tt: e.dma_start(
                    out=out_d[tt * 128:(tt + 1) * 128, :], in_=flw[:]),
                    after=(f"Fsc{tt}",), mile=f"OUT{tt}", dma=True)
                outw.append(f"OUT{tt}")

        if stages < 4:
            attn.close()

        qs_total = p.cnt["qS"]
        qg_total = p.cnt["qG"]
        p.ops["sync"].append(([("qS", qs_total)], lambda e: e.nop(), None))
        if qg_total:
            p.ops["gpsimd"].append(
                ([("qG", qg_total)], lambda e: e.nop(), None))

        with nc.Block() as block:
            p.emit_all(block, semh)

    return nc


_NC_CACHE = {}


def _get_nc(stages=5, dbg=False):
    key = (stages, dbg)
    if key not in _NC_CACHE:
        _NC_CACHE[key] = build(stages, dbg)
    return _NC_CACHE[key]


def make_in_maps(inp):
    f32 = np.float32
    resid = np.asarray(inp["L0_attn_resid"], f32)
    in_wT = np.ascontiguousarray(np.asarray(inp["L0_attn_in_w"], f32).T)
    skip_wT = np.ascontiguousarray(np.asarray(inp["L0_attn_skip_w"], f32).T)
    normw8 = np.ascontiguousarray(
        np.asarray(inp["L0_attn_norm_w"], f32).reshape(DCH, 128).T)
    wd1T = np.asarray(inp["L0_attn_wd1"], f32).T
    wBT = np.asarray(inp["L0_attn_wB"], f32).T
    wCT = np.asarray(inp["L0_attn_wC"], f32).T
    wcat = np.ascontiguousarray(np.concatenate([wd1T, wBT, wCT], axis=1))
    wcatp0 = wcat.copy()
    wcatp0[:, 64:80] = 0.0
    wd2T = np.ascontiguousarray(np.asarray(inp["L0_attn_wd2"], f32).T)
    wd2b = np.ascontiguousarray(
        np.asarray(inp["L0_attn_wd2_b"], f32).reshape(EJ, 128).T)
    convw = np.ascontiguousarray(
        np.asarray(inp["L0_attn_conv_w"], f32).reshape(EJ, 128, DCV)
        .transpose(1, 0, 2))
    convb = np.ascontiguousarray(
        np.asarray(inp["L0_attn_conv_b"], f32).reshape(EJ, 128).T)
    alog = np.ascontiguousarray(
        np.asarray(inp["L0_attn_A_log"], f32).reshape(EJ, 128, NS)
        .transpose(1, 0, 2))
    wdv = np.ascontiguousarray(
        np.asarray(inp["L0_attn_W_D"], f32).reshape(EJ, 128).T)
    out_wT = np.ascontiguousarray(np.asarray(inp["L0_attn_out_w"], f32).T)
    router = np.ascontiguousarray(np.asarray(inp["L0_ffn_router_weight"], f32))
    gup = np.asarray(inp["L0_ffn_gate_up_proj"], f32)
    gatew = np.ascontiguousarray(gup[:, :, 0::2])
    upw = np.ascontiguousarray(gup[:, :, 1::2])
    gupb = np.asarray(inp["L0_ffn_gate_up_proj_bias"], f32)
    gateb = np.ascontiguousarray(
        gupb[:, 0::2].reshape(EXP, 4, 128).transpose(2, 0, 1))
    upb = np.ascontiguousarray(
        gupb[:, 1::2].reshape(EXP, 4, 128).transpose(2, 0, 1))
    downw = np.ascontiguousarray(np.asarray(inp["L0_ffn_down_proj"], f32))
    downb = np.ascontiguousarray(np.asarray(inp["L0_ffn_down_proj_bias"], f32))
    zeros = np.zeros((TOK, D), f32)
    fconsts = np.tile(np.array(CONST_VALS + (0.0,) * (8 - len(CONST_VALS)),
                               f32), (128, 1))

    shared = dict(in_wT=in_wT, skip_wT=skip_wT, normw8=normw8, wcatT=wcat,
                  wd2T=wd2T, wd2b=wd2b, convw=convw, convb=convb, alog=alog,
                  wdvec=wdv, out_wT=out_wT, router=router, gatew=gatew,
                  fconsts=fconsts,
                  upw=upw, gateb=gateb, upb=upb, downw=downw, downb=downb)
    in_maps = []
    for c in range(8):
        b, r = c // 2, c % 2
        m = dict(shared)
        m["resid_own"] = np.ascontiguousarray(resid[b, r * TOK:(r + 1) * TOK])
        m["resid_pfx"] = zeros if r == 0 else np.ascontiguousarray(
            resid[b, 0:TOK])
        m["wcatTp"] = wcatp0 if r == 0 else wcat
        in_maps.append(m)
    return in_maps


def kernel(**inp):
    nc = _get_nc(5, False)
    in_maps = make_in_maps(inp)
    res = run_bass_kernel_spmd(nc, in_maps, core_ids=list(range(8)))
    out = np.empty((B, L, D), np.float32)
    for c in range(8):
        b, r = c // 2, c % 2
        out[b, r * TOK:(r + 1) * TOK] = res.results[c]["out"]
    return out


# version 05
# speedup vs baseline: 1.4190x; 1.3020x over previous
import numpy as np
from contextlib import ExitStack

import concourse.bass as bass
import concourse.mybir as mybir
from concourse.bass_utils import run_bass_kernel_spmd

F32 = mybir.dt.float32
BF16 = mybir.dt.bfloat16
AF = mybir.ActivationFunctionType
ALU = mybir.AluOpType
AX = mybir.AxisListType

B, L, D, E, NS, DR, DCV = 4, 4096, 1024, 2048, 16, 64, 4
NV = 6
EXP, FI = 16, 512
TOK = 2048
C = 512
NCH = TOK // C
EJ = E // 128
DCH = D // 128
EPS = 1e-5
INV_D = 1.0 / 1024.0
CONST_VALS = (EPS, INV_D, -1.0, 1.0, 1.702)

ENGINES = ("sync", "scalar", "vector", "tensor", "gpsimd")


class Ring:

    def __init__(self, depth):
        self.depth = depth
        self.hist = []

    def acquire(self):
        i = len(self.hist)
        slot = i % self.depth
        waits = list(self.hist[i - self.depth]) if i >= self.depth else []
        self.hist.append([])
        return slot, waits

    def consumed(self, *miles):
        self.hist[-1].extend(m for m in miles if m)


class Prog:
    def __init__(self):
        self.ops = {e: [] for e in ENGINES}
        self.cnt = {s: 0 for s in ("qS", "qG", "sS", "sV", "sP", "sG")}
        self.ms = {}
        self.wm = {}
        self.esem = {"sync": "qS", "scalar": "sS", "vector": "sV",
                     "tensor": "sP", "gpsimd": "sG"}
        self.pending = {e: [] for e in ENGINES}
        self.pending_counts = {}

    def barrier(self, names):
        for e in ENGINES:
            self.pending[e].extend(names)

    def barrier_counts(self):
        snap = [(s, v) for s, v in self.cnt.items() if v > 0]
        for e in ENGINES:
            self.pending_counts.setdefault(e, []).extend(snap)

    def add(self, eng, fn, after=(), mile=None, dma=False, strict=False):
        names = [m for m in after if m] + self.pending[eng]
        self.pending[eng] = []
        waits = []
        for s, v in self.pending_counts.pop(eng, []):
            own = (s == self.esem[eng]) and s not in ("qS", "qG")
            if own:
                continue
            key = (eng, s)
            if self.wm.get(key, -1) < v:
                self.wm[key] = v
                waits.append((s, v))
        my_sem = "qG" if (eng == "gpsimd" and dma) else self.esem[eng]
        for m in names:
            assert m in self.ms, f"unknown milestone {m}"
            s, v = self.ms[m]
            own = (s == self.esem[eng]) and s not in ("qS", "qG")
            if own and not strict:
                continue
            key = (eng, s)
            if self.wm.get(key, -1) >= v:
                continue
            self.wm[key] = v
            waits.append((s, v))
        inc = None
        if mile is not None:
            amt = 16 if dma else 1
            self.cnt[my_sem] += amt
            self.ms[mile] = (my_sem, self.cnt[my_sem])
            inc = (my_sem, amt)
        self.ops[eng].append((waits, fn, inc))
        return mile

    def emit_all(self, block, semh):
        for name in ENGINES:
            def section(eng_obj, name=name):
                for waits, fn, inc in self.ops[name]:
                    for (s, v) in waits:
                        eng_obj.wait_ge(semh[s], v)
                    inst = fn(eng_obj)
                    if inc is not None:
                        inst.then_inc(semh[inc[0]], inc[1])
            getattr(block, name)(section)


def build(stages=5, dbg=False):
    nc = bass.Bass("TRN2", target_bir_lowering=False)

    def din(name, shape, dtype=F32):
        return nc.dram_tensor(name, shape, dtype, kind="ExternalInput").ap()

    resid_own = din("resid_own", [TOK, D])
    resid_pfx = din("resid_pfx", [TOK, D])
    in_wT = din("in_wT", [D, E])
    skip_wT = din("skip_wT", [D, E])
    normw8 = din("normw8", [128, DCH])
    wcatT = din("wcatT", [E, 96])
    wcatTp = din("wcatTp", [E, 96])
    wd2T_d = din("wd2T", [DR, E])
    wd2b_d = din("wd2b", [128, EJ])
    convw_d = din("convw", [128, EJ, DCV])
    convb_d = din("convb", [128, EJ])
    alog_d = din("alog", [128, EJ, NS])
    wd_d = din("wdvec", [128, EJ])
    out_wT = din("out_wT", [E, D])
    router_d = din("router", [D, EXP])
    gatew_d = din("gatew", [EXP, D, FI])
    upw_d = din("upw", [EXP, D, FI])
    gateb_d = din("gateb", [128, EXP, FI // 128])
    upb_d = din("upb", [128, EXP, FI // 128])
    downw_d = din("downw", [EXP, FI, D])
    downb_d = din("downb", [EXP, D])
    fconst_d = din("fconsts", [128, 8])

    in_bfd = nc.dram_tensor("in_bfd", [DCH, 128, E], BF16, kind="Internal").ap()
    skip_bfd = nc.dram_tensor("skip_bfd", [DCH, 128, E], BF16, kind="Internal").ap()
    out_bfd = nc.dram_tensor("out_bfd", [EJ, 128, D], BF16, kind="Internal").ap()
    x1td = nc.dram_tensor("x1td", [TOK, D], BF16, kind="Internal").ap()
    out_d = nc.dram_tensor("out", [TOK, D], F32, kind="ExternalOutput").ap()
    dbg_t = {}
    if dbg:
        for nm, shape, dty in (
                ("dbg_xbf", [128, EJ, C], BF16), ("dbg_dlr", [96, C], BF16),
                ("dbg_delta", [128, 2, C], F32), ("dbg_y", [128, EJ, C], BF16),
                ("dbg_hcar", [128, 256], F32), ("dbg_x1", [TOK, D], BF16),
                ("dbg_x1p", [128, D], F32), ("dbg_st", [128, 4], F32),
                ("dbg_rt", [128, 64], F32), ("dbg_oa0", [128, D], F32),
                ("dbg_oaf", [128, D], F32), ("dbg_act", [128, 4, 512], BF16),
                ("dbg_scores", [128, EXP], F32)):
            dbg_t[nm] = nc.dram_tensor(nm, shape, dty, kind="ExternalOutput").ap()

    p = Prog()

    with ExitStack() as top:
        semh = {s: top.enter_context(nc.semaphore(s))
                for s in ("qS", "qG", "sS", "sV", "sP", "sG")}
        fconst = top.enter_context(nc.sbuf_tensor("fconst", [128, 8], F32))
        for _i, _v in enumerate(CONST_VALS):
            nc.const_aps.register(F32, _v, fconst[:, _i:_i + 1])
        attn = ExitStack()

        def sbA(name, shape, dtype):
            return attn.enter_context(nc.sbuf_tensor(name, shape, dtype))

        wbuf = sbA("wbuf", [128, DCH, E], BF16)
        wbuf_o = wbuf.ap().rearrange("p a b -> p (a b)").rearrange(
            "p (a b) -> p a b", a=EJ)
        wcat_sb = sbA("wcat_sb", [128, EJ, 96], BF16)
        wcatp_sb = sbA("wcatp_sb", [128, EJ, 96], BF16)
        wd2_sb = sbA("wd2_sb", [DR, E], BF16)
        ident = sbA("ident", [128, 128], BF16)
        cdiag = sbA("cdiag", [128, EJ * DCV, 128], BF16)
        wddiag = sbA("wddiag", [128, EJ, 128], BF16)
        A_sb = sbA("A_sb", [128, EJ, NS], F32)
        negA_sb = sbA("negA_sb", [128, EJ, NS], F32)
        normw_sb = sbA("normw_sb", [128, DCH], F32)
        wd2b_sb = sbA("wd2b_sb", [128, EJ], F32)
        convb_sb = sbA("convb_sb", [128, EJ], F32)
        convw_sb = sbA("convw_sb", [128, EJ, DCV], F32)
        wd_sb = sbA("wd_sb", [128, EJ], F32)
        alog_sb = sbA("alog_sb", [128, EJ, NS], F32)
        selB = sbA("selB", [96, NS, 128], BF16)
        selC = sbA("selC", [96, NS, 128], BF16)
        rs_tile = sbA("rs_tile", [128, D], F32)
        sp2 = rs_tile.ap()[:, 0:C]
        sp1 = rs_tile.ap()[:, C:2 * C]
        sq_scr = sbA("sq_scr", [128, D], F32)
        ssq = sbA("ssq", [128, 4], F32)
        mcol = sbA("mcol", [128, 8], F32)
        xn_feat = sbA("xn_feat", [128, DCH, C], BF16)
        xpre_r = sbA("xpre_r", [128, 2, C + 3], BF16)
        halo = sbA("halo", [128, EJ, 3], BF16)
        x_bf = sbA("x_bf", [128, EJ, C], BF16)
        skip_s = sbA("skip_s", [128, EJ, C], BF16)
        dlrBC_bf = sbA("dlrBC_bf", [96, C], BF16)

        delta_r = sbA("delta_r", [128, 2, C], F32)
        dx_r = sbA("dx_r", [128, 2, C], BF16)
        ones_sb = sbA("ones_sb", [128, C], BF16)
        a_r = sbA("a_r", [128, 2, C], F32)
        u_r = sbA("u_r", [128, 2, C], BF16)
        h_ring = sbA("h_ring", [128, 8, C], BF16)
        g_r = sbA("g_r", [128, 2, C], BF16)

        x1w_full = None
        Gcols = sbA("Gcols", [128, NS], F32)
        Pfac = sbA("Pfac", [128, NS], F32)
        hcar = sbA("hcar", [128, 256], F32)
        Bmb_all = sbA("Bmb_all", [128, NS, C], BF16)
        Cmb_all = sbA("Cmb_all", [128, NS, C], BF16)
        y_bf = sbA("y_bf", [128, EJ, C], BF16)
        wcast_sb = y_bf.ap().rearrange("p a b -> p (a b)")[:, 0:E]
        x1w = sbA("x1w", [128, E], F32)
        Dm_r = x1w.ap()[:, D:E].rearrange("p (a b) -> p a b", a=2)

        x1b_t = sbA("x1b_t", [128, D], BF16)
        gdead = x1b_t.ap()[:, 0:C]
        fmca = sbA("fmca", [128, 4], F32)

        p_tr = attn.enter_context(nc.psum_tensor("p_tr", [128, 1024], BF16))
        p_x = attn.enter_context(nc.psum_tensor("p_x", [128, 2, 512], F32))
        p_c = attn.enter_context(nc.psum_tensor("p_c", [128, 512], F32))
        p_bc = attn.enter_context(nc.psum_tensor("p_bc", [128, 512], F32))
        p_sel = attn.enter_context(nc.psum_tensor("p_sel", [128, 2, 512], F32))
        p_y = attn.enter_context(nc.psum_tensor("p_y", [128, 512], F32))

        p.add("sync", lambda e: e.dma_start(out=fconst[:], in_=fconst_d),
              mile="fcst", dma=True)
        csts = []
        for i, (sb, dr) in enumerate((
                (normw_sb, normw8), (wd2b_sb, wd2b_d), (convb_sb, convb_d),
                (convw_sb, convw_d), (wd_sb, wd_d), (alog_sb, alog_d))):
            p.add("sync", lambda e, sb=sb, dr=dr: e.dma_start(out=sb[:], in_=dr),
                  mile=f"cst{i}", dma=True)
            csts.append(f"cst{i}")
        prev = None
        for nm, src, dst in (("wc", wcatT, wcat_sb), ("wcp", wcatTp, wcatp_sb)):
            p.add("sync", lambda e, src=src: e.dma_start(
                out=x1w[:, 0:EJ * 96].rearrange("p (j k) -> p j k", j=EJ),
                in_=src.rearrange("(j p) k -> p j k", p=128)),
                after=(prev,), mile=f"{nm}L", dma=True)
            p.add("scalar", lambda e, dst=dst: e.copy(
                out=dst[:], in_=x1w[:, 0:EJ * 96].rearrange(
                    "p (j k) -> p j k", j=EJ)),
                after=(f"{nm}L",), mile=f"{nm}C")
            prev = f"{nm}C"
        p.add("sync", lambda e: e.dma_start(out=x1w[0:DR, 0:E], in_=wd2T_d),
              after=("wcpC",), mile="wd2L", dma=True)
        p.add("scalar", lambda e: e.copy(out=wd2_sb[:], in_=x1w[0:DR, 0:E]),
              after=("wd2L",), mile="wd2C")
        p.add("scalar", lambda e: e.activation(
            out=negA_sb[:], in_=alog_sb[:], func=AF.Exp), after=("cst5",),
            mile="negA")
        p.add("vector", lambda e: e.tensor_scalar(
            out=A_sb[:], in0=negA_sb[:], scalar1=-1.0, scalar2=None,
            op0=ALU.mult), after=("negA",), mile="A_")
        p.add("gpsimd", lambda e: e.memset(ident[:], 0.0))
        p.add("gpsimd", lambda e: e.affine_select(
            out=ident[:], in_=ident[:], compare_op=ALU.not_equal, fill=1.0,
            base=0, pattern=[[-1, 128]], channel_multiplier=1), mile="idI")
        p.add("gpsimd", lambda e: e.memset(selB[:], 0.0))
        p.add("gpsimd", lambda e: e.memset(selC[:], 0.0))
        for n in range(NS):
            p.add("gpsimd", lambda e, n=n: e.affine_select(
                out=selB[:, n, :], in_=selB[:, n, :],
                compare_op=ALU.not_equal, fill=1.0, base=-(64 + n),
                pattern=[[0, 128]], channel_multiplier=1),
                mile=f"selB{n}" if n == NS - 1 else None)
            p.add("gpsimd", lambda e, n=n: e.affine_select(
                out=selC[:, n, :], in_=selC[:, n, :],
                compare_op=ALU.not_equal, fill=1.0, base=-(80 + n),
                pattern=[[0, 128]], channel_multiplier=1),
                mile="selCZ" if n == NS - 1 else None)
        p.add("gpsimd", lambda e: e.memset(ones_sb[:], 1.0), mile="ones")
        p.add("gpsimd", lambda e: e.memset(hcar[:], 0.0), mile="hcar0")
        p.add("gpsimd", lambda e: e.memset(halo[:], 0.0), mile="halo0")
        for ej in range(EJ):
            for k in range(DCV):
                p.add("vector", lambda e, ej=ej, k=k: e.tensor_scalar(
                    out=cdiag[:, ej * DCV + k, :], in0=ident[:],
                    scalar1=convw_sb[:, ej, k:k + 1], scalar2=None,
                    op0=ALU.mult), after=("idI", "cst3"),
                    mile=f"cd{ej}" if k == DCV - 1 else None)
            p.add("vector", lambda e, ej=ej: e.tensor_scalar(
                out=wddiag[:, ej, :], in0=ident[:],
                scalar1=wd_sb[:, ej:ej + 1], scalar2=None, op0=ALU.mult),
                after=("idI", "cst4"), mile=f"wdd{ej}")
        prev = "wd2C"
        for nm, src, dst, fold in (("in", in_wT, in_bfd, True),
                                   ("sk", skip_wT, skip_bfd, True)):
            for dc in range(DCH):
                p.add("sync", lambda e, src=src, dc=dc: e.dma_start(
                    out=x1w[:], in_=src[dc * 128:(dc + 1) * 128, :]),
                    after=(prev,), mile=f"{nm}L{dc}", dma=True)
                p.add("vector", lambda e, dc=dc: e.tensor_scalar(
                    out=wcast_sb, in0=x1w[:],
                    scalar1=normw_sb[:, dc:dc + 1], scalar2=None,
                    op0=ALU.mult), after=(f"{nm}L{dc}", "cst0"),
                    mile=f"{nm}C{dc}")
                p.add("sync", lambda e, dst=dst, dc=dc: e.dma_start(
                    out=dst[dc], in_=wcast_sb),
                    after=(f"{nm}C{dc}",), mile=f"{nm}S{dc}", dma=True)
                prev = f"{nm}C{dc}"
        for ec in range(EJ):
            p.add("sync", lambda e, ec=ec: e.dma_start(
                out=x1w[:, 0:D], in_=out_wT[ec * 128:(ec + 1) * 128, :]),
                after=(prev,), mile=f"oL{ec}", dma=True)
            p.add("scalar", lambda e: e.copy(
                out=wcast_sb[:, 0:D], in_=x1w[:, 0:D]),
                after=(f"oL{ec}", f"oS{ec-1}" if ec else None),
                mile=f"oC{ec}")
            p.add("sync", lambda e, ec=ec: e.dma_start(
                out=out_bfd[ec], in_=wcast_sb[:, 0:D]),
                after=(f"oC{ec}",), mile=f"oS{ec}", dma=True)
            prev = f"oC{ec}"

        rs_free = [f"oS{EJ-1}"]
        x1b_free = ["wd2C"]
        xnfeat_free = [None]
        wbuf_free = [f"skS{DCH-1}"]
        xpre = Ring(2)
        pxr = Ring(2)
        pc_r = Ring(1)
        pbc_r = Ring(1)
        psel_r = Ring(2)
        py_r = Ring(1)
        ptr_r = Ring(1)
        dlt = Ring(2)
        dxr = Ring(2)
        ar = Ring(2)
        ur = Ring(2)
        hr = Ring(8)
        gr = Ring(2)
        bmb = Ring(4)
        cmb = Ring(4)
        dmr = Ring(2)
        sp1r = Ring(1)
        sp2r = Ring(1)
        xbf_free = {ej: ["A_"] for ej in range(EJ)}
        skip_free = {ej: [None] for ej in range(EJ)}
        ybf_free = {ej: [None] for ej in range(EJ)}
        dlr_free = [None]
        halo_v = {ej: "halo0" for ej in range(EJ)}
        gcols_free = ["hcar0"]
        hcar_mile = {ej: None for ej in range(EJ)}

        for ch in range(8):
            pfx = ch < 4
            t0 = (ch % 4) * C
            rsrc = resid_pfx if pfx else resid_own

            for tt in range(4):
                p.add("sync", lambda e, tt=tt, rsrc=rsrc, t0=t0: e.dma_start(
                    out=rs_tile[:], in_=rsrc[t0 + tt * 128:t0 + (tt + 1) * 128, :]),
                    after=tuple(rs_free) + ((f"dlt{ch-1}_{EJ-1}",)
                                            if tt == 0 and ch > 0 else ()),
                    mile=f"rs{ch}_{tt}", dma=True)
                p.add("scalar", lambda e, tt=tt: e.activation(
                    out=sq_scr[:], in_=rs_tile[:], func=AF.Square,
                    accum_out=ssq[:, tt:tt + 1]),
                    after=(f"rs{ch}_{tt}",), mile=f"sq{ch}_{tt}")
                p.add("scalar", lambda e, tt=tt: e.activation(
                    out=mcol[:, tt:tt + 1], in_=ssq[:, tt:tt + 1],
                    func=AF.Sqrt, scale=INV_D, bias=EPS),
                    after=(f"sq{ch}_{tt}", "fcst",
                           f"xnc{ch-1}_{tt}" if ch else None),
                    strict=True, mile=f"ms{ch}_{tt}")
                p.add("vector", lambda e, tt=tt: e.reciprocal(
                    out=mcol[:, 4 + tt:5 + tt], in_=mcol[:, tt:tt + 1]),
                    after=(f"ms{ch}_{tt}",), mile=f"rst{ch}_{tt}")
                p.add("scalar", lambda e, tt=tt: e.activation(
                    out=x1b_t[:], in_=rs_tile[:], func=AF.Copy,
                    scale=mcol[:, 4 + tt:5 + tt]),
                    after=tuple([f"rst{ch}_{tt}"] + x1b_free),
                    mile=f"xnc{ch}_{tt}")
                rs_free = [f"xnc{ch}_{tt}"]
                slot, w = ptr_r.acquire()

                def trf(e, tt=tt):
                    for j in range(DCH - 1):
                        e.transpose(out=p_tr[:, j * 128:(j + 1) * 128],
                                    in_=x1b_t[:, j * 128:(j + 1) * 128],
                                    identity=ident[:])
                    return e.transpose(out=p_tr[:, (DCH - 1) * 128:DCH * 128],
                                       in_=x1b_t[:, (DCH - 1) * 128:DCH * 128],
                                       identity=ident[:])
                p.add("tensor", trf,
                      after=tuple([f"xnc{ch}_{tt}", "idI"] + w),
                      mile=f"tr{ch}_{tt}")
                x1b_free = [f"tr{ch}_{tt}"]
                p.add("vector", lambda e, tt=tt: e.tensor_copy(
                    out=xn_feat[:, :, tt * 128:(tt + 1) * 128],
                    in_=p_tr[:].rearrange("p (a b) -> p a b", a=DCH)),
                    after=tuple([f"tr{ch}_{tt}"] + xnfeat_free),
                    mile=f"xnf{ch}_{tt}")
                ptr_r.consumed(f"xnf{ch}_{tt}")

            p.add("sync", lambda e: e.dma_start(
                out=wbuf[:], in_=in_bfd.rearrange("a p b -> p a b")),
                after=tuple(wbuf_free + [f"inS{DCH-1}"]),
                mile=f"inL{ch}", dma=True)
            for ej in range(EJ):
                slot, w = pxr.acquire()

                def inmm(e, ej=ej, slot=slot):
                    for dc in range(DCH - 1):
                        e.matmul(out=p_x[:, slot, :],
                                 lhsT=wbuf[:, dc, ej * 128:(ej + 1) * 128],
                                 rhs=xn_feat[:, dc, :],
                                 start=(dc == 0), stop=False)
                    return e.matmul(out=p_x[:, slot, :],
                                    lhsT=wbuf[:, DCH - 1, ej * 128:(ej + 1) * 128],
                                    rhs=xn_feat[:, DCH - 1, :],
                                    start=False, stop=True)
                p.add("tensor", inmm,
                      after=tuple([f"xnf{ch}_3", f"inL{ch}"] + w),
                      mile=f"Bx{ch}_{ej}")
                xslot, xw = xpre.acquire()
                p.add("scalar", lambda e, slot=slot, xslot=xslot: e.copy(
                    out=xpre_r[:, xslot, 3:3 + C], in_=p_x[:, slot, :]),
                    after=tuple([f"Bx{ch}_{ej}"] + xw), mile=f"evX{ch}_{ej}")
                pxr.consumed(f"evX{ch}_{ej}")
                p.add("vector", lambda e, ej=ej, xslot=xslot: e.tensor_copy(
                    out=xpre_r[:, xslot, 0:3], in_=halo[:, ej, :]),
                    after=(f"evX{ch}_{ej}", halo_v[ej]), mile=f"hin{ch}_{ej}")
                p.add("vector", lambda e, ej=ej, xslot=xslot: e.tensor_copy(
                    out=halo[:, ej, :], in_=xpre_r[:, xslot, C:C + 3]),
                    after=(f"evX{ch}_{ej}",), mile=f"hout{ch}_{ej}")
                halo_v[ej] = f"hout{ch}_{ej}"
                _, cw = pc_r.acquire()

                def convmm(e, ej=ej, xslot=xslot):
                    for k in range(DCV - 1):
                        e.matmul(out=p_c[:], lhsT=cdiag[:, ej * DCV + k, :],
                                 rhs=xpre_r[:, xslot, k:k + C],
                                 start=(k == 0), stop=False)
                    return e.matmul(out=p_c[:],
                                    lhsT=cdiag[:, ej * DCV + DCV - 1, :],
                                    rhs=xpre_r[:, xslot, DCV - 1:DCV - 1 + C],
                                    start=False, stop=True)
                p.add("tensor", convmm,
                      after=tuple([f"evX{ch}_{ej}", f"hin{ch}_{ej}",
                                   f"cd{ej}"] + cw),
                      mile=f"Bc{ch}_{ej}")
                xpre.consumed(f"Bc{ch}_{ej}", f"hout{ch}_{ej}")
                p.add("scalar", lambda e, ej=ej: e.activation(
                    out=x_bf[:, ej, :], in_=p_c[:], func=AF.Silu,
                    bias=convb_sb[:, ej:ej + 1]),
                    after=tuple([f"Bc{ch}_{ej}", "cst2"] + xbf_free[ej]),
                    mile=f"evC{ch}_{ej}")
                pc_r.consumed(f"evC{ch}_{ej}")
                xbf_free[ej] = []

            if not pfx:
                p.add("sync", lambda e: e.dma_start(
                    out=wbuf[:], in_=skip_bfd.rearrange("a p b -> p a b")),
                    after=(f"Bx{ch}_{EJ-1}", f"skS{DCH-1}"),
                    mile=f"skL{ch}", dma=True)
                for ej in range(EJ):
                    slot, w = pxr.acquire()

                    def skmm(e, ej=ej, slot=slot):
                        for dc in range(DCH - 1):
                            e.matmul(out=p_x[:, slot, :],
                                     lhsT=wbuf[:, dc, ej * 128:(ej + 1) * 128],
                                     rhs=xn_feat[:, dc, :],
                                     start=(dc == 0), stop=False)
                        return e.matmul(
                            out=p_x[:, slot, :],
                            lhsT=wbuf[:, DCH - 1, ej * 128:(ej + 1) * 128],
                            rhs=xn_feat[:, DCH - 1, :], start=False, stop=True)
                    p.add("tensor", skmm,
                          after=tuple([f"skL{ch}"] + w), mile=f"Bs{ch}_{ej}")
                    p.add("scalar", lambda e, ej=ej, slot=slot: e.activation(
                        out=skip_s[:, ej, :], in_=p_x[:, slot, :],
                        func=AF.Silu),
                        after=tuple([f"Bs{ch}_{ej}"] + skip_free[ej]),
                        mile=f"evS{ch}_{ej}")
                    pxr.consumed(f"evS{ch}_{ej}")
                    skip_free[ej] = []
                wbuf_free = [f"Bs{ch}_{EJ-1}"]
                xnfeat_free = [f"Bs{ch}_{EJ-1}"]
            else:
                wbuf_free = [f"Bx{ch}_{EJ-1}"]
                xnfeat_free = [f"Bx{ch}_{EJ-1}"]

            wsb = wcatp_sb if pfx else wcat_sb
            _, bw = pbc_r.acquire()

            def wcmm(e, wsb=wsb):
                for kj in range(EJ - 1):
                    e.matmul(out=p_bc[0:96, :], lhsT=wsb[:, kj, 0:96],
                             rhs=x_bf[:, kj, :], start=(kj == 0), stop=False)
                return e.matmul(out=p_bc[0:96, :], lhsT=wsb[:, EJ - 1, 0:96],
                                rhs=x_bf[:, EJ - 1, :], start=False, stop=True)
            p.add("tensor", wcmm,
                  after=tuple([f"evC{ch}_{ej}" for ej in range(EJ)]
                              + ["wcpC"] + bw),
                  mile=f"Cmm{ch}")
            p.add("vector", lambda e: e.tensor_copy(
                out=dlrBC_bf[:], in_=p_bc[0:96, :]),
                after=tuple([f"Cmm{ch}"] + dlr_free), mile=f"dlrC{ch}")
            pbc_r.consumed(f"dlrC{ch}")
            dlr_free = []
            if dbg and ch == 4:
                p.add("sync", lambda e: e.dma_start(
                    out=dbg_t["dbg_dlr"], in_=dlrBC_bf[:]),
                    after=(f"dlrC{ch}",), mile="DBGdlr", dma=True)
                p.add("sync", lambda e: e.dma_start(
                    out=dbg_t["dbg_xbf"], in_=x_bf[:]),
                    after=tuple(f"evC{ch}_{ej}" for ej in range(EJ)),
                    mile="DBGxbf", dma=True)

            if ch == 0:
                bc_last = []
            elif ch <= 4:
                bc_last = [f"t{ch-1}_{EJ-1}_{NS-1}", f"t{ch-1}_{EJ-1}_{4}"]
            else:
                bc_last = [f"u{ch-1}_{EJ-1}_{NS-1}", f"u{ch-1}_{EJ-1}_{4}",
                           f"g{ch-1}_{EJ-1}_{NS-1}", f"g{ch-1}_{EJ-1}_{4}"]
            if stages >= 2:
                for n in range(NS):
                    sslot, sw = psel_r.acquire()
                    p.add("tensor", lambda e, n=n, sslot=sslot: e.matmul(
                        out=p_sel[:, sslot, :], lhsT=selB[:, n, :],
                        rhs=dlrBC_bf[:], start=True, stop=True),
                        after=tuple([f"dlrC{ch}", f"selB{NS-1}"] + sw),
                        mile=f"pB{ch}_{n}")
                    p.add("scalar", lambda e, n=n, sslot=sslot: e.copy(
                        out=Bmb_all[:, n, :], in_=p_sel[:, sslot, :]),
                        after=tuple([f"pB{ch}_{n}"] + bc_last),
                        mile=f"bm{ch}_{n}")
                    psel_r.consumed(f"bm{ch}_{n}")
                    if not pfx:
                        sslot, sw = psel_r.acquire()
                        p.add("tensor", lambda e, n=n, sslot=sslot: e.matmul(
                            out=p_sel[:, sslot, :], lhsT=selC[:, n, :],
                            rhs=dlrBC_bf[:], start=True, stop=True),
                            after=tuple([f"dlrC{ch}", "selCZ"] + sw),
                            mile=f"pC{ch}_{n}")
                        p.add("scalar", lambda e, n=n, sslot=sslot: e.copy(
                            out=Cmb_all[:, n, :], in_=p_sel[:, sslot, :]),
                            after=tuple([f"pC{ch}_{n}"] + bc_last),
                            mile=f"cm{ch}_{n}")
                        psel_r.consumed(f"cm{ch}_{n}")
            if stages < 2:
                dlr_free = [f"dlrC{ch}"]
                for ej in range(EJ):
                    xbf_free[ej] = [f"Cmm{ch}"]
                    skip_free[ej] = [f"evS{ch}_{ej}"] if not pfx else [None]
                continue

            for ej in range(EJ):
                _, bw = pbc_r.acquire()

                def wd2mm(e, ej=ej):
                    return e.matmul(out=p_bc[:],
                                    lhsT=wd2_sb[:, ej * 128:(ej + 1) * 128],
                                    rhs=dlrBC_bf[0:DR, :],
                                    start=True, stop=True)
                p.add("tensor", wd2mm,
                      after=tuple([f"dlrC{ch}", "wd2C"] + bw),
                      mile=f"Dw{ch}_{ej}")
                _, s1w = sp1r.acquire()
                p.add("scalar", lambda e, ej=ej: e.activation(
                    out=sp1, in_=p_bc[:], func=AF.Abs,
                    bias=wd2b_sb[:, ej:ej + 1]),
                    after=tuple([f"Dw{ch}_{ej}", "cst1"] + s1w))
                p.add("scalar", lambda e: e.activation(
                    out=sp1, in_=sp1, func=AF.Exp, scale=-1.0),
                    after=("fcst",))
                p.add("scalar", lambda e: e.activation(
                    out=sp1, in_=sp1, func=AF.Ln, bias=1.0))
                _, s2w = sp2r.acquire()
                p.add("scalar", lambda e, ej=ej: e.activation(
                    out=sp2, in_=p_bc[:], func=AF.Relu,
                    bias=wd2b_sb[:, ej:ej + 1]),
                    after=tuple(s2w), mile=f"spR{ch}_{ej}")
                pbc_r.consumed(f"spR{ch}_{ej}")
                dslot, dw = dlt.acquire()
                p.add("vector", lambda e, dslot=dslot: e.tensor_add(
                    out=delta_r[:, dslot, :], in0=sp1, in1=sp2),
                    after=tuple([f"spR{ch}_{ej}"] + dw), mile=f"dlt{ch}_{ej}")
                sp1r.consumed(f"dlt{ch}_{ej}")
                sp2r.consumed(f"dlt{ch}_{ej}")
                if dbg and ch == 4 and ej < 2:
                    p.add("sync", lambda e, dslot=dslot, ej=ej: e.dma_start(
                        out=dbg_t["dbg_delta"][:, ej, :],
                        in_=delta_r[:, dslot, :]),
                        after=(f"dlt{ch}_{ej}",), mile=f"DBGd{ej}", dma=True)
                    dlt.consumed(f"DBGd{ej}")
                xslot_d, dxw = dxr.acquire()
                p.add("gpsimd", lambda e, dslot=dslot, xslot_d=xslot_d, ej=ej:
                      e.tensor_tensor(out=dx_r[:, xslot_d, :],
                                      in0=delta_r[:, dslot, :],
                                      in1=x_bf[:, ej, :], op=ALU.mult),
                      after=tuple([f"dlt{ch}_{ej}"] + dxw),
                      mile=f"dx{ch}_{ej}")
                xbf_free[ej].append(f"dx{ch}_{ej}")

                if pfx:
                    aslot, aw = ar.acquire()
                    p.add("vector", lambda e, dslot=dslot, aslot=aslot:
                          e.tensor_tensor_scan(
                              out=a_r[:, aslot, :], data0=ones_sb[:],
                              data1=delta_r[:, dslot, :], initial=0.0,
                              op0=ALU.mult, op1=ALU.add),
                          after=tuple([f"dlt{ch}_{ej}", "ones"] + aw),
                          mile=f"S{ch}_{ej}")
                    dlt.consumed(f"S{ch}_{ej}", f"dx{ch}_{ej}")
                    mslot, mw = dmr.acquire()
                    p.add("gpsimd", lambda e, aslot=aslot, mslot=mslot:
                          e.tensor_scalar(
                              out=Dm_r[:, mslot, :], in0=a_r[:, aslot, :],
                              scalar1=a_r[:, aslot, C - 1:C], scalar2=None,
                              op0=ALU.subtract),
                          after=tuple([f"S{ch}_{ej}"] + mw),
                          mile=f"Dm{ch}_{ej}")
                    for n in range(NS):
                        uslot, uw = ur.acquire()
                        p.add("scalar", lambda e, mslot=mslot, uslot=uslot,
                              ej=ej, n=n: e.activation(
                                  out=u_r[:, uslot, :],
                                  in_=Dm_r[:, mslot, :], func=AF.Exp,
                                  scale=negA_sb[:, ej, n:n + 1]),
                              after=tuple([f"Dm{ch}_{ej}", "A_"] + uw),
                              mile=f"wx{ch}_{ej}_{n}")
                        gslot, gw = gr.acquire()
                        p.add("gpsimd" if n < 5 else "vector",
                              lambda e, uslot=uslot, n=n,
                              gslot=gslot: e.tensor_tensor(
                                  out=g_r[:, gslot, :],
                                  in0=u_r[:, uslot, :],
                                  in1=Bmb_all[:, n, :], op=ALU.mult),
                              after=tuple([f"wx{ch}_{ej}_{n}",
                                           f"bm{ch}_{n}"] + gw),
                              mile=f"t{ch}_{ej}_{n}")
                        ur.consumed(f"t{ch}_{ej}_{n}")
                        p.add("vector",
                              lambda e, gslot=gslot, xslot_d=xslot_d,
                              n=n: e.scalar_tensor_tensor(
                                  out=gdead, in0=g_r[:, gslot, :],
                                  scalar=1.0, in1=dx_r[:, xslot_d, :],
                                  op0=ALU.mult, op1=ALU.mult,
                                  accum_out=Gcols[:, n:n + 1]),
                              after=tuple([f"t{ch}_{ej}_{n}", f"dx{ch}_{ej}"]
                                          + (gcols_free if n == 0 else [])),
                              mile=f"sg{ch}_{ej}_{n}")
                        gr.consumed(f"sg{ch}_{ej}_{n}")
                    dxr.consumed(f"sg{ch}_{ej}_{NS-1}")
                    dmr.consumed(f"wx{ch}_{ej}_{NS-1}")
                    p.add("scalar", lambda e, ej=ej, aslot=aslot:
                          e.activation(out=Pfac[:], in_=A_sb[:, ej, :],
                                       func=AF.Exp,
                                       scale=a_r[:, aslot, C - 1:C]),
                          after=(f"S{ch}_{ej}", "A_",
                                 f"hup{ch}_{ej-1}" if ej else
                                 (f"hup{ch-1}_{EJ-1}" if ch else None)),
                          mile=f"pf{ch}_{ej}")
                    ar.consumed(f"Dm{ch}_{ej}", f"pf{ch}_{ej}")
                    p.add("vector", lambda e, ej=ej: e.tensor_mul(
                        out=hcar[:, ej * NS:(ej + 1) * NS],
                        in0=hcar[:, ej * NS:(ej + 1) * NS], in1=Pfac[:]),
                        after=(f"pf{ch}_{ej}", "hcar0")
                              + ((hcar_mile[ej],) if hcar_mile[ej] else ()),
                        strict=True, mile=f"hu1{ch}_{ej}")
                    p.add("vector", lambda e, ej=ej: e.tensor_add(
                        out=hcar[:, ej * NS:(ej + 1) * NS],
                        in0=hcar[:, ej * NS:(ej + 1) * NS], in1=Gcols[:]),
                        after=(f"sg{ch}_{ej}_{NS-1}", f"hu1{ch}_{ej}"),
                        strict=True, mile=f"hup{ch}_{ej}")
                    hcar_mile[ej] = f"hup{ch}_{ej}"
                    gcols_free = [f"hup{ch}_{ej}"]
                    if ej == EJ - 1:
                        x1b_free = [f"sg{ch}_{ej}_{NS-1}"] + x1b_free
                else:
                    for n in range(NS):
                        aslot, aw = ar.acquire()
                        p.add("scalar", lambda e, dslot=dslot, aslot=aslot,
                              ej=ej, n=n: e.activation(
                                  out=a_r[:, aslot, :],
                                  in_=delta_r[:, dslot, :], func=AF.Exp,
                                  scale=A_sb[:, ej, n:n + 1]),
                              after=tuple([f"dlt{ch}_{ej}", "A_"] + aw),
                              mile=f"ax{ch}_{ej}_{n}")
                        uslot, uw = ur.acquire()
                        p.add("gpsimd" if n < 5 else "vector",
                              lambda e, xslot_d=xslot_d, n=n,
                              uslot=uslot: e.tensor_tensor(
                                  out=u_r[:, uslot, :],
                                  in0=dx_r[:, xslot_d, :],
                                  in1=Bmb_all[:, n, :], op=ALU.mult),
                              after=tuple([f"dx{ch}_{ej}",
                                           f"bm{ch}_{n}"] + uw),
                              mile=f"u{ch}_{ej}_{n}")
                        hslot, hw = hr.acquire()
                        p.add("vector", lambda e, aslot=aslot, uslot=uslot,
                              hslot=hslot, ej=ej, n=n: e.tensor_tensor_scan(
                                  out=h_ring[:, hslot, :],
                                  data0=a_r[:, aslot, :],
                                  data1=u_r[:, uslot, :],
                                  initial=hcar[:, ej * NS + n:ej * NS + n + 1],
                                  op0=ALU.mult, op1=ALU.add),
                              after=tuple([f"ax{ch}_{ej}_{n}",
                                           f"u{ch}_{ej}_{n}"] + hw
                                          + ([hcar_mile[ej]]
                                             if n == 0 and hcar_mile[ej]
                                             else [])),
                              strict=(n == 0), mile=f"sc{ch}_{ej}_{n}")
                        ar.consumed(f"sc{ch}_{ej}_{n}")
                        ur.consumed(f"sc{ch}_{ej}_{n}")
                        if n == 7 or n == NS - 1:
                            base = ej * NS + (0 if n == 7 else 8)
                            p.add("vector", lambda e, base=base: e.tensor_copy(
                                out=hcar[:, base:base + 8],
                                in_=h_ring[:, 0:8, C - 1:C]),
                                after=(f"sc{ch}_{ej}_{n}",), strict=True,
                                mile=f"cy{ch}_{ej}_{n}")
                            if n == NS - 1:
                                hcar_mile[ej] = f"cy{ch}_{ej}_{n}"
                        gslot, gw = gr.acquire()
                        p.add("gpsimd" if n < 5 else "vector",
                              lambda e, hslot=hslot, n=n,
                              gslot=gslot: e.tensor_tensor(
                                  out=g_r[:, gslot, :],
                                  in0=h_ring[:, hslot, :],
                                  in1=Cmb_all[:, n, :], op=ALU.mult),
                              after=tuple([f"sc{ch}_{ej}_{n}",
                                           f"cm{ch}_{n}"] + gw),
                              mile=f"g{ch}_{ej}_{n}")
                        hr.consumed(f"g{ch}_{ej}_{n}",
                                    f"cy{ch}_{ej}_{7 if n <= 7 else NS-1}"
                                    if (n == 7 or n == NS - 1) else None)
                        _, yw = (py_r.acquire() if n == 0 else (None, []))
                        p.add("tensor", lambda e, gslot=gslot, n=n: e.matmul(
                            out=p_y[:], lhsT=ident[:], rhs=g_r[:, gslot, :],
                            start=(n == 0), stop=False),
                            after=tuple([f"g{ch}_{ej}_{n}"] + yw),
                            mile=f"ya{ch}_{ej}_{n}")
                        gr.consumed(f"ya{ch}_{ej}_{n}")
                    dxr.consumed(f"u{ch}_{ej}_{NS-1}")
                    dlt.consumed(f"ax{ch}_{ej}_{NS-1}", f"dx{ch}_{ej}")
                    p.add("tensor", lambda e, ej=ej: e.matmul(
                        out=p_y[:], lhsT=wddiag[:, ej, :],
                        rhs=x_bf[:, ej, :], start=False, stop=True),
                        after=(f"wdd{ej}", f"evC{ch}_{ej}"),
                        mile=f"ywd{ch}_{ej}")
                    xbf_free[ej].append(f"ywd{ch}_{ej}")
                    p.add("vector", lambda e, ej=ej: e.tensor_tensor(
                        out=y_bf[:, ej, :], in0=p_y[:],
                        in1=skip_s[:, ej, :], op=ALU.mult),
                        after=tuple([f"ywd{ch}_{ej}", f"evS{ch}_{ej}"]
                                    + ybf_free[ej]),
                        mile=f"yev{ch}_{ej}")
                    py_r.consumed(f"yev{ch}_{ej}")
                    skip_free[ej] = [f"yev{ch}_{ej}"]
                    ybf_free[ej] = []
            dlr_free = ([f"pB{ch}_{NS-1}", f"Dw{ch}_{EJ-1}"]
                        if pfx else [f"ywd{ch}_{EJ-1}"])
            for _ej in range(EJ):
                xbf_free[_ej].append(f"Cmm{ch}")
            if dbg and ch == 3 and stages >= 2:
                p.add("sync", lambda e: e.dma_start(
                    out=dbg_t["dbg_hcar"], in_=hcar[:]),
                    after=(f"hup{ch}_{EJ-1}",), mile="DBGh", dma=True)
            if dbg and ch == 4 and stages >= 2:
                p.add("sync", lambda e: e.dma_start(
                    out=dbg_t["dbg_y"], in_=y_bf[:]),
                    after=tuple(f"yev{ch}_{ej}" for ej in range(EJ)),
                    mile="DBGy", dma=True)

            if pfx or stages < 3:
                if stages < 3 and not pfx:
                    for ej in range(EJ):
                        ybf_free[ej] = [f"yev{ch}_{ej}"]
                continue

            p.add("sync", lambda e: e.dma_start(
                out=wbuf[:], in_=out_bfd.rearrange("a p b -> p a b")),
                after=tuple(wbuf_free + [f"oS{EJ-1}"]),
                mile=f"oLd{ch}", dma=True)
            wbuf_free = []
            for lt in range(4):
                slot0, w0 = pxr.acquire()
                slot1, w1 = pxr.acquire()
                assert slot0 == 0 and slot1 == 1

                def outmm(e, lt=lt):
                    last = None
                    for half in range(2):
                        for ec in range(EJ):
                            last = e.matmul(
                                out=p_x[:, half, :],
                                lhsT=y_bf[:, ec, lt * 128:(lt + 1) * 128],
                                rhs=wbuf_o[:, ec, half * 512:(half + 1) * 512],
                                start=(ec == 0), stop=(ec == EJ - 1))
                    return last
                p.add("tensor", outmm,
                      after=tuple([f"yev{ch}_{EJ-1}", f"oLd{ch}"] + w0 + w1),
                      mile=f"Fmm{ch}_{lt}")
                p.add("sync", lambda e, lt=lt, t0=t0: e.dma_start(
                    out=rs_tile[:],
                    in_=resid_own[t0 + lt * 128:t0 + (lt + 1) * 128, :]),
                    after=tuple(rs_free) + ((f"dlt{ch}_{EJ-1}",)
                                            if lt == 0 else ()),
                    mile=f"rs2{ch}_{lt}", dma=True)
                p.add("vector", lambda e: e.scalar_tensor_tensor(
                    out=x1w[:, 0:D], in0=rs_tile[:], scalar=2.0,
                    in1=p_x[:].rearrange("p a b -> p (a b)"),
                    op0=ALU.mult, op1=ALU.add),
                    after=(f"Fmm{ch}_{lt}", f"rs2{ch}_{lt}"),
                    mile=f"x1p{ch}_{lt}")
                pxr.consumed(f"x1p{ch}_{lt}")
                pxr.consumed(f"x1p{ch}_{lt}")
                rs_free = [f"x1p{ch}_{lt}"]
                if dbg and ch == 4 and lt == 0:
                    p.add("sync", lambda e: e.dma_start(
                        out=dbg_t["dbg_x1p"], in_=x1w[:, 0:D]),
                        after=(f"x1p{ch}_{lt}",), mile="DBGxp", dma=True)
                p.add("vector", lambda e: e.reduce_sum(
                    out=fmca[:, 0:1], in_=x1w[:, 0:D], axis=AX.X),
                    mile=f"msm{ch}_{lt}")
                p.add("vector", lambda e: e.tensor_scalar(
                    out=fmca[:, 1:2], in0=fmca[:, 0:1], scalar1=INV_D,
                    scalar2=None, op0=ALU.mult),
                    after=(f"msm{ch}_{lt}",), strict=True,
                    mile=f"m2{ch}_{lt}")
                p.add("vector", lambda e: e.tensor_scalar(
                    out=x1w[:, 0:D], in0=x1w[:, 0:D],
                    scalar1=fmca[:, 1:2], scalar2=None, op0=ALU.subtract),
                    after=(f"m2{ch}_{lt}",) + (("DBGxp",)
                          if (dbg and ch == 4 and lt == 0) else ()),
                    strict=True,
                    mile=f"ctr{ch}_{lt}")
                p.add("scalar", lambda e: e.activation(
                    out=sq_scr[:], in_=x1w[:, 0:D], func=AF.Square,
                    accum_out=fmca[:, 2:3]),
                    after=(f"ctr{ch}_{lt}",), mile=f"vq{ch}_{lt}")
                p.add("scalar", lambda e: e.activation(
                    out=fmca[:, 3:4], in_=fmca[:, 2:3], func=AF.Sqrt,
                    scale=INV_D, bias=EPS),
                    after=(f"vq{ch}_{lt}",), strict=True, mile=f"vs{ch}_{lt}")
                p.add("vector", lambda e: e.reciprocal(
                    out=fmca[:, 2:3], in_=fmca[:, 3:4]),
                    after=(f"vs{ch}_{lt}",), mile=f"rc{ch}_{lt}")
                p.add("vector", lambda e: e.tensor_scalar(
                    out=x1b_t[:], in0=x1w[:, 0:D], scalar1=fmca[:, 2:3],
                    scalar2=None, op0=ALU.mult),
                    after=tuple([f"rc{ch}_{lt}"] + x1b_free), strict=True,
                    mile=f"x1c{ch}_{lt}")
                if dbg and ch == 4 and lt == 0:
                    p.add("sync", lambda e: e.dma_start(
                        out=dbg_t["dbg_st"], in_=fmca[:]),
                        after=(f"x1c{ch}_{lt}",), mile="DBGst", dma=True)
                p.add("sync", lambda e, lt=lt, t0=t0: e.dma_start(
                    out=x1td[t0 + lt * 128:t0 + (lt + 1) * 128, :],
                    in_=x1b_t[:]),
                    after=(f"x1c{ch}_{lt}",), mile=f"x1S{ch}_{lt}", dma=True)
                if dbg:
                    p.add("sync", lambda e, lt=lt, t0=t0: e.dma_start(
                        out=dbg_t["dbg_x1"][t0 + lt * 128:t0 + (lt + 1) * 128, :],
                        in_=x1b_t[:]),
                        after=(f"x1c{ch}_{lt}",), mile=f"DBGx1{ch}_{lt}",
                        dma=True)
                x1b_free = ([f"x1S{ch}_{lt}"]
                            + ([f"DBGx1{ch}_{lt}"] if dbg else []))
            for ej in range(EJ):
                ybf_free[ej] = [f"Fmm{ch}_3"]
            wbuf_free = [f"Fmm{ch}_3"]

        if stages >= 4:
            attn.close()
            p.barrier_counts()

            def sbF(name, shape, dtype):
                return top.enter_context(nc.sbuf_tensor(name, shape, dtype))

            x1f = sbF("x1f", [128, DCH, TOK], BF16)
            out_acc = sbF("out_acc", [128, 16, D], F32)
            land = sbF("land", [128, DCH * FI], F32)
            gwb = sbF("gwb", [128, 2, DCH, FI], BF16)
            uwb = sbF("uwb", [128, 2, DCH, FI], BF16)
            dwb = sbF("dwb", [128, 2, 4, D], BF16)
            acts = sbF("acts", [128, 2, 4, 512], BF16)
            gc_r = sbF("gc_r", [128, 2, 512], BF16)
            sg_r = sbF("sg_r", [128, 2, 512], BF16)
            up1_r = sbF("up1_r", [128, 2, 512], BF16)
            up2_r = sbF("up2_r", [128, 2, 512], BF16)
            glu_r = sbF("glu_r", [128, 2, 512], BF16)
            gc_bf = sbF("gc_bf", [128, 512], BF16)
            sg_bf = sbF("sg_bf", [128, 512], BF16)
            up1_bf = sbF("up1_bf", [128, 512], BF16)
            up2_bf = sbF("up2_bf", [128, 512], BF16)
            glu_bf = sbF("glu_bf", [128, 512], BF16)
            x1tok = sbF("x1tok", [128, D], BF16)
            lgt = sbF("lgt", [128, EXP], F32)
            fm1 = sbF("fm1", [128, EXP], F32)
            fl2 = sbF("fl2", [128, EXP], F32)
            fmk2 = sbF("fmk2", [128, EXP], F32)
            scores = sbF("scores", [128, 16, EXP], F32)
            sc_bf = sbF("sc_bf", [128, EXP], BF16)
            fcol = sbF("fcol", [128, 8], F32)
            srT = sbF("srT", [16, TOK], BF16)
            identF = sbF("identF", [128, 128], BF16)
            router_bf = sbF("router_bf", [128, DCH, EXP], BF16)
            downb_bf = sbF("downb_bf", [16, D], BF16)
            gb_sb = sbF("gb_sb", [128, EXP, 4], F32)
            ub_sb = sbF("ub_sb", [128, EXP, 4], F32)
            flw = sbF("flw", [128, D], F32)
            fmc = sbF("fmc", [128, 4], F32)
            fsq = sbF("fsq", [128, 512], F32)

            p_ftr = top.enter_context(nc.psum_tensor("p_ftr", [128, 1024], BF16))
            p_ftr2 = top.enter_context(nc.psum_tensor("p_ftr2", [128, 512], F32))
            p_fg = top.enter_context(nc.psum_tensor("p_fg", [128, 2, 512], F32))
            p_fu = top.enter_context(nc.psum_tensor("p_fu", [128, 2, 512], F32))
            p_fd = top.enter_context(nc.psum_tensor("p_fd", [128, 1024], F32))


            p.add("gpsimd", lambda e: e.memset(identF[:], 0.0))
            p.add("gpsimd", lambda e: e.affine_select(
                out=identF[:], in_=identF[:], compare_op=ALU.not_equal,
                fill=1.0, base=0, pattern=[[-1, 128]], channel_multiplier=1),
                mile="FidI")
            p.add("sync", lambda e: e.dma_start(
                out=flw[:, 0:DCH * EXP].rearrange("p (a b) -> p a b", a=DCH),
                in_=router_d.rearrange("(a p) b -> p a b", p=128)),
                mile="rLd", dma=True)
            p.add("scalar", lambda e: e.copy(
                out=router_bf[:],
                in_=flw[:, 0:DCH * EXP].rearrange("p (a b) -> p a b", a=DCH)),
                after=("rLd",), mile="rC")
            p.add("sync", lambda e: e.dma_start(
                out=flw[0:16, 0:D], in_=downb_d), after=("rC",),
                mile="dbLd", dma=True)
            p.add("scalar", lambda e: e.copy(
                out=downb_bf[:], in_=flw[0:16, 0:D]), after=("dbLd",),
                mile="dbC")
            p.add("sync", lambda e: e.dma_start(out=gb_sb[:], in_=gateb_d),
                  mile="gbL", dma=True)
            p.add("sync", lambda e: e.dma_start(out=ub_sb[:], in_=upb_d),
                  mile="ubL", dma=True)

            x1tok_free = ["dbC"]
            for tt in range(16):
                p.add("sync", lambda e, tt=tt: e.dma_start(
                    out=x1tok[:], in_=x1td[tt * 128:(tt + 1) * 128, :]),
                    after=tuple(x1tok_free), mile=f"fxL{tt}", dma=True)

                def ftr(e, tt=tt):
                    for j in range(DCH - 1):
                        e.transpose(out=p_ftr[:, j * 128:(j + 1) * 128],
                                    in_=x1tok[:, j * 128:(j + 1) * 128],
                                    identity=identF[:])
                    return e.transpose(
                        out=p_ftr[:, (DCH - 1) * 128:DCH * 128],
                        in_=x1tok[:, (DCH - 1) * 128:DCH * 128],
                        identity=identF[:])
                p.add("tensor", ftr,
                      after=(f"fxL{tt}", "FidI",
                             f"fxE{tt-1}" if tt else None),
                      mile=f"fxT{tt}")
                x1tok_free = [f"fxT{tt}"]
                p.add("vector", lambda e, tt=tt: e.tensor_copy(
                    out=x1f[:, :, tt * 128:(tt + 1) * 128],
                    in_=p_ftr[:].rearrange("p (a b) -> p a b", a=DCH)),
                    after=(f"fxT{tt}",), mile=f"fxE{tt}")

            for tt in range(16):
                def rmm(e, tt=tt):
                    for dc in range(DCH - 1):
                        e.matmul(out=p_ftr2[:, 0:EXP],
                                 lhsT=x1f[:, dc, tt * 128:(tt + 1) * 128],
                                 rhs=router_bf[:, dc, :],
                                 start=(dc == 0), stop=False)
                    return e.matmul(
                        out=p_ftr2[:, 0:EXP],
                        lhsT=x1f[:, DCH - 1, tt * 128:(tt + 1) * 128],
                        rhs=router_bf[:, DCH - 1, :], start=False, stop=True)
                p.add("tensor", rmm,
                      after=(f"fxE{tt}", "rC",
                             f"rcp{tt-1}" if tt else None),
                      mile=f"rmm{tt}")
                p.add("vector", lambda e: e.tensor_copy(
                    out=lgt[:], in_=p_ftr2[:, 0:EXP]),
                    after=(f"rmm{tt}",), mile=f"rcp{tt}")
                p.add("vector", lambda e: e.reduce_max(
                    out=fcol[:, 0:1], in_=lgt[:], axis=AX.X),
                    after=(f"rcp{tt}",), strict=True, mile=f"rm1{tt}")
                p.add("vector", lambda e: e.tensor_scalar(
                    out=fm1[:], in0=lgt[:], scalar1=fcol[:, 0:1],
                    scalar2=None, op0=ALU.is_equal),
                    after=(f"rm1{tt}",), strict=True, mile=f"fm1m{tt}")

                p.add("vector", lambda e: e.scalar_tensor_tensor(
                    out=fl2[:], in0=fm1[:], scalar=-1e30, in1=lgt[:],
                    op0=ALU.mult, op1=ALU.add),
                    after=(f"fm1m{tt}",), strict=True, mile=f"fl2m{tt}")
                p.add("vector", lambda e: e.reduce_max(
                    out=fcol[:, 1:2], in_=fl2[:], axis=AX.X),
                    after=(f"fl2m{tt}",), strict=True, mile=f"rm2{tt}")
                p.add("vector", lambda e: e.tensor_scalar(
                    out=fmk2[:], in0=fl2[:], scalar1=fcol[:, 1:2],
                    scalar2=None, op0=ALU.is_equal),
                    after=(f"rm2{tt}",), strict=True, mile=f"mk2{tt}")
                p.add("vector", lambda e: e.tensor_scalar(
                    out=fcol[:, 2:3], in0=fcol[:, 1:2],
                    scalar1=fcol[:, 0:1], scalar2=None, op0=ALU.subtract),
                    after=(f"rm2{tt}",), strict=True, mile=f"d21{tt}")
                p.add("scalar", lambda e: e.activation(
                    out=fcol[:, 3:4], in_=fcol[:, 2:3], func=AF.Sigmoid),
                    after=(f"d21{tt}",), mile=f"w2{tt}")
                p.add("vector", lambda e: e.tensor_scalar(
                    out=fcol[:, 4:5], in0=fcol[:, 3:4], scalar1=-1.0,
                    scalar2=1.0, op0=ALU.mult, op1=ALU.add),
                    after=(f"w2{tt}", f"mk2{tt}"), strict=True,
                    mile=f"w1{tt}")
                p.add("vector", lambda e, tt=tt: e.tensor_scalar(
                    out=scores[:, tt, :], in0=fm1[:],
                    scalar1=fcol[:, 4:5], scalar2=None, op0=ALU.mult),
                    after=(f"w1{tt}",), strict=True, mile=f"sa{tt}")
                p.add("vector", lambda e, tt=tt: e.scalar_tensor_tensor(
                    out=scores[:, tt, :], in0=fmk2[:],
                    scalar=fcol[:, 3:4], in1=scores[:, tt, :],
                    op0=ALU.mult, op1=ALU.add),
                    after=(f"sa{tt}", f"mk2{tt}"), strict=True,
                    mile=f"scs{tt}")
                p.add("vector", lambda e, tt=tt: e.tensor_copy(
                    out=sc_bf[:], in_=scores[:, tt, :]),
                    after=(f"scs{tt}", f"sTe{tt-1}" if tt else None),
                    strict=True, mile=f"scb{tt}")
                p.add("tensor", lambda e: e.transpose(
                    out=p_ftr[0:EXP, 0:128], in_=sc_bf[:], identity=identF[:]),
                    after=(f"scb{tt}", f"fxE{15}",
                           f"sTe{tt-1}" if tt else None),
                    mile=f"sT{tt}")
                p.add("scalar", lambda e, tt=tt: e.copy(
                    out=srT[:, tt * 128:(tt + 1) * 128],
                    in_=p_ftr[0:EXP, 0:128]),
                    after=(f"sT{tt}",), mile=f"sTe{tt}")
                if dbg and tt == 0:
                    p.add("sync", lambda e: e.dma_start(
                        out=dbg_t["dbg_scores"], in_=scores[:, 0, :]),
                        after=("scs0",), mile="DBGsc", dma=True)
                    p.add("sync", lambda e: e.dma_start(
                        out=dbg_t["dbg_rt"][:, 0:16], in_=lgt[:]),
                        after=("scs0",), mile="DBGr1", dma=True)
                    p.add("sync", lambda e: e.dma_start(
                        out=dbg_t["dbg_rt"][:, 16:32], in_=fmk2[:]),
                        after=("scs0",), mile="DBGr2", dma=True)
                    p.add("sync", lambda e: e.dma_start(
                        out=dbg_t["dbg_rt"][:, 32:48], in_=fm1[:]),
                        after=("scs0",), mile="DBGr3", dma=True)
                    p.add("sync", lambda e: e.dma_start(
                        out=dbg_t["dbg_rt"][:, 48:56], in_=fcol[:]),
                        after=("scs0",), mile="DBGr4", dma=True)
                    p.add("vector", lambda e: e.engine_nop(),
                          after=("DBGr1", "DBGr2", "DBGr3", "DBGr4"),
                          mile="DBGrwait")

            for tt in range(16):
                def bmm(e, tt=tt):
                    e.matmul(out=p_fd[:, 0:512],
                             lhsT=srT[:, tt * 128:(tt + 1) * 128],
                             rhs=downb_bf[:, 0:512], start=True, stop=True)
                    return e.matmul(out=p_fd[:, 512:1024],
                                    lhsT=srT[:, tt * 128:(tt + 1) * 128],
                                    rhs=downb_bf[:, 512:1024],
                                    start=True, stop=True)
                p.add("tensor", bmm,
                      after=(f"sTe{tt}", "dbC", f"oaI{tt-1}" if tt else None),
                      mile=f"bmm{tt}")
                p.add("sync", lambda e, tt=tt: e.dma_start(
                    out=x1tok[:], in_=x1td[tt * 128:(tt + 1) * 128, :]),
                    after=tuple([f"bmm{tt}"] + x1tok_free),
                    mile=f"fx2L{tt}", dma=True)
                p.add("vector", lambda e, tt=tt: e.tensor_add(
                    out=out_acc[:, tt, :],
                    in0=p_fd[:], in1=x1tok[:]),
                    after=(f"bmm{tt}", f"fx2L{tt}"), mile=f"oaI{tt}")
                if dbg and tt == 0:
                    p.add("sync", lambda e: e.dma_start(
                        out=dbg_t["dbg_oa0"], in_=out_acc[:, 0, :]),
                        after=("oaI0",), mile="DBGoa0", dma=True)
                x1tok_free = [f"oaI{tt}"]

            pfg_r = Ring(2)
            pfu_r = Ring(2)
            gcr = Ring(2)
            sgr = Ring(2)
            u1r = Ring(2)
            u2r = Ring(2)
            glr = Ring(2)
            actr = Ring(2)
            acts_slot = {}
            prevload = "ubL"
            for ex in range(EXP):
                es = ex % 2
                p.add("sync", lambda e, ex=ex: e.dma_start(
                    out=land[:].rearrange("p (a b) -> p a b", a=DCH),
                    in_=gatew_d[ex].rearrange("(a p) b -> p a b", p=128)),
                    after=(prevload, f"gmmE{ex-2}" if ex >= 2 else None),
                    mile=f"gL{ex}", dma=True)
                p.add("scalar", lambda e, es=es: e.copy(
                    out=gwb[:, es, :, :],
                    in_=land[:].rearrange("p (a b) -> p a b", a=DCH)),
                    after=(f"gL{ex}",), mile=f"gC{ex}")
                p.add("sync", lambda e, ex=ex: e.dma_start(
                    out=land[:].rearrange("p (a b) -> p a b", a=DCH),
                    in_=upw_d[ex].rearrange("(a p) b -> p a b", p=128)),
                    after=(f"gC{ex}", f"ummE{ex-2}" if ex >= 2 else None),
                    mile=f"uL{ex}", dma=True)
                p.add("scalar", lambda e, es=es: e.copy(
                    out=uwb[:, es, :, :],
                    in_=land[:].rearrange("p (a b) -> p a b", a=DCH)),
                    after=(f"uL{ex}",), mile=f"uC{ex}")
                p.add("sync", lambda e, ex=ex: e.dma_start(
                    out=land[:, 0:4 * D].rearrange("p (a b) -> p a b", a=4),
                    in_=downw_d[ex].rearrange("(a p) b -> p a b", p=128)),
                    after=(f"uC{ex}", f"dmmE{ex-2}" if ex >= 2 else None),
                    mile=f"dL{ex}", dma=True)
                p.add("scalar", lambda e, es=es: e.copy(
                    out=dwb[:, es, :, :],
                    in_=land[:, 0:4 * D].rearrange("p (a b) -> p a b", a=4)),
                    after=(f"dL{ex}",), mile=f"dC{ex}")
                prevload = f"dC{ex}"
                for tsb in range(4):
                    acs_, acw_ = actr.acquire()
                    acts_slot[(ex, tsb)] = acs_
                    for fc in range(4):
                        gk = f"{ex}_{tsb}_{fc}"
                        gslot_, gw_ = pfg_r.acquire()

                        def gmm(e, es=es, tsb=tsb, fc=fc, s=gslot_):
                            last = None
                            for dc in range(DCH):
                                last = e.matmul(
                                    out=p_fg[:, s, :],
                                    lhsT=gwb[:, es, dc, fc * 128:(fc + 1) * 128],
                                    rhs=x1f[:, dc, tsb * 512:(tsb + 1) * 512],
                                    start=(dc == 0), stop=(dc == DCH - 1))
                            return last
                        p.add("tensor", gmm,
                              after=tuple([f"gC{ex}", f"fxE{15}"] + gw_),
                              mile=f"gmm{gk}" + ("E" if tsb == 3 and fc == 3
                                                 else ""))
                        gmm_m = f"gmm{gk}" + ("E" if tsb == 3 and fc == 3 else "")
                        gcs_, gcw_ = gcr.acquire()
                        p.add("vector", lambda e, ex=ex, fc=fc, s=gslot_,
                              o=gcs_: e.tensor_scalar(
                                  out=gc_r[:, o, :], in0=p_fg[:, s, :],
                                  scalar1=gb_sb[:, ex, fc:fc + 1],
                                  scalar2=7.0, op0=ALU.add, op1=ALU.min),
                              after=tuple([gmm_m, "gbL"] + gcw_),
                              mile=f"gev{gk}")
                        pfg_r.consumed(f"gev{gk}")
                        sgs_, sgw_ = sgr.acquire()
                        p.add("scalar", lambda e, i=gcs_, o=sgs_: e.activation(
                            out=sg_r[:, o, :], in_=gc_r[:, i, :],
                            func=AF.Sigmoid, scale=1.702),
                            after=tuple([f"gev{gk}"] + sgw_),
                            mile=f"sig{gk}")
                        uslot_, uw_ = pfu_r.acquire()

                        def umm(e, es=es, tsb=tsb, fc=fc, s=uslot_):
                            last = None
                            for dc in range(DCH):
                                last = e.matmul(
                                    out=p_fu[:, s, :],
                                    lhsT=uwb[:, es, dc, fc * 128:(fc + 1) * 128],
                                    rhs=x1f[:, dc, tsb * 512:(tsb + 1) * 512],
                                    start=(dc == 0), stop=(dc == DCH - 1))
                            return last
                        p.add("tensor", umm,
                              after=tuple([f"uC{ex}"] + uw_),
                              mile=f"umm{gk}" + ("E" if tsb == 3 and fc == 3
                                                 else ""))
                        umm_m = f"umm{gk}" + ("E" if tsb == 3 and fc == 3 else "")
                        u1s_, u1w_ = u1r.acquire()
                        p.add("vector", lambda e, ex=ex, fc=fc, s=uslot_,
                              o=u1s_: e.tensor_scalar(
                                  out=up1_r[:, o, :], in0=p_fu[:, s, :],
                                  scalar1=ub_sb[:, ex, fc:fc + 1],
                                  scalar2=7.0, op0=ALU.add, op1=ALU.min),
                              after=tuple([umm_m, "ubL"] + u1w_),
                              mile=f"uev{gk}")
                        pfu_r.consumed(f"uev{gk}")
                        u2s_, u2w_ = u2r.acquire()
                        p.add("gpsimd", lambda e, i=u1s_, o=u2s_: e.tensor_scalar(
                            out=up2_r[:, o, :], in0=up1_r[:, i, :],
                            scalar1=-7.0, scalar2=1.0, op0=ALU.max,
                            op1=ALU.add),
                            after=tuple([f"uev{gk}"] + u2w_),
                            mile=f"up2{gk}")
                        u1r.consumed(f"up2{gk}")
                        gls_, glw_ = glr.acquire()
                        p.add("vector", lambda e, i1=gcs_, i2=sgs_, o=gls_:
                              e.tensor_tensor(
                                  out=glu_r[:, o, :], in0=gc_r[:, i1, :],
                                  in1=sg_r[:, i2, :], op=ALU.mult),
                              after=tuple([f"sig{gk}", f"gev{gk}"] + glw_),
                              mile=f"glu{gk}")
                        gcr.consumed(f"glu{gk}")
                        sgr.consumed(f"glu{gk}")
                        p.add("gpsimd", lambda e, i1=u2s_, i2=gls_, o=acs_,
                              fc=fc: e.tensor_tensor(
                                  out=acts[:, o, fc, :],
                                  in0=up2_r[:, i1, :],
                                  in1=glu_r[:, i2, :], op=ALU.mult),
                              after=tuple(
                                  [f"up2{gk}", f"glu{gk}",
                                   "DBGact" if dbg and ex == 0 and tsb == 1
                                   and fc == 0 else None]
                                  + (acw_ if fc == 0 else [])),
                              mile=f"act{gk}")
                        u2r.consumed(f"act{gk}")
                        glr.consumed(f"act{gk}")
                    if dbg and ex == 0 and tsb == 0:
                        p.add("sync", lambda e: e.dma_start(
                            out=dbg_t["dbg_act"],
                            in_=acts[:, acts_slot[(0, 0)], :, :]),
                            after=("act0_0_3",), mile="DBGact", dma=True)
                    for tt2 in range(4):
                        aslot2 = acts_slot[(ex, tsb)]

                        def dmm(e, es=es, tsb=tsb, tt2=tt2, s=aslot2):
                            last = None
                            for half in range(2):
                                for ic in range(4):
                                    last = e.matmul(
                                        out=p_fd[:, half * 512:(half + 1) * 512],
                                        lhsT=acts[:, s, ic,
                                                  tt2 * 128:(tt2 + 1) * 128],
                                        rhs=dwb[:, es, ic,
                                                half * 512:(half + 1) * 512],
                                        start=(ic == 0), stop=(ic == 3))
                            return last
                        dk = f"{ex}_{tsb}_{tt2}"
                        p.add("tensor", dmm,
                              after=(f"act{ex}_{tsb}_3", f"dC{ex}",
                                     f"oaI{15}",
                                     f"dev{ex}_{tsb}_{tt2-1}" if tt2 else
                                     (f"dev{ex}_{tsb-1}_3" if tsb else
                                      (f"dev{ex-1}_3_3" if ex else None))),
                              mile=f"dmm{dk}")
                        if tt2 == 3:
                            actr.consumed(f"dmm{dk}")
                        tg = tsb * 4 + tt2
                        p.add("vector", lambda e, ex=ex, tg=tg: (
                            e.scalar_tensor_tensor(
                                out=out_acc[:, tg, 0:512],
                                in0=p_fd[:, 0:512],
                                scalar=scores[:, tg, ex:ex + 1],
                                in1=out_acc[:, tg, 0:512],
                                op0=ALU.mult, op1=ALU.add),
                            e.scalar_tensor_tensor(
                                out=out_acc[:, tg, 512:1024],
                                in0=p_fd[:, 512:1024],
                                scalar=scores[:, tg, ex:ex + 1],
                                in1=out_acc[:, tg, 512:1024],
                                op0=ALU.mult, op1=ALU.add))[-1],
                            after=(f"dmm{dk}", f"oaI{tg}", f"scs{tg}"),
                            strict=True, mile=f"dev{ex}_{tsb}_{tt2}")
                p.ms[f"gmmE{ex}"] = p.ms[f"gmm{ex}_3_3E"]
                p.ms[f"ummE{ex}"] = p.ms[f"umm{ex}_3_3E"]
                p.ms[f"dmmE{ex}"] = p.ms[f"dmm{ex}_3_3"]

            if dbg:
                p.add("sync", lambda e: e.dma_start(
                    out=dbg_t["dbg_oaf"], in_=out_acc[:, 0, :]),
                    after=(f"dev{EXP-1}_0_0",), mile="DBGoaf", dma=True)
            outw = []
            for tt in range(16):
                p.add("vector", lambda e, tt=tt: e.reduce_sum(
                    out=fmc[:, 0:1], in_=out_acc[:, tt, :], axis=AX.X),
                    after=(f"dev{EXP-1}_{tt//4}_{tt%4}",), mile=f"Fms{tt}")
                p.add("vector", lambda e: e.tensor_scalar(
                    out=fmc[:, 1:2], in0=fmc[:, 0:1], scalar1=INV_D,
                    scalar2=None, op0=ALU.mult),
                    after=(f"Fms{tt}",), strict=True, mile=f"Fm2{tt}")
                p.add("vector", lambda e, tt=tt: e.tensor_scalar(
                    out=flw[:], in0=out_acc[:, tt, :], scalar1=fmc[:, 1:2],
                    scalar2=None, op0=ALU.subtract),
                    after=(f"Fm2{tt}", f"OUT{tt-1}" if tt else "dbC"),
                    strict=True, mile=f"Fctr{tt}")
                p.add("scalar", lambda e: e.activation(
                    out=fsq[:], in_=flw[:, 0:512], func=AF.Square,
                    accum_out=fmc[:, 2:3]),
                    after=(f"Fctr{tt}",))
                p.add("scalar", lambda e: e.activation(
                    out=fsq[:], in_=flw[:, 512:1024], func=AF.Square,
                    accum_out=fmc[:, 3:4]), mile=f"Fvq{tt}")
                p.add("vector", lambda e: e.tensor_add(
                    out=fmc[:, 2:3], in0=fmc[:, 2:3], in1=fmc[:, 3:4]),
                    after=(f"Fvq{tt}",), strict=True, mile=f"Fva{tt}")
                p.add("vector", lambda e: e.tensor_scalar(
                    out=fmc[:, 2:3], in0=fmc[:, 2:3], scalar1=INV_D,
                    scalar2=EPS, op0=ALU.mult, op1=ALU.add),
                    after=(f"Fva{tt}",), strict=True, mile=f"Fv{tt}")
                p.add("scalar", lambda e: e.activation(
                    out=fmc[:, 3:4], in_=fmc[:, 2:3], func=AF.Sqrt),
                    after=(f"Fv{tt}",), mile=f"Fsd{tt}")
                p.add("vector", lambda e: e.reciprocal(
                    out=fmc[:, 2:3], in_=fmc[:, 3:4]),
                    after=(f"Fsd{tt}",), mile=f"Frc{tt}")
                p.add("vector", lambda e: e.tensor_scalar(
                    out=flw[:], in0=flw[:], scalar1=fmc[:, 2:3],
                    scalar2=None, op0=ALU.mult),
                    after=(f"Frc{tt}",), strict=True, mile=f"Fsc{tt}")
                p.add("sync", lambda e, tt=tt: e.dma_start(
                    out=out_d[tt * 128:(tt + 1) * 128, :], in_=flw[:]),
                    after=(f"Fsc{tt}",), mile=f"OUT{tt}", dma=True)
                outw.append(f"OUT{tt}")

        if stages < 4:
            attn.close()

        qs_total = p.cnt["qS"]
        qg_total = p.cnt["qG"]
        p.ops["sync"].append(([("qS", qs_total)], lambda e: e.nop(), None))
        if qg_total:
            p.ops["gpsimd"].append(
                ([("qG", qg_total)], lambda e: e.nop(), None))

        with nc.Block() as block:
            p.emit_all(block, semh)

    return nc


_NC_CACHE = {}


def _get_nc(stages=5, dbg=False):
    key = (stages, dbg)
    if key not in _NC_CACHE:
        _NC_CACHE[key] = build(stages, dbg)
    return _NC_CACHE[key]


def make_in_maps(inp):
    f32 = np.float32
    resid = np.asarray(inp["L0_attn_resid"], f32)
    in_wT = np.ascontiguousarray(np.asarray(inp["L0_attn_in_w"], f32).T)
    skip_wT = np.ascontiguousarray(np.asarray(inp["L0_attn_skip_w"], f32).T)
    normw8 = np.ascontiguousarray(
        np.asarray(inp["L0_attn_norm_w"], f32).reshape(DCH, 128).T)
    wd1T = np.asarray(inp["L0_attn_wd1"], f32).T
    wBT = np.asarray(inp["L0_attn_wB"], f32).T
    wCT = np.asarray(inp["L0_attn_wC"], f32).T
    wcat = np.ascontiguousarray(np.concatenate([wd1T, wBT, wCT], axis=1))
    wcatp0 = wcat.copy()
    wcatp0[:, 64:80] = 0.0
    wd2T = np.ascontiguousarray(np.asarray(inp["L0_attn_wd2"], f32).T)
    wd2b = np.ascontiguousarray(
        np.asarray(inp["L0_attn_wd2_b"], f32).reshape(EJ, 128).T)
    convw = np.ascontiguousarray(
        np.asarray(inp["L0_attn_conv_w"], f32).reshape(EJ, 128, DCV)
        .transpose(1, 0, 2))
    convb = np.ascontiguousarray(
        np.asarray(inp["L0_attn_conv_b"], f32).reshape(EJ, 128).T)
    alog = np.ascontiguousarray(
        np.asarray(inp["L0_attn_A_log"], f32).reshape(EJ, 128, NS)
        .transpose(1, 0, 2))
    wdv = np.ascontiguousarray(
        np.asarray(inp["L0_attn_W_D"], f32).reshape(EJ, 128).T)
    out_wT = np.ascontiguousarray(np.asarray(inp["L0_attn_out_w"], f32).T)
    router = np.ascontiguousarray(np.asarray(inp["L0_ffn_router_weight"], f32))
    gup = np.asarray(inp["L0_ffn_gate_up_proj"], f32)
    gatew = np.ascontiguousarray(gup[:, :, 0::2])
    upw = np.ascontiguousarray(gup[:, :, 1::2])
    gupb = np.asarray(inp["L0_ffn_gate_up_proj_bias"], f32)
    gateb = np.ascontiguousarray(
        gupb[:, 0::2].reshape(EXP, 4, 128).transpose(2, 0, 1))
    upb = np.ascontiguousarray(
        gupb[:, 1::2].reshape(EXP, 4, 128).transpose(2, 0, 1))
    downw = np.ascontiguousarray(np.asarray(inp["L0_ffn_down_proj"], f32))
    downb = np.ascontiguousarray(np.asarray(inp["L0_ffn_down_proj_bias"], f32))
    zeros = np.zeros((TOK, D), f32)
    fconsts = np.tile(np.array(CONST_VALS + (0.0,) * (8 - len(CONST_VALS)),
                               f32), (128, 1))

    shared = dict(in_wT=in_wT, skip_wT=skip_wT, normw8=normw8, wcatT=wcat,
                  wd2T=wd2T, wd2b=wd2b, convw=convw, convb=convb, alog=alog,
                  wdvec=wdv, out_wT=out_wT, router=router, gatew=gatew,
                  fconsts=fconsts,
                  upw=upw, gateb=gateb, upb=upb, downw=downw, downb=downb)
    in_maps = []
    for c in range(8):
        b, r = c // 2, c % 2
        m = dict(shared)
        m["resid_own"] = np.ascontiguousarray(resid[b, r * TOK:(r + 1) * TOK])
        m["resid_pfx"] = zeros if r == 0 else np.ascontiguousarray(
            resid[b, 0:TOK])
        m["wcatTp"] = wcatp0 if r == 0 else wcat
        in_maps.append(m)
    return in_maps


def kernel(**inp):
    nc = _get_nc(5, False)
    in_maps = make_in_maps(inp)
    res = run_bass_kernel_spmd(nc, in_maps, core_ids=list(range(8)))
    out = np.empty((B, L, D), np.float32)
    for c in range(8):
        b, r = c // 2, c % 2
        out[b, r * TOK:(r + 1) * TOK] = res.results[c]["out"]
    return out


# version 06
# speedup vs baseline: 1.4199x; 1.0006x over previous
import numpy as np
from contextlib import ExitStack

import concourse.bass as bass
import concourse.mybir as mybir
from concourse.bass_utils import run_bass_kernel_spmd

F32 = mybir.dt.float32
BF16 = mybir.dt.bfloat16
AF = mybir.ActivationFunctionType
ALU = mybir.AluOpType
AX = mybir.AxisListType

B, L, D, E, NS, DR, DCV = 4, 4096, 1024, 2048, 16, 64, 4
NV = 6
EXP, FI = 16, 512
TOK = 2048
C = 512
NCH = TOK // C
EJ = E // 128
DCH = D // 128
EPS = 1e-5
INV_D = 1.0 / 1024.0
CONST_VALS = (EPS, INV_D, -1.0, 1.0, 1.702)

ENGINES = ("sync", "scalar", "vector", "tensor", "gpsimd")


class Ring:

    def __init__(self, depth):
        self.depth = depth
        self.hist = []

    def acquire(self):
        i = len(self.hist)
        slot = i % self.depth
        waits = list(self.hist[i - self.depth]) if i >= self.depth else []
        self.hist.append([])
        return slot, waits

    def consumed(self, *miles):
        self.hist[-1].extend(m for m in miles if m)


class Prog:
    def __init__(self):
        self.ops = {e: [] for e in ENGINES}
        self.cnt = {s: 0 for s in ("qS", "qG", "sS", "sV", "sP", "sG")}
        self.ms = {}
        self.wm = {}
        self.esem = {"sync": "qS", "scalar": "sS", "vector": "sV",
                     "tensor": "sP", "gpsimd": "sG"}
        self.pending = {e: [] for e in ENGINES}
        self.pending_counts = {}

    def barrier(self, names):
        for e in ENGINES:
            self.pending[e].extend(names)

    def barrier_counts(self):
        snap = [(s, v) for s, v in self.cnt.items() if v > 0]
        for e in ENGINES:
            self.pending_counts.setdefault(e, []).extend(snap)

    def add(self, eng, fn, after=(), mile=None, dma=False, strict=False):
        names = [m for m in after if m] + self.pending[eng]
        self.pending[eng] = []
        waits = []
        for s, v in self.pending_counts.pop(eng, []):
            own = (s == self.esem[eng]) and s not in ("qS", "qG")
            if own:
                continue
            key = (eng, s)
            if self.wm.get(key, -1) < v:
                self.wm[key] = v
                waits.append((s, v))
        my_sem = "qG" if (eng == "gpsimd" and dma) else self.esem[eng]
        for m in names:
            assert m in self.ms, f"unknown milestone {m}"
            s, v = self.ms[m]
            own = (s == self.esem[eng]) and s not in ("qS", "qG")
            if own and not strict:
                continue
            key = (eng, s)
            if self.wm.get(key, -1) >= v:
                continue
            self.wm[key] = v
            waits.append((s, v))
        inc = None
        if mile is not None:
            amt = 16 if dma else 1
            self.cnt[my_sem] += amt
            self.ms[mile] = (my_sem, self.cnt[my_sem])
            inc = (my_sem, amt)
        self.ops[eng].append((waits, fn, inc))
        return mile

    def emit_all(self, block, semh):
        for name in ENGINES:
            def section(eng_obj, name=name):
                for waits, fn, inc in self.ops[name]:
                    for (s, v) in waits:
                        eng_obj.wait_ge(semh[s], v)
                    inst = fn(eng_obj)
                    if inc is not None:
                        inst.then_inc(semh[inc[0]], inc[1])
            getattr(block, name)(section)


def build(stages=5, dbg=False):
    nc = bass.Bass("TRN2", target_bir_lowering=False)

    def din(name, shape, dtype=F32):
        return nc.dram_tensor(name, shape, dtype, kind="ExternalInput").ap()

    resid_own = din("resid_own", [TOK, D])
    resid_pfx = din("resid_pfx", [TOK, D])
    in_wT = din("in_wT", [D, E])
    skip_wT = din("skip_wT", [D, E])
    normw8 = din("normw8", [128, DCH])
    wcatT = din("wcatT", [E, 96])
    wcatTp = din("wcatTp", [E, 96])
    wd2T_d = din("wd2T", [DR, E])
    wd2b_d = din("wd2b", [128, EJ])
    convw_d = din("convw", [128, EJ, DCV])
    convb_d = din("convb", [128, EJ])
    alog_d = din("alog", [128, EJ, NS])
    wd_d = din("wdvec", [128, EJ])
    out_wT = din("out_wT", [E, D])
    router_d = din("router", [D, EXP])
    gatew_d = din("gatew", [EXP, D, FI])
    upw_d = din("upw", [EXP, D, FI])
    gateb_d = din("gateb", [128, EXP, FI // 128])
    upb_d = din("upb", [128, EXP, FI // 128])
    downw_d = din("downw", [EXP, FI, D])
    downb_d = din("downb", [EXP, D])
    fconst_d = din("fconsts", [128, 8])

    in_bfd = nc.dram_tensor("in_bfd", [DCH, 128, E], BF16, kind="Internal").ap()
    skip_bfd = nc.dram_tensor("skip_bfd", [DCH, 128, E], BF16, kind="Internal").ap()
    out_bfd = nc.dram_tensor("out_bfd", [EJ, 128, D], BF16, kind="Internal").ap()
    x1td = nc.dram_tensor("x1td", [TOK, D], BF16, kind="Internal").ap()
    out_d = nc.dram_tensor("out", [TOK, D], F32, kind="ExternalOutput").ap()
    dbg_t = {}
    if dbg:
        for nm, shape, dty in (
                ("dbg_xbf", [128, EJ, C], BF16), ("dbg_dlr", [96, C], BF16),
                ("dbg_delta", [128, 2, C], F32), ("dbg_y", [128, EJ, C], BF16),
                ("dbg_hcar", [128, 256], F32), ("dbg_x1", [TOK, D], BF16),
                ("dbg_x1p", [128, D], F32), ("dbg_st", [128, 4], F32),
                ("dbg_rt", [128, 64], F32), ("dbg_oa0", [128, D], F32),
                ("dbg_oaf", [128, D], F32), ("dbg_act", [128, 4, 512], BF16),
                ("dbg_scores", [128, EXP], F32)):
            dbg_t[nm] = nc.dram_tensor(nm, shape, dty, kind="ExternalOutput").ap()

    p = Prog()

    with ExitStack() as top:
        semh = {s: top.enter_context(nc.semaphore(s))
                for s in ("qS", "qG", "sS", "sV", "sP", "sG")}
        fconst = top.enter_context(nc.sbuf_tensor("fconst", [128, 8], F32))
        for _i, _v in enumerate(CONST_VALS):
            nc.const_aps.register(F32, _v, fconst[:, _i:_i + 1])
        attn = ExitStack()

        def sbA(name, shape, dtype):
            return attn.enter_context(nc.sbuf_tensor(name, shape, dtype))

        wbuf = sbA("wbuf", [128, DCH, E], BF16)
        wbuf_o = wbuf.ap().rearrange("p a b -> p (a b)").rearrange(
            "p (a b) -> p a b", a=EJ)
        wcat_sb = sbA("wcat_sb", [128, EJ, 96], BF16)
        wcatp_sb = wcat_sb
        wd2_sb = sbA("wd2_sb", [DR, E], BF16)
        ident = sbA("ident", [128, 128], BF16)
        cdiag = sbA("cdiag", [128, EJ * DCV, 128], BF16)
        wddiag = sbA("wddiag", [128, EJ, 128], BF16)
        A_sb = sbA("A_sb", [128, EJ, NS], F32)
        negA_sb = sbA("negA_sb", [128, EJ, NS], F32)
        normw_sb = sbA("normw_sb", [128, DCH], F32)
        wd2b_sb = sbA("wd2b_sb", [128, EJ], F32)
        convb_sb = sbA("convb_sb", [128, EJ], F32)
        convw_sb = sbA("convw_sb", [128, EJ, DCV], F32)
        wd_sb = sbA("wd_sb", [128, EJ], F32)
        selB = sbA("selB", [96, NS, 128], BF16)
        selC = sbA("selC", [96, NS, 128], BF16)
        rs_tile = sbA("rs_tile", [128, D], F32)
        sp2 = rs_tile.ap()[:, 0:C]
        sp1 = rs_tile.ap()[:, C:2 * C]
        sq_scr = sbA("sq_scr", [128, D], F32)
        ssq = sbA("ssq", [128, 4], F32)
        mcol = sbA("mcol", [128, 8], F32)
        xn_feat = sbA("xn_feat", [128, DCH, C], BF16)
        xpre_r = sbA("xpre_r", [128, 2, C + 3], BF16)
        halo = sbA("halo", [128, EJ, 3], BF16)
        x_bf = sbA("x_bf", [128, EJ, C], BF16)
        skip_s = sbA("skip_s", [128, EJ, C], BF16)
        dlrBC_bf = sbA("dlrBC_bf", [96, C], BF16)

        delta_r = sbA("delta_r", [128, 2, C], F32)
        dx_r = sbA("dx_r", [128, 2, C], BF16)
        ones_sb = sbA("ones_sb", [128, C], BF16)
        a_r = sbA("a_r", [128, 3, C], F32)
        u_r = sbA("u_r", [128, 3, C], BF16)
        h_ring = sbA("h_ring", [128, 8, C], BF16)
        g_r = sbA("g_r", [128, 3, C], BF16)

        x1w_full = None
        Gcols = sbA("Gcols", [128, NS], F32)
        Pfac = sbA("Pfac", [128, NS], F32)
        hcar = sbA("hcar", [128, 256], F32)
        Bmb_all = sbA("Bmb_all", [128, NS, C], BF16)
        Cmb_all = sbA("Cmb_all", [128, NS, C], BF16)
        y_bf = sbA("y_bf", [128, EJ, C], BF16)
        wcast_sb = y_bf.ap().rearrange("p a b -> p (a b)")[:, 0:E]
        x1w = sbA("x1w", [128, E], F32)
        Dm_r = x1w.ap()[:, D:E].rearrange("p (a b) -> p a b", a=2)

        x1b_t = sbA("x1b_t", [128, D], BF16)
        gdead = x1b_t.ap()[:, 0:C]
        fmca = sbA("fmca", [128, 4], F32)

        p_tr = attn.enter_context(nc.psum_tensor("p_tr", [128, 1024], BF16))
        p_x = attn.enter_context(nc.psum_tensor("p_x", [128, 2, 512], F32))
        p_c = attn.enter_context(nc.psum_tensor("p_c", [128, 512], F32))
        p_bc = attn.enter_context(nc.psum_tensor("p_bc", [128, 512], F32))
        p_sel = attn.enter_context(nc.psum_tensor("p_sel", [128, 2, 512], F32))
        p_y = attn.enter_context(nc.psum_tensor("p_y", [128, 512], F32))

        p.add("sync", lambda e: e.dma_start(out=fconst[:], in_=fconst_d),
              mile="fcst", dma=True)
        csts = []
        alog_sb = x1w.ap()[:, 0:EJ * NS].rearrange("p (a b) -> p a b", a=EJ)
        for i, (sb, dr) in enumerate((
                (normw_sb, normw8), (wd2b_sb, wd2b_d), (convb_sb, convb_d),
                (convw_sb, convw_d), (wd_sb, wd_d), (alog_sb, alog_d))):
            p.add("sync", lambda e, sb=sb, dr=dr: e.dma_start(
                out=sb if isinstance(sb, bass.AP) else sb[:], in_=dr),
                  mile=f"cst{i}", dma=True)
            csts.append(f"cst{i}")
        p.add("scalar", lambda e: e.activation(
            out=negA_sb[:], in_=alog_sb, func=AF.Exp), after=("cst5",),
            mile="negA")
        p.add("vector", lambda e: e.tensor_scalar(
            out=A_sb[:], in0=negA_sb[:], scalar1=-1.0, scalar2=None,
            op0=ALU.mult), after=("negA",), mile="A_")
        p.add("sync", lambda e: e.dma_start(
            out=x1w[:, 0:EJ * 96].rearrange("p (j k) -> p j k", j=EJ),
            in_=wcatTp.rearrange("(j p) k -> p j k", p=128)),
            after=("negA",), mile="wcpL", dma=True)
        p.add("scalar", lambda e: e.copy(
            out=wcat_sb[:], in_=x1w[:, 0:EJ * 96].rearrange(
                "p (j k) -> p j k", j=EJ)),
            after=("wcpL",), mile="wcpC")
        prev = "wcpC"
        p.add("sync", lambda e: e.dma_start(out=x1w[0:DR, 0:E], in_=wd2T_d),
              after=("wcpC",), mile="wd2L", dma=True)
        p.add("scalar", lambda e: e.copy(out=wd2_sb[:], in_=x1w[0:DR, 0:E]),
              after=("wd2L",), mile="wd2C")
        p.add("gpsimd", lambda e: e.memset(ident[:], 0.0))
        p.add("gpsimd", lambda e: e.affine_select(
            out=ident[:], in_=ident[:], compare_op=ALU.not_equal, fill=1.0,
            base=0, pattern=[[-1, 128]], channel_multiplier=1), mile="idI")
        p.add("gpsimd", lambda e: e.memset(selB[:], 0.0))
        p.add("gpsimd", lambda e: e.memset(selC[:], 0.0))
        for n in range(NS):
            p.add("gpsimd", lambda e, n=n: e.affine_select(
                out=selB[:, n, :], in_=selB[:, n, :],
                compare_op=ALU.not_equal, fill=1.0, base=-(64 + n),
                pattern=[[0, 128]], channel_multiplier=1),
                mile=f"selB{n}" if n == NS - 1 else None)
            p.add("gpsimd", lambda e, n=n: e.affine_select(
                out=selC[:, n, :], in_=selC[:, n, :],
                compare_op=ALU.not_equal, fill=1.0, base=-(80 + n),
                pattern=[[0, 128]], channel_multiplier=1),
                mile="selCZ" if n == NS - 1 else None)
        p.add("gpsimd", lambda e: e.memset(ones_sb[:], 1.0), mile="ones")
        p.add("gpsimd", lambda e: e.memset(hcar[:], 0.0), mile="hcar0")
        p.add("gpsimd", lambda e: e.memset(halo[:], 0.0), mile="halo0")
        for ej in range(EJ):
            for k in range(DCV):
                p.add("vector", lambda e, ej=ej, k=k: e.tensor_scalar(
                    out=cdiag[:, ej * DCV + k, :], in0=ident[:],
                    scalar1=convw_sb[:, ej, k:k + 1], scalar2=None,
                    op0=ALU.mult), after=("idI", "cst3"),
                    mile=f"cd{ej}" if k == DCV - 1 else None)
            p.add("vector", lambda e, ej=ej: e.tensor_scalar(
                out=wddiag[:, ej, :], in0=ident[:],
                scalar1=wd_sb[:, ej:ej + 1], scalar2=None, op0=ALU.mult),
                after=("idI", "cst4"), mile=f"wdd{ej}")
        prev = "wd2C"
        for nm, src, dst, fold in (("in", in_wT, in_bfd, True),
                                   ("sk", skip_wT, skip_bfd, True)):
            for dc in range(DCH):
                p.add("sync", lambda e, src=src, dc=dc: e.dma_start(
                    out=x1w[:], in_=src[dc * 128:(dc + 1) * 128, :]),
                    after=(prev,), mile=f"{nm}L{dc}", dma=True)
                p.add("vector", lambda e, dc=dc: e.tensor_scalar(
                    out=wcast_sb, in0=x1w[:],
                    scalar1=normw_sb[:, dc:dc + 1], scalar2=None,
                    op0=ALU.mult), after=(f"{nm}L{dc}", "cst0"),
                    mile=f"{nm}C{dc}")
                p.add("sync", lambda e, dst=dst, dc=dc: e.dma_start(
                    out=dst[dc], in_=wcast_sb),
                    after=(f"{nm}C{dc}",), mile=f"{nm}S{dc}", dma=True)
                prev = f"{nm}C{dc}"
        for ec in range(EJ):
            p.add("sync", lambda e, ec=ec: e.dma_start(
                out=x1w[:, 0:D], in_=out_wT[ec * 128:(ec + 1) * 128, :]),
                after=(prev,), mile=f"oL{ec}", dma=True)
            p.add("scalar", lambda e: e.copy(
                out=wcast_sb[:, 0:D], in_=x1w[:, 0:D]),
                after=(f"oL{ec}", f"oS{ec-1}" if ec else None),
                mile=f"oC{ec}")
            p.add("sync", lambda e, ec=ec: e.dma_start(
                out=out_bfd[ec], in_=wcast_sb[:, 0:D]),
                after=(f"oC{ec}",), mile=f"oS{ec}", dma=True)
            prev = f"oC{ec}"

        rs_free = [f"oS{EJ-1}"]
        x1b_free = ["wd2C"]
        xnfeat_free = [None]
        wbuf_free = [f"skS{DCH-1}"]
        xpre = Ring(2)
        pxr = Ring(2)
        pc_r = Ring(1)
        pbc_r = Ring(1)
        psel_r = Ring(2)
        py_r = Ring(1)
        ptr_r = Ring(1)
        dlt = Ring(2)
        dxr = Ring(2)
        ar = Ring(3)
        ur = Ring(3)
        hr = Ring(8)
        gr = Ring(3)
        bmb = Ring(4)
        cmb = Ring(4)
        dmr = Ring(2)
        sp1r = Ring(1)
        sp2r = Ring(1)
        xbf_free = {ej: ["A_"] for ej in range(EJ)}
        skip_free = {ej: [None] for ej in range(EJ)}
        ybf_free = {ej: [None] for ej in range(EJ)}
        dlr_free = [None]
        halo_v = {ej: "halo0" for ej in range(EJ)}
        gcols_free = ["hcar0"]
        hcar_mile = {ej: None for ej in range(EJ)}

        for ch in range(8):
            pfx = ch < 4
            t0 = (ch % 4) * C
            rsrc = resid_pfx if pfx else resid_own

            for tt in range(4):
                p.add("sync", lambda e, tt=tt, rsrc=rsrc, t0=t0: e.dma_start(
                    out=rs_tile[:], in_=rsrc[t0 + tt * 128:t0 + (tt + 1) * 128, :]),
                    after=tuple(rs_free) + ((f"dlt{ch-1}_{EJ-1}",)
                                            if tt == 0 and ch > 0 else ()),
                    mile=f"rs{ch}_{tt}", dma=True)
                p.add("scalar", lambda e, tt=tt: e.activation(
                    out=sq_scr[:], in_=rs_tile[:], func=AF.Square,
                    accum_out=ssq[:, tt:tt + 1]),
                    after=(f"rs{ch}_{tt}",), mile=f"sq{ch}_{tt}")
                p.add("scalar", lambda e, tt=tt: e.activation(
                    out=mcol[:, tt:tt + 1], in_=ssq[:, tt:tt + 1],
                    func=AF.Sqrt, scale=INV_D, bias=EPS),
                    after=(f"sq{ch}_{tt}", "fcst",
                           f"xnc{ch-1}_{tt}" if ch else None),
                    strict=True, mile=f"ms{ch}_{tt}")
                p.add("vector", lambda e, tt=tt: e.reciprocal(
                    out=mcol[:, 4 + tt:5 + tt], in_=mcol[:, tt:tt + 1]),
                    after=(f"ms{ch}_{tt}",), mile=f"rst{ch}_{tt}")
                p.add("scalar", lambda e, tt=tt: e.activation(
                    out=x1b_t[:], in_=rs_tile[:], func=AF.Copy,
                    scale=mcol[:, 4 + tt:5 + tt]),
                    after=tuple([f"rst{ch}_{tt}"] + x1b_free),
                    mile=f"xnc{ch}_{tt}")
                rs_free = [f"xnc{ch}_{tt}"]
                slot, w = ptr_r.acquire()

                def trf(e, tt=tt):
                    for j in range(DCH - 1):
                        e.transpose(out=p_tr[:, j * 128:(j + 1) * 128],
                                    in_=x1b_t[:, j * 128:(j + 1) * 128],
                                    identity=ident[:])
                    return e.transpose(out=p_tr[:, (DCH - 1) * 128:DCH * 128],
                                       in_=x1b_t[:, (DCH - 1) * 128:DCH * 128],
                                       identity=ident[:])
                p.add("tensor", trf,
                      after=tuple([f"xnc{ch}_{tt}", "idI"] + w),
                      mile=f"tr{ch}_{tt}")
                x1b_free = [f"tr{ch}_{tt}"]
                p.add("vector", lambda e, tt=tt: e.tensor_copy(
                    out=xn_feat[:, :, tt * 128:(tt + 1) * 128],
                    in_=p_tr[:].rearrange("p (a b) -> p a b", a=DCH)),
                    after=tuple([f"tr{ch}_{tt}"] + xnfeat_free),
                    mile=f"xnf{ch}_{tt}")
                ptr_r.consumed(f"xnf{ch}_{tt}")

            p.add("sync", lambda e: e.dma_start(
                out=wbuf[:], in_=in_bfd.rearrange("a p b -> p a b")),
                after=tuple(wbuf_free + [f"inS{DCH-1}"]),
                mile=f"inL{ch}", dma=True)
            for ej in range(EJ):
                slot, w = pxr.acquire()

                def inmm(e, ej=ej, slot=slot):
                    for dc in range(DCH - 1):
                        e.matmul(out=p_x[:, slot, :],
                                 lhsT=wbuf[:, dc, ej * 128:(ej + 1) * 128],
                                 rhs=xn_feat[:, dc, :],
                                 start=(dc == 0), stop=False)
                    return e.matmul(out=p_x[:, slot, :],
                                    lhsT=wbuf[:, DCH - 1, ej * 128:(ej + 1) * 128],
                                    rhs=xn_feat[:, DCH - 1, :],
                                    start=False, stop=True)
                p.add("tensor", inmm,
                      after=tuple([f"xnf{ch}_3", f"inL{ch}"] + w),
                      mile=f"Bx{ch}_{ej}")
                xslot, xw = xpre.acquire()
                p.add("scalar", lambda e, slot=slot, xslot=xslot: e.copy(
                    out=xpre_r[:, xslot, 3:3 + C], in_=p_x[:, slot, :]),
                    after=tuple([f"Bx{ch}_{ej}"] + xw), mile=f"evX{ch}_{ej}")
                pxr.consumed(f"evX{ch}_{ej}")
                p.add("vector", lambda e, ej=ej, xslot=xslot: e.tensor_copy(
                    out=xpre_r[:, xslot, 0:3], in_=halo[:, ej, :]),
                    after=(f"evX{ch}_{ej}", halo_v[ej]), mile=f"hin{ch}_{ej}")
                p.add("vector", lambda e, ej=ej, xslot=xslot: e.tensor_copy(
                    out=halo[:, ej, :], in_=xpre_r[:, xslot, C:C + 3]),
                    after=(f"evX{ch}_{ej}",), mile=f"hout{ch}_{ej}")
                halo_v[ej] = f"hout{ch}_{ej}"
                _, cw = pc_r.acquire()

                def convmm(e, ej=ej, xslot=xslot):
                    for k in range(DCV - 1):
                        e.matmul(out=p_c[:], lhsT=cdiag[:, ej * DCV + k, :],
                                 rhs=xpre_r[:, xslot, k:k + C],
                                 start=(k == 0), stop=False)
                    return e.matmul(out=p_c[:],
                                    lhsT=cdiag[:, ej * DCV + DCV - 1, :],
                                    rhs=xpre_r[:, xslot, DCV - 1:DCV - 1 + C],
                                    start=False, stop=True)
                p.add("tensor", convmm,
                      after=tuple([f"evX{ch}_{ej}", f"hin{ch}_{ej}",
                                   f"cd{ej}"] + cw),
                      mile=f"Bc{ch}_{ej}")
                xpre.consumed(f"Bc{ch}_{ej}", f"hout{ch}_{ej}")
                p.add("scalar", lambda e, ej=ej: e.activation(
                    out=x_bf[:, ej, :], in_=p_c[:], func=AF.Silu,
                    bias=convb_sb[:, ej:ej + 1]),
                    after=tuple([f"Bc{ch}_{ej}", "cst2"] + xbf_free[ej]),
                    mile=f"evC{ch}_{ej}")
                pc_r.consumed(f"evC{ch}_{ej}")
                xbf_free[ej] = []

            if not pfx:
                p.add("sync", lambda e: e.dma_start(
                    out=wbuf[:], in_=skip_bfd.rearrange("a p b -> p a b")),
                    after=(f"Bx{ch}_{EJ-1}", f"skS{DCH-1}"),
                    mile=f"skL{ch}", dma=True)
                for ej in range(EJ):
                    slot, w = pxr.acquire()

                    def skmm(e, ej=ej, slot=slot):
                        for dc in range(DCH - 1):
                            e.matmul(out=p_x[:, slot, :],
                                     lhsT=wbuf[:, dc, ej * 128:(ej + 1) * 128],
                                     rhs=xn_feat[:, dc, :],
                                     start=(dc == 0), stop=False)
                        return e.matmul(
                            out=p_x[:, slot, :],
                            lhsT=wbuf[:, DCH - 1, ej * 128:(ej + 1) * 128],
                            rhs=xn_feat[:, DCH - 1, :], start=False, stop=True)
                    p.add("tensor", skmm,
                          after=tuple([f"skL{ch}"] + w), mile=f"Bs{ch}_{ej}")
                    p.add("scalar", lambda e, ej=ej, slot=slot: e.activation(
                        out=skip_s[:, ej, :], in_=p_x[:, slot, :],
                        func=AF.Silu),
                        after=tuple([f"Bs{ch}_{ej}"] + skip_free[ej]),
                        mile=f"evS{ch}_{ej}")
                    pxr.consumed(f"evS{ch}_{ej}")
                    skip_free[ej] = []
                wbuf_free = [f"Bs{ch}_{EJ-1}"]
                xnfeat_free = [f"Bs{ch}_{EJ-1}"]
            else:
                wbuf_free = [f"Bx{ch}_{EJ-1}"]
                xnfeat_free = [f"Bx{ch}_{EJ-1}"]

            if ch == 4:
                p.add("sync", lambda e: e.dma_start(
                    out=x1w[:, 0:EJ * 96].rearrange("p (j k) -> p j k", j=EJ),
                    in_=wcatT.rearrange("(j p) k -> p j k", p=128)),
                    after=(f"Cmm{3}", f"dlt{3}_{EJ-1}"), mile="wcL", dma=True)
                p.add("scalar", lambda e: e.copy(
                    out=wcat_sb[:], in_=x1w[:, 0:EJ * 96].rearrange(
                        "p (j k) -> p j k", j=EJ)),
                    after=("wcL", f"Cmm{3}"), mile="wcC")
            wsb = wcat_sb
            _, bw = pbc_r.acquire()

            def wcmm(e, wsb=wsb):
                for kj in range(EJ - 1):
                    e.matmul(out=p_bc[0:96, :], lhsT=wsb[:, kj, 0:96],
                             rhs=x_bf[:, kj, :], start=(kj == 0), stop=False)
                return e.matmul(out=p_bc[0:96, :], lhsT=wsb[:, EJ - 1, 0:96],
                                rhs=x_bf[:, EJ - 1, :], start=False, stop=True)
            p.add("tensor", wcmm,
                  after=tuple([f"evC{ch}_{ej}" for ej in range(EJ)]
                              + ["wcC" if not pfx else "wcpC"] + bw),
                  mile=f"Cmm{ch}")
            p.add("scalar", lambda e: e.copy(
                out=dlrBC_bf[:], in_=p_bc[0:96, :]),
                after=tuple([f"Cmm{ch}"] + dlr_free), mile=f"dlrC{ch}")
            pbc_r.consumed(f"dlrC{ch}")
            dlr_free = []
            if dbg and ch == 4:
                p.add("sync", lambda e: e.dma_start(
                    out=dbg_t["dbg_dlr"], in_=dlrBC_bf[:]),
                    after=(f"dlrC{ch}",), mile="DBGdlr", dma=True)
                p.add("sync", lambda e: e.dma_start(
                    out=dbg_t["dbg_xbf"], in_=x_bf[:]),
                    after=tuple(f"evC{ch}_{ej}" for ej in range(EJ)),
                    mile="DBGxbf", dma=True)

            if ch == 0:
                bc_last = []
            elif ch <= 4:
                bc_last = [f"t{ch-1}_{EJ-1}_{NS-1}", f"t{ch-1}_{EJ-1}_{4}"]
            else:
                bc_last = [f"u{ch-1}_{EJ-1}_{NS-1}", f"u{ch-1}_{EJ-1}_{4}",
                           f"g{ch-1}_{EJ-1}_{NS-1}", f"g{ch-1}_{EJ-1}_{4}"]
            if stages >= 2:
                for n in range(NS):
                    sslot, sw = psel_r.acquire()
                    p.add("tensor", lambda e, n=n, sslot=sslot: e.matmul(
                        out=p_sel[:, sslot, :], lhsT=selB[:, n, :],
                        rhs=dlrBC_bf[:], start=True, stop=True),
                        after=tuple([f"dlrC{ch}", f"selB{NS-1}"] + sw),
                        mile=f"pB{ch}_{n}")
                    p.add("scalar", lambda e, n=n, sslot=sslot: e.copy(
                        out=Bmb_all[:, n, :], in_=p_sel[:, sslot, :]),
                        after=tuple([f"pB{ch}_{n}"] + bc_last),
                        mile=f"bm{ch}_{n}")
                    psel_r.consumed(f"bm{ch}_{n}")
                    if not pfx:
                        sslot, sw = psel_r.acquire()
                        p.add("tensor", lambda e, n=n, sslot=sslot: e.matmul(
                            out=p_sel[:, sslot, :], lhsT=selC[:, n, :],
                            rhs=dlrBC_bf[:], start=True, stop=True),
                            after=tuple([f"dlrC{ch}", "selCZ"] + sw),
                            mile=f"pC{ch}_{n}")
                        p.add("scalar", lambda e, n=n, sslot=sslot: e.copy(
                            out=Cmb_all[:, n, :], in_=p_sel[:, sslot, :]),
                            after=tuple([f"pC{ch}_{n}"] + bc_last),
                            mile=f"cm{ch}_{n}")
                        psel_r.consumed(f"cm{ch}_{n}")
            if stages < 2:
                dlr_free = [f"dlrC{ch}"]
                for ej in range(EJ):
                    xbf_free[ej] = [f"Cmm{ch}"]
                    skip_free[ej] = [f"evS{ch}_{ej}"] if not pfx else [None]
                continue

            for ej in range(EJ):
                _, bw = pbc_r.acquire()

                def wd2mm(e, ej=ej):
                    return e.matmul(out=p_bc[:],
                                    lhsT=wd2_sb[:, ej * 128:(ej + 1) * 128],
                                    rhs=dlrBC_bf[0:DR, :],
                                    start=True, stop=True)
                p.add("tensor", wd2mm,
                      after=tuple([f"dlrC{ch}", "wd2C"] + bw),
                      mile=f"Dw{ch}_{ej}")
                _, s1w = sp1r.acquire()
                p.add("scalar", lambda e, ej=ej: e.activation(
                    out=sp1, in_=p_bc[:], func=AF.Abs,
                    bias=wd2b_sb[:, ej:ej + 1]),
                    after=tuple([f"Dw{ch}_{ej}", "cst1"] + s1w))
                p.add("scalar", lambda e: e.activation(
                    out=sp1, in_=sp1, func=AF.Exp, scale=-1.0),
                    after=("fcst",))
                p.add("scalar", lambda e: e.activation(
                    out=sp1, in_=sp1, func=AF.Ln, bias=1.0))
                _, s2w = sp2r.acquire()
                p.add("scalar", lambda e, ej=ej: e.activation(
                    out=sp2, in_=p_bc[:], func=AF.Relu,
                    bias=wd2b_sb[:, ej:ej + 1]),
                    after=tuple(s2w), mile=f"spR{ch}_{ej}")
                pbc_r.consumed(f"spR{ch}_{ej}")
                dslot, dw = dlt.acquire()
                p.add("vector", lambda e, dslot=dslot: e.tensor_add(
                    out=delta_r[:, dslot, :], in0=sp1, in1=sp2),
                    after=tuple([f"spR{ch}_{ej}"] + dw), mile=f"dlt{ch}_{ej}")
                sp1r.consumed(f"dlt{ch}_{ej}")
                sp2r.consumed(f"dlt{ch}_{ej}")
                if dbg and ch == 4 and ej < 2:
                    p.add("sync", lambda e, dslot=dslot, ej=ej: e.dma_start(
                        out=dbg_t["dbg_delta"][:, ej, :],
                        in_=delta_r[:, dslot, :]),
                        after=(f"dlt{ch}_{ej}",), mile=f"DBGd{ej}", dma=True)
                    dlt.consumed(f"DBGd{ej}")
                xslot_d, dxw = dxr.acquire()
                p.add("gpsimd", lambda e, dslot=dslot, xslot_d=xslot_d, ej=ej:
                      e.tensor_tensor(out=dx_r[:, xslot_d, :],
                                      in0=delta_r[:, dslot, :],
                                      in1=x_bf[:, ej, :], op=ALU.mult),
                      after=tuple([f"dlt{ch}_{ej}"] + dxw),
                      mile=f"dx{ch}_{ej}")
                xbf_free[ej].append(f"dx{ch}_{ej}")

                if pfx:
                    aslot, aw = ar.acquire()
                    p.add("vector", lambda e, dslot=dslot, aslot=aslot:
                          e.tensor_tensor_scan(
                              out=a_r[:, aslot, :], data0=ones_sb[:],
                              data1=delta_r[:, dslot, :], initial=0.0,
                              op0=ALU.mult, op1=ALU.add),
                          after=tuple([f"dlt{ch}_{ej}", "ones"] + aw),
                          mile=f"S{ch}_{ej}")
                    dlt.consumed(f"S{ch}_{ej}", f"dx{ch}_{ej}")
                    mslot, mw = dmr.acquire()
                    p.add("gpsimd", lambda e, aslot=aslot, mslot=mslot:
                          e.tensor_scalar(
                              out=Dm_r[:, mslot, :], in0=a_r[:, aslot, :],
                              scalar1=a_r[:, aslot, C - 1:C], scalar2=None,
                              op0=ALU.subtract),
                          after=tuple([f"S{ch}_{ej}"] + mw),
                          mile=f"Dm{ch}_{ej}")
                    for n in range(NS):
                        uslot, uw = ur.acquire()
                        p.add("scalar", lambda e, mslot=mslot, uslot=uslot,
                              ej=ej, n=n: e.activation(
                                  out=u_r[:, uslot, :],
                                  in_=Dm_r[:, mslot, :], func=AF.Exp,
                                  scale=negA_sb[:, ej, n:n + 1]),
                              after=tuple([f"Dm{ch}_{ej}", "A_"] + uw),
                              mile=f"wx{ch}_{ej}_{n}")
                        gslot, gw = gr.acquire()
                        p.add("gpsimd" if n < 5 else "vector",
                              lambda e, uslot=uslot, n=n,
                              gslot=gslot: e.tensor_tensor(
                                  out=g_r[:, gslot, :],
                                  in0=u_r[:, uslot, :],
                                  in1=Bmb_all[:, n, :], op=ALU.mult),
                              after=tuple([f"wx{ch}_{ej}_{n}",
                                           f"bm{ch}_{n}"] + gw),
                              mile=f"t{ch}_{ej}_{n}")
                        ur.consumed(f"t{ch}_{ej}_{n}")
                        p.add("vector",
                              lambda e, gslot=gslot, xslot_d=xslot_d,
                              n=n: e.scalar_tensor_tensor(
                                  out=gdead, in0=g_r[:, gslot, :],
                                  scalar=1.0, in1=dx_r[:, xslot_d, :],
                                  op0=ALU.mult, op1=ALU.mult,
                                  accum_out=Gcols[:, n:n + 1]),
                              after=tuple([f"t{ch}_{ej}_{n}", f"dx{ch}_{ej}"]
                                          + (gcols_free if n == 0 else [])),
                              mile=f"sg{ch}_{ej}_{n}")
                        gr.consumed(f"sg{ch}_{ej}_{n}")
                    dxr.consumed(f"sg{ch}_{ej}_{NS-1}")
                    dmr.consumed(f"wx{ch}_{ej}_{NS-1}")
                    p.add("scalar", lambda e, ej=ej, aslot=aslot:
                          e.activation(out=Pfac[:], in_=A_sb[:, ej, :],
                                       func=AF.Exp,
                                       scale=a_r[:, aslot, C - 1:C]),
                          after=(f"S{ch}_{ej}", "A_",
                                 f"hup{ch}_{ej-1}" if ej else
                                 (f"hup{ch-1}_{EJ-1}" if ch else None)),
                          mile=f"pf{ch}_{ej}")
                    ar.consumed(f"Dm{ch}_{ej}", f"pf{ch}_{ej}")
                    p.add("vector", lambda e, ej=ej: e.tensor_mul(
                        out=hcar[:, ej * NS:(ej + 1) * NS],
                        in0=hcar[:, ej * NS:(ej + 1) * NS], in1=Pfac[:]),
                        after=(f"pf{ch}_{ej}", "hcar0")
                              + ((hcar_mile[ej],) if hcar_mile[ej] else ()),
                        strict=True, mile=f"hu1{ch}_{ej}")
                    p.add("vector", lambda e, ej=ej: e.tensor_add(
                        out=hcar[:, ej * NS:(ej + 1) * NS],
                        in0=hcar[:, ej * NS:(ej + 1) * NS], in1=Gcols[:]),
                        after=(f"sg{ch}_{ej}_{NS-1}", f"hu1{ch}_{ej}"),
                        strict=True, mile=f"hup{ch}_{ej}")
                    hcar_mile[ej] = f"hup{ch}_{ej}"
                    gcols_free = [f"hup{ch}_{ej}"]
                    if ej == EJ - 1:
                        x1b_free = [f"sg{ch}_{ej}_{NS-1}"] + x1b_free
                else:
                    for n in range(NS):
                        aslot, aw = ar.acquire()
                        p.add("scalar", lambda e, dslot=dslot, aslot=aslot,
                              ej=ej, n=n: e.activation(
                                  out=a_r[:, aslot, :],
                                  in_=delta_r[:, dslot, :], func=AF.Exp,
                                  scale=A_sb[:, ej, n:n + 1]),
                              after=tuple([f"dlt{ch}_{ej}", "A_"] + aw),
                              mile=f"ax{ch}_{ej}_{n}")
                        uslot, uw = ur.acquire()
                        p.add("gpsimd" if n < 5 else "vector",
                              lambda e, xslot_d=xslot_d, n=n,
                              uslot=uslot: e.tensor_tensor(
                                  out=u_r[:, uslot, :],
                                  in0=dx_r[:, xslot_d, :],
                                  in1=Bmb_all[:, n, :], op=ALU.mult),
                              after=tuple([f"dx{ch}_{ej}",
                                           f"bm{ch}_{n}"] + uw),
                              mile=f"u{ch}_{ej}_{n}")
                        hslot, hw = hr.acquire()
                        p.add("vector", lambda e, aslot=aslot, uslot=uslot,
                              hslot=hslot, ej=ej, n=n: e.tensor_tensor_scan(
                                  out=h_ring[:, hslot, :],
                                  data0=a_r[:, aslot, :],
                                  data1=u_r[:, uslot, :],
                                  initial=hcar[:, ej * NS + n:ej * NS + n + 1],
                                  op0=ALU.mult, op1=ALU.add),
                              after=tuple([f"ax{ch}_{ej}_{n}",
                                           f"u{ch}_{ej}_{n}"] + hw
                                          + ([hcar_mile[ej]]
                                             if n == 0 and hcar_mile[ej]
                                             else [])),
                              strict=(n == 0), mile=f"sc{ch}_{ej}_{n}")
                        ar.consumed(f"sc{ch}_{ej}_{n}")
                        ur.consumed(f"sc{ch}_{ej}_{n}")
                        if n == 7 or n == NS - 1:
                            base = ej * NS + (0 if n == 7 else 8)
                            p.add("vector", lambda e, base=base: e.tensor_copy(
                                out=hcar[:, base:base + 8],
                                in_=h_ring[:, 0:8, C - 1:C]),
                                after=(f"sc{ch}_{ej}_{n}",), strict=True,
                                mile=f"cy{ch}_{ej}_{n}")
                            if n == NS - 1:
                                hcar_mile[ej] = f"cy{ch}_{ej}_{n}"
                        gslot, gw = gr.acquire()
                        p.add("gpsimd" if n < 5 else "vector",
                              lambda e, hslot=hslot, n=n,
                              gslot=gslot: e.tensor_tensor(
                                  out=g_r[:, gslot, :],
                                  in0=h_ring[:, hslot, :],
                                  in1=Cmb_all[:, n, :], op=ALU.mult),
                              after=tuple([f"sc{ch}_{ej}_{n}",
                                           f"cm{ch}_{n}"] + gw),
                              mile=f"g{ch}_{ej}_{n}")
                        hr.consumed(f"g{ch}_{ej}_{n}",
                                    f"cy{ch}_{ej}_{7 if n <= 7 else NS-1}"
                                    if (n == 7 or n == NS - 1) else None)
                        _, yw = (py_r.acquire() if n == 0 else (None, []))
                        p.add("tensor", lambda e, gslot=gslot, n=n: e.matmul(
                            out=p_y[:], lhsT=ident[:], rhs=g_r[:, gslot, :],
                            start=(n == 0), stop=False),
                            after=tuple([f"g{ch}_{ej}_{n}"] + yw),
                            mile=f"ya{ch}_{ej}_{n}")
                        gr.consumed(f"ya{ch}_{ej}_{n}")
                    dxr.consumed(f"u{ch}_{ej}_{NS-1}")
                    dlt.consumed(f"ax{ch}_{ej}_{NS-1}", f"dx{ch}_{ej}")
                    p.add("tensor", lambda e, ej=ej: e.matmul(
                        out=p_y[:], lhsT=wddiag[:, ej, :],
                        rhs=x_bf[:, ej, :], start=False, stop=True),
                        after=(f"wdd{ej}", f"evC{ch}_{ej}"),
                        mile=f"ywd{ch}_{ej}")
                    xbf_free[ej].append(f"ywd{ch}_{ej}")
                    p.add("vector", lambda e, ej=ej: e.tensor_tensor(
                        out=y_bf[:, ej, :], in0=p_y[:],
                        in1=skip_s[:, ej, :], op=ALU.mult),
                        after=tuple([f"ywd{ch}_{ej}", f"evS{ch}_{ej}"]
                                    + ybf_free[ej]),
                        mile=f"yev{ch}_{ej}")
                    py_r.consumed(f"yev{ch}_{ej}")
                    skip_free[ej] = [f"yev{ch}_{ej}"]
                    ybf_free[ej] = []
            dlr_free = ([f"pB{ch}_{NS-1}", f"Dw{ch}_{EJ-1}"]
                        if pfx else [f"ywd{ch}_{EJ-1}"])
            for _ej in range(EJ):
                xbf_free[_ej].append(f"Cmm{ch}")
            if dbg and ch == 3 and stages >= 2:
                p.add("sync", lambda e: e.dma_start(
                    out=dbg_t["dbg_hcar"], in_=hcar[:]),
                    after=(f"hup{ch}_{EJ-1}",), mile="DBGh", dma=True)
            if dbg and ch == 4 and stages >= 2:
                p.add("sync", lambda e: e.dma_start(
                    out=dbg_t["dbg_y"], in_=y_bf[:]),
                    after=tuple(f"yev{ch}_{ej}" for ej in range(EJ)),
                    mile="DBGy", dma=True)

            if pfx or stages < 3:
                if stages < 3 and not pfx:
                    for ej in range(EJ):
                        ybf_free[ej] = [f"yev{ch}_{ej}"]
                continue

            p.add("sync", lambda e: e.dma_start(
                out=wbuf[:], in_=out_bfd.rearrange("a p b -> p a b")),
                after=tuple(wbuf_free + [f"oS{EJ-1}"]),
                mile=f"oLd{ch}", dma=True)
            wbuf_free = []
            for lt in range(4):
                slot0, w0 = pxr.acquire()
                slot1, w1 = pxr.acquire()
                assert slot0 == 0 and slot1 == 1

                def outmm(e, lt=lt):
                    last = None
                    for half in range(2):
                        for ec in range(EJ):
                            last = e.matmul(
                                out=p_x[:, half, :],
                                lhsT=y_bf[:, ec, lt * 128:(lt + 1) * 128],
                                rhs=wbuf_o[:, ec, half * 512:(half + 1) * 512],
                                start=(ec == 0), stop=(ec == EJ - 1))
                    return last
                p.add("tensor", outmm,
                      after=tuple([f"yev{ch}_{EJ-1}", f"oLd{ch}"] + w0 + w1),
                      mile=f"Fmm{ch}_{lt}")
                p.add("sync", lambda e, lt=lt, t0=t0: e.dma_start(
                    out=rs_tile[:],
                    in_=resid_own[t0 + lt * 128:t0 + (lt + 1) * 128, :]),
                    after=tuple(rs_free) + ((f"dlt{ch}_{EJ-1}",)
                                            if lt == 0 else ()),
                    mile=f"rs2{ch}_{lt}", dma=True)
                p.add("vector", lambda e: e.scalar_tensor_tensor(
                    out=x1w[:, 0:D], in0=rs_tile[:], scalar=2.0,
                    in1=p_x[:].rearrange("p a b -> p (a b)"),
                    op0=ALU.mult, op1=ALU.add),
                    after=(f"Fmm{ch}_{lt}", f"rs2{ch}_{lt}"),
                    mile=f"x1p{ch}_{lt}")
                pxr.consumed(f"x1p{ch}_{lt}")
                pxr.consumed(f"x1p{ch}_{lt}")
                rs_free = [f"x1p{ch}_{lt}"]
                if dbg and ch == 4 and lt == 0:
                    p.add("sync", lambda e: e.dma_start(
                        out=dbg_t["dbg_x1p"], in_=x1w[:, 0:D]),
                        after=(f"x1p{ch}_{lt}",), mile="DBGxp", dma=True)
                p.add("vector", lambda e: e.reduce_sum(
                    out=fmca[:, 0:1], in_=x1w[:, 0:D], axis=AX.X),
                    mile=f"msm{ch}_{lt}")
                p.add("vector", lambda e: e.tensor_scalar(
                    out=fmca[:, 1:2], in0=fmca[:, 0:1], scalar1=INV_D,
                    scalar2=None, op0=ALU.mult),
                    after=(f"msm{ch}_{lt}",), strict=True,
                    mile=f"m2{ch}_{lt}")
                p.add("vector", lambda e: e.tensor_scalar(
                    out=x1w[:, 0:D], in0=x1w[:, 0:D],
                    scalar1=fmca[:, 1:2], scalar2=None, op0=ALU.subtract),
                    after=(f"m2{ch}_{lt}",) + (("DBGxp",)
                          if (dbg and ch == 4 and lt == 0) else ()),
                    strict=True,
                    mile=f"ctr{ch}_{lt}")
                p.add("scalar", lambda e: e.activation(
                    out=sq_scr[:], in_=x1w[:, 0:D], func=AF.Square,
                    accum_out=fmca[:, 2:3]),
                    after=(f"ctr{ch}_{lt}",), mile=f"vq{ch}_{lt}")
                p.add("scalar", lambda e: e.activation(
                    out=fmca[:, 3:4], in_=fmca[:, 2:3], func=AF.Sqrt,
                    scale=INV_D, bias=EPS),
                    after=(f"vq{ch}_{lt}",), strict=True, mile=f"vs{ch}_{lt}")
                p.add("vector", lambda e: e.reciprocal(
                    out=fmca[:, 2:3], in_=fmca[:, 3:4]),
                    after=(f"vs{ch}_{lt}",), mile=f"rc{ch}_{lt}")
                p.add("vector", lambda e: e.tensor_scalar(
                    out=x1b_t[:], in0=x1w[:, 0:D], scalar1=fmca[:, 2:3],
                    scalar2=None, op0=ALU.mult),
                    after=tuple([f"rc{ch}_{lt}"] + x1b_free), strict=True,
                    mile=f"x1c{ch}_{lt}")
                if dbg and ch == 4 and lt == 0:
                    p.add("sync", lambda e: e.dma_start(
                        out=dbg_t["dbg_st"], in_=fmca[:]),
                        after=(f"x1c{ch}_{lt}",), mile="DBGst", dma=True)
                p.add("sync", lambda e, lt=lt, t0=t0: e.dma_start(
                    out=x1td[t0 + lt * 128:t0 + (lt + 1) * 128, :],
                    in_=x1b_t[:]),
                    after=(f"x1c{ch}_{lt}",), mile=f"x1S{ch}_{lt}", dma=True)
                if dbg:
                    p.add("sync", lambda e, lt=lt, t0=t0: e.dma_start(
                        out=dbg_t["dbg_x1"][t0 + lt * 128:t0 + (lt + 1) * 128, :],
                        in_=x1b_t[:]),
                        after=(f"x1c{ch}_{lt}",), mile=f"DBGx1{ch}_{lt}",
                        dma=True)
                x1b_free = ([f"x1S{ch}_{lt}"]
                            + ([f"DBGx1{ch}_{lt}"] if dbg else []))
            for ej in range(EJ):
                ybf_free[ej] = [f"Fmm{ch}_3"]
            wbuf_free = [f"Fmm{ch}_3"]

        if stages >= 4:
            attn.close()
            p.barrier_counts()

            def sbF(name, shape, dtype):
                return top.enter_context(nc.sbuf_tensor(name, shape, dtype))

            x1f = sbF("x1f", [128, DCH, TOK], BF16)
            out_acc = sbF("out_acc", [128, 16, D], F32)
            land = sbF("land", [128, DCH * FI], F32)
            gwb = sbF("gwb", [128, 2, DCH, FI], BF16)
            uwb = sbF("uwb", [128, 2, DCH, FI], BF16)
            dwb = sbF("dwb", [128, 2, 4, D], BF16)
            acts = sbF("acts", [128, 2, 4, 512], BF16)
            gc_r = sbF("gc_r", [128, 2, 512], BF16)
            sg_r = sbF("sg_r", [128, 2, 512], BF16)
            up1_r = sbF("up1_r", [128, 2, 512], BF16)
            up2_r = sbF("up2_r", [128, 2, 512], BF16)
            glu_r = sbF("glu_r", [128, 2, 512], BF16)
            gc_bf = sbF("gc_bf", [128, 512], BF16)
            sg_bf = sbF("sg_bf", [128, 512], BF16)
            up1_bf = sbF("up1_bf", [128, 512], BF16)
            up2_bf = sbF("up2_bf", [128, 512], BF16)
            glu_bf = sbF("glu_bf", [128, 512], BF16)
            x1tok = sbF("x1tok", [128, D], BF16)
            lgt = sbF("lgt", [128, EXP], F32)
            fm1 = sbF("fm1", [128, EXP], F32)
            fl2 = sbF("fl2", [128, EXP], F32)
            fmk2 = sbF("fmk2", [128, EXP], F32)
            scores = sbF("scores", [128, 16, EXP], F32)
            sc_bf = sbF("sc_bf", [128, EXP], BF16)
            fcol = sbF("fcol", [128, 8], F32)
            srT = sbF("srT", [16, TOK], BF16)
            identF = sbF("identF", [128, 128], BF16)
            router_bf = sbF("router_bf", [128, DCH, EXP], BF16)
            downb_bf = sbF("downb_bf", [16, D], BF16)
            gb_sb = sbF("gb_sb", [128, EXP, 4], F32)
            ub_sb = sbF("ub_sb", [128, EXP, 4], F32)
            flw = sbF("flw", [128, D], F32)
            fmc = sbF("fmc", [128, 4], F32)
            fsq = sbF("fsq", [128, 512], F32)

            p_ftr = top.enter_context(nc.psum_tensor("p_ftr", [128, 1024], BF16))
            p_ftr2 = top.enter_context(nc.psum_tensor("p_ftr2", [128, 512], F32))
            p_fg = top.enter_context(nc.psum_tensor("p_fg", [128, 2, 512], F32))
            p_fu = top.enter_context(nc.psum_tensor("p_fu", [128, 2, 512], F32))
            p_fd = top.enter_context(nc.psum_tensor("p_fd", [128, 1024], F32))


            p.add("gpsimd", lambda e: e.memset(identF[:], 0.0))
            p.add("gpsimd", lambda e: e.affine_select(
                out=identF[:], in_=identF[:], compare_op=ALU.not_equal,
                fill=1.0, base=0, pattern=[[-1, 128]], channel_multiplier=1),
                mile="FidI")
            p.add("sync", lambda e: e.dma_start(
                out=flw[:, 0:DCH * EXP].rearrange("p (a b) -> p a b", a=DCH),
                in_=router_d.rearrange("(a p) b -> p a b", p=128)),
                mile="rLd", dma=True)
            p.add("scalar", lambda e: e.copy(
                out=router_bf[:],
                in_=flw[:, 0:DCH * EXP].rearrange("p (a b) -> p a b", a=DCH)),
                after=("rLd",), mile="rC")
            p.add("sync", lambda e: e.dma_start(
                out=flw[0:16, 0:D], in_=downb_d), after=("rC",),
                mile="dbLd", dma=True)
            p.add("scalar", lambda e: e.copy(
                out=downb_bf[:], in_=flw[0:16, 0:D]), after=("dbLd",),
                mile="dbC")
            p.add("sync", lambda e: e.dma_start(out=gb_sb[:], in_=gateb_d),
                  mile="gbL", dma=True)
            p.add("sync", lambda e: e.dma_start(out=ub_sb[:], in_=upb_d),
                  mile="ubL", dma=True)

            x1tok_free = ["dbC"]
            for tt in range(16):
                p.add("sync", lambda e, tt=tt: e.dma_start(
                    out=x1tok[:], in_=x1td[tt * 128:(tt + 1) * 128, :]),
                    after=tuple(x1tok_free), mile=f"fxL{tt}", dma=True)

                def ftr(e, tt=tt):
                    for j in range(DCH - 1):
                        e.transpose(out=p_ftr[:, j * 128:(j + 1) * 128],
                                    in_=x1tok[:, j * 128:(j + 1) * 128],
                                    identity=identF[:])
                    return e.transpose(
                        out=p_ftr[:, (DCH - 1) * 128:DCH * 128],
                        in_=x1tok[:, (DCH - 1) * 128:DCH * 128],
                        identity=identF[:])
                p.add("tensor", ftr,
                      after=(f"fxL{tt}", "FidI",
                             f"fxE{tt-1}" if tt else None),
                      mile=f"fxT{tt}")
                x1tok_free = [f"fxT{tt}"]
                p.add("vector", lambda e, tt=tt: e.tensor_copy(
                    out=x1f[:, :, tt * 128:(tt + 1) * 128],
                    in_=p_ftr[:].rearrange("p (a b) -> p a b", a=DCH)),
                    after=(f"fxT{tt}",), mile=f"fxE{tt}")

            for tt in range(16):
                def rmm(e, tt=tt):
                    for dc in range(DCH - 1):
                        e.matmul(out=p_ftr2[:, 0:EXP],
                                 lhsT=x1f[:, dc, tt * 128:(tt + 1) * 128],
                                 rhs=router_bf[:, dc, :],
                                 start=(dc == 0), stop=False)
                    return e.matmul(
                        out=p_ftr2[:, 0:EXP],
                        lhsT=x1f[:, DCH - 1, tt * 128:(tt + 1) * 128],
                        rhs=router_bf[:, DCH - 1, :], start=False, stop=True)
                p.add("tensor", rmm,
                      after=(f"fxE{tt}", "rC",
                             f"rcp{tt-1}" if tt else None),
                      mile=f"rmm{tt}")
                p.add("vector", lambda e: e.tensor_copy(
                    out=lgt[:], in_=p_ftr2[:, 0:EXP]),
                    after=(f"rmm{tt}",), mile=f"rcp{tt}")
                p.add("vector", lambda e: e.reduce_max(
                    out=fcol[:, 0:1], in_=lgt[:], axis=AX.X),
                    after=(f"rcp{tt}",), strict=True, mile=f"rm1{tt}")
                p.add("vector", lambda e: e.tensor_scalar(
                    out=fm1[:], in0=lgt[:], scalar1=fcol[:, 0:1],
                    scalar2=None, op0=ALU.is_equal),
                    after=(f"rm1{tt}",), strict=True, mile=f"fm1m{tt}")

                p.add("vector", lambda e: e.scalar_tensor_tensor(
                    out=fl2[:], in0=fm1[:], scalar=-1e30, in1=lgt[:],
                    op0=ALU.mult, op1=ALU.add),
                    after=(f"fm1m{tt}",), strict=True, mile=f"fl2m{tt}")
                p.add("vector", lambda e: e.reduce_max(
                    out=fcol[:, 1:2], in_=fl2[:], axis=AX.X),
                    after=(f"fl2m{tt}",), strict=True, mile=f"rm2{tt}")
                p.add("vector", lambda e: e.tensor_scalar(
                    out=fmk2[:], in0=fl2[:], scalar1=fcol[:, 1:2],
                    scalar2=None, op0=ALU.is_equal),
                    after=(f"rm2{tt}",), strict=True, mile=f"mk2{tt}")
                p.add("vector", lambda e: e.tensor_scalar(
                    out=fcol[:, 2:3], in0=fcol[:, 1:2],
                    scalar1=fcol[:, 0:1], scalar2=None, op0=ALU.subtract),
                    after=(f"rm2{tt}",), strict=True, mile=f"d21{tt}")
                p.add("scalar", lambda e: e.activation(
                    out=fcol[:, 3:4], in_=fcol[:, 2:3], func=AF.Sigmoid),
                    after=(f"d21{tt}",), mile=f"w2{tt}")
                p.add("vector", lambda e: e.tensor_scalar(
                    out=fcol[:, 4:5], in0=fcol[:, 3:4], scalar1=-1.0,
                    scalar2=1.0, op0=ALU.mult, op1=ALU.add),
                    after=(f"w2{tt}", f"mk2{tt}"), strict=True,
                    mile=f"w1{tt}")
                p.add("vector", lambda e, tt=tt: e.tensor_scalar(
                    out=scores[:, tt, :], in0=fm1[:],
                    scalar1=fcol[:, 4:5], scalar2=None, op0=ALU.mult),
                    after=(f"w1{tt}",), strict=True, mile=f"sa{tt}")
                p.add("vector", lambda e, tt=tt: e.scalar_tensor_tensor(
                    out=scores[:, tt, :], in0=fmk2[:],
                    scalar=fcol[:, 3:4], in1=scores[:, tt, :],
                    op0=ALU.mult, op1=ALU.add),
                    after=(f"sa{tt}", f"mk2{tt}"), strict=True,
                    mile=f"scs{tt}")
                p.add("vector", lambda e, tt=tt: e.tensor_copy(
                    out=sc_bf[:], in_=scores[:, tt, :]),
                    after=(f"scs{tt}", f"sTe{tt-1}" if tt else None),
                    strict=True, mile=f"scb{tt}")
                p.add("tensor", lambda e: e.transpose(
                    out=p_ftr[0:EXP, 0:128], in_=sc_bf[:], identity=identF[:]),
                    after=(f"scb{tt}", f"fxE{15}",
                           f"sTe{tt-1}" if tt else None),
                    mile=f"sT{tt}")
                p.add("scalar", lambda e, tt=tt: e.copy(
                    out=srT[:, tt * 128:(tt + 1) * 128],
                    in_=p_ftr[0:EXP, 0:128]),
                    after=(f"sT{tt}",), mile=f"sTe{tt}")
                if dbg and tt == 0:
                    p.add("sync", lambda e: e.dma_start(
                        out=dbg_t["dbg_scores"], in_=scores[:, 0, :]),
                        after=("scs0",), mile="DBGsc", dma=True)
                    p.add("sync", lambda e: e.dma_start(
                        out=dbg_t["dbg_rt"][:, 0:16], in_=lgt[:]),
                        after=("scs0",), mile="DBGr1", dma=True)
                    p.add("sync", lambda e: e.dma_start(
                        out=dbg_t["dbg_rt"][:, 16:32], in_=fmk2[:]),
                        after=("scs0",), mile="DBGr2", dma=True)
                    p.add("sync", lambda e: e.dma_start(
                        out=dbg_t["dbg_rt"][:, 32:48], in_=fm1[:]),
                        after=("scs0",), mile="DBGr3", dma=True)
                    p.add("sync", lambda e: e.dma_start(
                        out=dbg_t["dbg_rt"][:, 48:56], in_=fcol[:]),
                        after=("scs0",), mile="DBGr4", dma=True)
                    p.add("vector", lambda e: e.engine_nop(),
                          after=("DBGr1", "DBGr2", "DBGr3", "DBGr4"),
                          mile="DBGrwait")

            for tt in range(16):
                def bmm(e, tt=tt):
                    e.matmul(out=p_fd[:, 0:512],
                             lhsT=srT[:, tt * 128:(tt + 1) * 128],
                             rhs=downb_bf[:, 0:512], start=True, stop=True)
                    return e.matmul(out=p_fd[:, 512:1024],
                                    lhsT=srT[:, tt * 128:(tt + 1) * 128],
                                    rhs=downb_bf[:, 512:1024],
                                    start=True, stop=True)
                p.add("tensor", bmm,
                      after=(f"sTe{tt}", "dbC", f"oaI{tt-1}" if tt else None),
                      mile=f"bmm{tt}")
                p.add("sync", lambda e, tt=tt: e.dma_start(
                    out=x1tok[:], in_=x1td[tt * 128:(tt + 1) * 128, :]),
                    after=tuple([f"bmm{tt}"] + x1tok_free),
                    mile=f"fx2L{tt}", dma=True)
                p.add("vector", lambda e, tt=tt: e.tensor_add(
                    out=out_acc[:, tt, :],
                    in0=p_fd[:], in1=x1tok[:]),
                    after=(f"bmm{tt}", f"fx2L{tt}"), mile=f"oaI{tt}")
                if dbg and tt == 0:
                    p.add("sync", lambda e: e.dma_start(
                        out=dbg_t["dbg_oa0"], in_=out_acc[:, 0, :]),
                        after=("oaI0",), mile="DBGoa0", dma=True)
                x1tok_free = [f"oaI{tt}"]

            pfg_r = Ring(2)
            pfu_r = Ring(2)
            gcr = Ring(2)
            sgr = Ring(2)
            u1r = Ring(2)
            u2r = Ring(2)
            glr = Ring(2)
            actr = Ring(2)
            acts_slot = {}
            prevload = "ubL"
            for ex in range(EXP):
                es = ex % 2
                p.add("sync", lambda e, ex=ex: e.dma_start(
                    out=land[:].rearrange("p (a b) -> p a b", a=DCH),
                    in_=gatew_d[ex].rearrange("(a p) b -> p a b", p=128)),
                    after=(prevload, f"gmmE{ex-2}" if ex >= 2 else None),
                    mile=f"gL{ex}", dma=True)
                p.add("scalar", lambda e, es=es: e.copy(
                    out=gwb[:, es, :, :],
                    in_=land[:].rearrange("p (a b) -> p a b", a=DCH)),
                    after=(f"gL{ex}",), mile=f"gC{ex}")
                p.add("sync", lambda e, ex=ex: e.dma_start(
                    out=land[:].rearrange("p (a b) -> p a b", a=DCH),
                    in_=upw_d[ex].rearrange("(a p) b -> p a b", p=128)),
                    after=(f"gC{ex}", f"ummE{ex-2}" if ex >= 2 else None),
                    mile=f"uL{ex}", dma=True)
                p.add("scalar", lambda e, es=es: e.copy(
                    out=uwb[:, es, :, :],
                    in_=land[:].rearrange("p (a b) -> p a b", a=DCH)),
                    after=(f"uL{ex}",), mile=f"uC{ex}")
                p.add("sync", lambda e, ex=ex: e.dma_start(
                    out=land[:, 0:4 * D].rearrange("p (a b) -> p a b", a=4),
                    in_=downw_d[ex].rearrange("(a p) b -> p a b", p=128)),
                    after=(f"uC{ex}", f"dmmE{ex-2}" if ex >= 2 else None),
                    mile=f"dL{ex}", dma=True)
                p.add("scalar", lambda e, es=es: e.copy(
                    out=dwb[:, es, :, :],
                    in_=land[:, 0:4 * D].rearrange("p (a b) -> p a b", a=4)),
                    after=(f"dL{ex}",), mile=f"dC{ex}")
                prevload = f"dC{ex}"
                for tsb in range(4):
                    acs_, acw_ = actr.acquire()
                    acts_slot[(ex, tsb)] = acs_
                    for fc in range(4):
                        gk = f"{ex}_{tsb}_{fc}"
                        gslot_, gw_ = pfg_r.acquire()

                        def gmm(e, es=es, tsb=tsb, fc=fc, s=gslot_):
                            last = None
                            for dc in range(DCH):
                                last = e.matmul(
                                    out=p_fg[:, s, :],
                                    lhsT=gwb[:, es, dc, fc * 128:(fc + 1) * 128],
                                    rhs=x1f[:, dc, tsb * 512:(tsb + 1) * 512],
                                    start=(dc == 0), stop=(dc == DCH - 1))
                            return last
                        p.add("tensor", gmm,
                              after=tuple([f"gC{ex}", f"fxE{15}"] + gw_),
                              mile=f"gmm{gk}" + ("E" if tsb == 3 and fc == 3
                                                 else ""))
                        gmm_m = f"gmm{gk}" + ("E" if tsb == 3 and fc == 3 else "")
                        gcs_, gcw_ = gcr.acquire()
                        p.add("vector", lambda e, ex=ex, fc=fc, s=gslot_,
                              o=gcs_: e.tensor_scalar(
                                  out=gc_r[:, o, :], in0=p_fg[:, s, :],
                                  scalar1=gb_sb[:, ex, fc:fc + 1],
                                  scalar2=7.0, op0=ALU.add, op1=ALU.min),
                              after=tuple([gmm_m, "gbL"] + gcw_),
                              mile=f"gev{gk}")
                        pfg_r.consumed(f"gev{gk}")
                        sgs_, sgw_ = sgr.acquire()
                        p.add("scalar", lambda e, i=gcs_, o=sgs_: e.activation(
                            out=sg_r[:, o, :], in_=gc_r[:, i, :],
                            func=AF.Sigmoid, scale=1.702),
                            after=tuple([f"gev{gk}"] + sgw_),
                            mile=f"sig{gk}")
                        uslot_, uw_ = pfu_r.acquire()

                        def umm(e, es=es, tsb=tsb, fc=fc, s=uslot_):
                            last = None
                            for dc in range(DCH):
                                last = e.matmul(
                                    out=p_fu[:, s, :],
                                    lhsT=uwb[:, es, dc, fc * 128:(fc + 1) * 128],
                                    rhs=x1f[:, dc, tsb * 512:(tsb + 1) * 512],
                                    start=(dc == 0), stop=(dc == DCH - 1))
                            return last
                        p.add("tensor", umm,
                              after=tuple([f"uC{ex}"] + uw_),
                              mile=f"umm{gk}" + ("E" if tsb == 3 and fc == 3
                                                 else ""))
                        umm_m = f"umm{gk}" + ("E" if tsb == 3 and fc == 3 else "")
                        u1s_, u1w_ = u1r.acquire()
                        p.add("vector", lambda e, ex=ex, fc=fc, s=uslot_,
                              o=u1s_: e.tensor_scalar(
                                  out=up1_r[:, o, :], in0=p_fu[:, s, :],
                                  scalar1=ub_sb[:, ex, fc:fc + 1],
                                  scalar2=7.0, op0=ALU.add, op1=ALU.min),
                              after=tuple([umm_m, "ubL"] + u1w_),
                              mile=f"uev{gk}")
                        pfu_r.consumed(f"uev{gk}")
                        u2s_, u2w_ = u2r.acquire()
                        p.add("gpsimd", lambda e, i=u1s_, o=u2s_: e.tensor_scalar(
                            out=up2_r[:, o, :], in0=up1_r[:, i, :],
                            scalar1=-7.0, scalar2=1.0, op0=ALU.max,
                            op1=ALU.add),
                            after=tuple([f"uev{gk}"] + u2w_),
                            mile=f"up2{gk}")
                        u1r.consumed(f"up2{gk}")
                        gls_, glw_ = glr.acquire()
                        p.add("vector", lambda e, i1=gcs_, i2=sgs_, o=gls_:
                              e.tensor_tensor(
                                  out=glu_r[:, o, :], in0=gc_r[:, i1, :],
                                  in1=sg_r[:, i2, :], op=ALU.mult),
                              after=tuple([f"sig{gk}", f"gev{gk}"] + glw_),
                              mile=f"glu{gk}")
                        gcr.consumed(f"glu{gk}")
                        sgr.consumed(f"glu{gk}")
                        p.add("vector", lambda e, i1=u2s_, i2=gls_, o=acs_,
                              fc=fc: e.tensor_tensor(
                                  out=acts[:, o, fc, :],
                                  in0=up2_r[:, i1, :],
                                  in1=glu_r[:, i2, :], op=ALU.mult),
                              after=tuple(
                                  [f"up2{gk}", f"glu{gk}",
                                   "DBGact" if dbg and ex == 0 and tsb == 1
                                   and fc == 0 else None]
                                  + (acw_ if fc == 0 else [])),
                              mile=f"act{gk}")
                        u2r.consumed(f"act{gk}")
                        glr.consumed(f"act{gk}")
                    if dbg and ex == 0 and tsb == 0:
                        p.add("sync", lambda e: e.dma_start(
                            out=dbg_t["dbg_act"],
                            in_=acts[:, acts_slot[(0, 0)], :, :]),
                            after=("act0_0_3",), mile="DBGact", dma=True)
                    for tt2 in range(4):
                        aslot2 = acts_slot[(ex, tsb)]

                        def dmm(e, es=es, tsb=tsb, tt2=tt2, s=aslot2):
                            last = None
                            for half in range(2):
                                for ic in range(4):
                                    last = e.matmul(
                                        out=p_fd[:, half * 512:(half + 1) * 512],
                                        lhsT=acts[:, s, ic,
                                                  tt2 * 128:(tt2 + 1) * 128],
                                        rhs=dwb[:, es, ic,
                                                half * 512:(half + 1) * 512],
                                        start=(ic == 0), stop=(ic == 3))
                            return last
                        dk = f"{ex}_{tsb}_{tt2}"
                        p.add("tensor", dmm,
                              after=(f"act{ex}_{tsb}_3", f"dC{ex}",
                                     f"oaI{15}",
                                     f"dev{ex}_{tsb}_{tt2-1}" if tt2 else
                                     (f"dev{ex}_{tsb-1}_3" if tsb else
                                      (f"dev{ex-1}_3_3" if ex else None))),
                              mile=f"dmm{dk}")
                        if tt2 == 3:
                            actr.consumed(f"dmm{dk}")
                        tg = tsb * 4 + tt2
                        p.add("vector", lambda e, ex=ex, tg=tg: (
                            e.scalar_tensor_tensor(
                                out=out_acc[:, tg, 0:512],
                                in0=p_fd[:, 0:512],
                                scalar=scores[:, tg, ex:ex + 1],
                                in1=out_acc[:, tg, 0:512],
                                op0=ALU.mult, op1=ALU.add),
                            e.scalar_tensor_tensor(
                                out=out_acc[:, tg, 512:1024],
                                in0=p_fd[:, 512:1024],
                                scalar=scores[:, tg, ex:ex + 1],
                                in1=out_acc[:, tg, 512:1024],
                                op0=ALU.mult, op1=ALU.add))[-1],
                            after=(f"dmm{dk}", f"oaI{tg}", f"scs{tg}"),
                            strict=True, mile=f"dev{ex}_{tsb}_{tt2}")
                p.ms[f"gmmE{ex}"] = p.ms[f"gmm{ex}_3_3E"]
                p.ms[f"ummE{ex}"] = p.ms[f"umm{ex}_3_3E"]
                p.ms[f"dmmE{ex}"] = p.ms[f"dmm{ex}_3_3"]

            if dbg:
                p.add("sync", lambda e: e.dma_start(
                    out=dbg_t["dbg_oaf"], in_=out_acc[:, 0, :]),
                    after=(f"dev{EXP-1}_0_0",), mile="DBGoaf", dma=True)
            outw = []
            for tt in range(16):
                p.add("vector", lambda e, tt=tt: e.reduce_sum(
                    out=fmc[:, 0:1], in_=out_acc[:, tt, :], axis=AX.X),
                    after=(f"dev{EXP-1}_{tt//4}_{tt%4}",), mile=f"Fms{tt}")
                p.add("vector", lambda e: e.tensor_scalar(
                    out=fmc[:, 1:2], in0=fmc[:, 0:1], scalar1=INV_D,
                    scalar2=None, op0=ALU.mult),
                    after=(f"Fms{tt}",), strict=True, mile=f"Fm2{tt}")
                p.add("vector", lambda e, tt=tt: e.tensor_scalar(
                    out=flw[:], in0=out_acc[:, tt, :], scalar1=fmc[:, 1:2],
                    scalar2=None, op0=ALU.subtract),
                    after=(f"Fm2{tt}", f"OUT{tt-1}" if tt else "dbC"),
                    strict=True, mile=f"Fctr{tt}")
                p.add("scalar", lambda e: e.activation(
                    out=fsq[:], in_=flw[:, 0:512], func=AF.Square,
                    accum_out=fmc[:, 2:3]),
                    after=(f"Fctr{tt}",))
                p.add("scalar", lambda e: e.activation(
                    out=fsq[:], in_=flw[:, 512:1024], func=AF.Square,
                    accum_out=fmc[:, 3:4]), mile=f"Fvq{tt}")
                p.add("vector", lambda e: e.tensor_add(
                    out=fmc[:, 2:3], in0=fmc[:, 2:3], in1=fmc[:, 3:4]),
                    after=(f"Fvq{tt}",), strict=True, mile=f"Fva{tt}")
                p.add("vector", lambda e: e.tensor_scalar(
                    out=fmc[:, 2:3], in0=fmc[:, 2:3], scalar1=INV_D,
                    scalar2=EPS, op0=ALU.mult, op1=ALU.add),
                    after=(f"Fva{tt}",), strict=True, mile=f"Fv{tt}")
                p.add("scalar", lambda e: e.activation(
                    out=fmc[:, 3:4], in_=fmc[:, 2:3], func=AF.Sqrt),
                    after=(f"Fv{tt}",), mile=f"Fsd{tt}")
                p.add("vector", lambda e: e.reciprocal(
                    out=fmc[:, 2:3], in_=fmc[:, 3:4]),
                    after=(f"Fsd{tt}",), mile=f"Frc{tt}")
                p.add("vector", lambda e: e.tensor_scalar(
                    out=flw[:], in0=flw[:], scalar1=fmc[:, 2:3],
                    scalar2=None, op0=ALU.mult),
                    after=(f"Frc{tt}",), strict=True, mile=f"Fsc{tt}")
                p.add("sync", lambda e, tt=tt: e.dma_start(
                    out=out_d[tt * 128:(tt + 1) * 128, :], in_=flw[:]),
                    after=(f"Fsc{tt}",), mile=f"OUT{tt}", dma=True)
                outw.append(f"OUT{tt}")

        if stages < 4:
            attn.close()

        qs_total = p.cnt["qS"]
        qg_total = p.cnt["qG"]
        p.ops["sync"].append(([("qS", qs_total)], lambda e: e.nop(), None))
        if qg_total:
            p.ops["gpsimd"].append(
                ([("qG", qg_total)], lambda e: e.nop(), None))

        with nc.Block() as block:
            p.emit_all(block, semh)

    return nc


_NC_CACHE = {}


def _get_nc(stages=5, dbg=False):
    key = (stages, dbg)
    if key not in _NC_CACHE:
        _NC_CACHE[key] = build(stages, dbg)
    return _NC_CACHE[key]


def make_in_maps(inp):
    f32 = np.float32
    resid = np.asarray(inp["L0_attn_resid"], f32)
    in_wT = np.ascontiguousarray(np.asarray(inp["L0_attn_in_w"], f32).T)
    skip_wT = np.ascontiguousarray(np.asarray(inp["L0_attn_skip_w"], f32).T)
    normw8 = np.ascontiguousarray(
        np.asarray(inp["L0_attn_norm_w"], f32).reshape(DCH, 128).T)
    wd1T = np.asarray(inp["L0_attn_wd1"], f32).T
    wBT = np.asarray(inp["L0_attn_wB"], f32).T
    wCT = np.asarray(inp["L0_attn_wC"], f32).T
    wcat = np.ascontiguousarray(np.concatenate([wd1T, wBT, wCT], axis=1))
    wcatp0 = wcat.copy()
    wcatp0[:, 64:80] = 0.0
    wd2T = np.ascontiguousarray(np.asarray(inp["L0_attn_wd2"], f32).T)
    wd2b = np.ascontiguousarray(
        np.asarray(inp["L0_attn_wd2_b"], f32).reshape(EJ, 128).T)
    convw = np.ascontiguousarray(
        np.asarray(inp["L0_attn_conv_w"], f32).reshape(EJ, 128, DCV)
        .transpose(1, 0, 2))
    convb = np.ascontiguousarray(
        np.asarray(inp["L0_attn_conv_b"], f32).reshape(EJ, 128).T)
    alog = np.ascontiguousarray(
        np.asarray(inp["L0_attn_A_log"], f32).reshape(EJ, 128, NS)
        .transpose(1, 0, 2))
    wdv = np.ascontiguousarray(
        np.asarray(inp["L0_attn_W_D"], f32).reshape(EJ, 128).T)
    out_wT = np.ascontiguousarray(np.asarray(inp["L0_attn_out_w"], f32).T)
    router = np.ascontiguousarray(np.asarray(inp["L0_ffn_router_weight"], f32))
    gup = np.asarray(inp["L0_ffn_gate_up_proj"], f32)
    gatew = np.ascontiguousarray(gup[:, :, 0::2])
    upw = np.ascontiguousarray(gup[:, :, 1::2])
    gupb = np.asarray(inp["L0_ffn_gate_up_proj_bias"], f32)
    gateb = np.ascontiguousarray(
        gupb[:, 0::2].reshape(EXP, 4, 128).transpose(2, 0, 1))
    upb = np.ascontiguousarray(
        gupb[:, 1::2].reshape(EXP, 4, 128).transpose(2, 0, 1))
    downw = np.ascontiguousarray(np.asarray(inp["L0_ffn_down_proj"], f32))
    downb = np.ascontiguousarray(np.asarray(inp["L0_ffn_down_proj_bias"], f32))
    zeros = np.zeros((TOK, D), f32)
    fconsts = np.tile(np.array(CONST_VALS + (0.0,) * (8 - len(CONST_VALS)),
                               f32), (128, 1))

    shared = dict(in_wT=in_wT, skip_wT=skip_wT, normw8=normw8, wcatT=wcat,
                  wd2T=wd2T, wd2b=wd2b, convw=convw, convb=convb, alog=alog,
                  wdvec=wdv, out_wT=out_wT, router=router, gatew=gatew,
                  fconsts=fconsts,
                  upw=upw, gateb=gateb, upb=upb, downw=downw, downb=downb)
    in_maps = []
    for c in range(8):
        b, r = c // 2, c % 2
        m = dict(shared)
        m["resid_own"] = np.ascontiguousarray(resid[b, r * TOK:(r + 1) * TOK])
        m["resid_pfx"] = zeros if r == 0 else np.ascontiguousarray(
            resid[b, 0:TOK])
        m["wcatTp"] = wcatp0 if r == 0 else wcat
        in_maps.append(m)
    return in_maps


def kernel(**inp):
    nc = _get_nc(5, False)
    in_maps = make_in_maps(inp)
    res = run_bass_kernel_spmd(nc, in_maps, core_ids=list(range(8)))
    out = np.empty((B, L, D), np.float32)
    for c in range(8):
        b, r = c // 2, c % 2
        out[b, r * TOK:(r + 1) * TOK] = res.results[c]["out"]
    return out


# version 07
# speedup vs baseline: 1.7135x; 1.2068x over previous
import numpy as np
from contextlib import ExitStack

import concourse.bass as bass
import concourse.mybir as mybir
from concourse.bass_utils import run_bass_kernel_spmd

F32 = mybir.dt.float32
BF16 = mybir.dt.bfloat16
AF = mybir.ActivationFunctionType
ALU = mybir.AluOpType
AX = mybir.AxisListType

B, L, D, E, NS, DR, DCV = 4, 4096, 1024, 2048, 16, 64, 4
NV = 6
EXP, FI = 16, 512
TOK = 2048
C = 512
NCH = TOK // C
EJ = E // 128
DCH = D // 128
EPS = 1e-5
INV_D = 1.0 / 1024.0
CONST_VALS = (EPS, INV_D, -1.0, 1.0, 1.702)

ENGINES = ("sync", "scalar", "vector", "tensor", "gpsimd")


class Ring:

    def __init__(self, depth):
        self.depth = depth
        self.hist = []

    def acquire(self):
        i = len(self.hist)
        slot = i % self.depth
        waits = list(self.hist[i - self.depth]) if i >= self.depth else []
        self.hist.append([])
        return slot, waits

    def consumed(self, *miles):
        self.hist[-1].extend(m for m in miles if m)


class Prog:
    def __init__(self):
        self.ops = {e: [] for e in ENGINES}
        self.cnt = {s: 0 for s in ("qS", "qG", "sS", "sV", "sP", "sG")}
        self.ms = {}
        self.wm = {}
        self.esem = {"sync": "qS", "scalar": "sS", "vector": "sV",
                     "tensor": "sP", "gpsimd": "sG"}
        self.pending = {e: [] for e in ENGINES}
        self.pending_counts = {}

    def barrier(self, names):
        for e in ENGINES:
            self.pending[e].extend(names)

    def barrier_counts(self):
        snap = [(s, v) for s, v in self.cnt.items() if v > 0]
        for e in ENGINES:
            self.pending_counts.setdefault(e, []).extend(snap)

    def add(self, eng, fn, after=(), mile=None, dma=False, strict=False):
        names = [m for m in after if m] + self.pending[eng]
        self.pending[eng] = []
        waits = []
        for s, v in self.pending_counts.pop(eng, []):
            own = (s == self.esem[eng]) and s not in ("qS", "qG")
            if own:
                continue
            key = (eng, s)
            if self.wm.get(key, -1) < v:
                self.wm[key] = v
                waits.append((s, v))
        my_sem = "qG" if (eng == "gpsimd" and dma) else self.esem[eng]
        for m in names:
            assert m in self.ms, f"unknown milestone {m}"
            s, v = self.ms[m]
            own = (s == self.esem[eng]) and s not in ("qS", "qG")
            if own and not strict:
                continue
            key = (eng, s)
            if self.wm.get(key, -1) >= v:
                continue
            self.wm[key] = v
            waits.append((s, v))
        inc = None
        if mile is not None:
            amt = 16 if dma else 1
            self.cnt[my_sem] += amt
            self.ms[mile] = (my_sem, self.cnt[my_sem])
            inc = (my_sem, amt)
        self.ops[eng].append((waits, fn, inc))
        return mile

    def emit_all(self, block, semh):
        for name in ENGINES:
            def section(eng_obj, name=name):
                for waits, fn, inc in self.ops[name]:
                    for (s, v) in waits:
                        eng_obj.wait_ge(semh[s], v)
                    inst = fn(eng_obj)
                    if inc is not None:
                        inst.then_inc(semh[inc[0]], inc[1])
            getattr(block, name)(section)


def build(stages=5, dbg=False):
    nc = bass.Bass("TRN2", target_bir_lowering=False)

    def din(name, shape, dtype=F32):
        return nc.dram_tensor(name, shape, dtype, kind="ExternalInput").ap()

    resid_own = din("resid_own", [TOK, D])
    resid_pfx = din("resid_pfx", [TOK, D])
    in_wT = din("in_wT", [D, E])
    skip_wT = din("skip_wT", [D, E])
    normw8 = din("normw8", [128, DCH])
    wcatT = din("wcatT", [E, 96])
    wcatTp = din("wcatTp", [E, 96])
    wd2T_d = din("wd2T", [DR, E])
    wd2b_d = din("wd2b", [128, EJ])
    convw_d = din("convw", [128, EJ, DCV])
    convb_d = din("convb", [128, EJ])
    alog_d = din("alog", [128, EJ, NS])
    wd_d = din("wdvec", [128, EJ])
    out_wT = din("out_wT", [E, D])
    router_d = din("router", [D, EXP])
    gatew_d = din("gatew", [EXP, D, FI])
    upw_d = din("upw", [EXP, D, FI])
    gateb_d = din("gateb", [128, EXP, FI // 128])
    upb_d = din("upb", [128, EXP, FI // 128])
    downw_d = din("downw", [EXP, FI, D])
    downb_d = din("downb", [EXP, D])
    fconst_d = din("fconsts", [128, 8])

    in_bfd = nc.dram_tensor("in_bfd", [DCH, 128, E], BF16, kind="Internal").ap()
    skip_bfd = nc.dram_tensor("skip_bfd", [DCH, 128, E], BF16, kind="Internal").ap()
    out_bfd = nc.dram_tensor("out_bfd", [EJ, 128, D], BF16, kind="Internal").ap()
    x1td = nc.dram_tensor("x1td", [TOK, D], BF16, kind="Internal").ap()
    out_d = nc.dram_tensor("out", [TOK, D], F32, kind="ExternalOutput").ap()
    dbg_t = {}
    if dbg:
        for nm, shape, dty in (
                ("dbg_xbf", [128, EJ, C], BF16), ("dbg_dlr", [96, C], BF16),
                ("dbg_delta", [128, 2, C], F32), ("dbg_y", [128, EJ, C], BF16),
                ("dbg_hcar", [128, 256], F32), ("dbg_x1", [TOK, D], BF16),
                ("dbg_x1p", [128, D], F32), ("dbg_st", [128, 4], F32),
                ("dbg_rt", [128, 64], F32), ("dbg_oa0", [128, D], F32),
                ("dbg_oaf", [128, D], F32), ("dbg_act", [128, 4, 512], BF16),
                ("dbg_scores", [128, EXP], F32)):
            dbg_t[nm] = nc.dram_tensor(nm, shape, dty, kind="ExternalOutput").ap()

    p = Prog()

    with ExitStack() as top:
        semh = {s: top.enter_context(nc.semaphore(s))
                for s in ("qS", "qG", "sS", "sV", "sP", "sG")}
        fconst = top.enter_context(nc.sbuf_tensor("fconst", [128, 8], F32))
        for _i, _v in enumerate(CONST_VALS):
            nc.const_aps.register(F32, _v, fconst[:, _i:_i + 1])
        attn = ExitStack()

        def sbA(name, shape, dtype):
            return attn.enter_context(nc.sbuf_tensor(name, shape, dtype))

        wbuf = sbA("wbuf", [128, DCH, E], BF16)
        wbuf_o = wbuf.ap().rearrange("p a b -> p (a b)").rearrange(
            "p (a b) -> p a b", a=EJ)
        wcat_sb = sbA("wcat_sb", [128, EJ, 96], BF16)
        wcatp_sb = wcat_sb
        wd2_sb = sbA("wd2_sb", [DR, E], BF16)
        ident = sbA("ident", [128, 128], BF16)
        cdiag = sbA("cdiag", [128, EJ * DCV, 128], BF16)
        wddiag = sbA("wddiag", [128, EJ, 128], BF16)
        A_sb = sbA("A_sb", [128, EJ, NS], F32)
        negA_sb = sbA("negA_sb", [128, EJ, NS], F32)
        normw_sb = sbA("normw_sb", [128, DCH], F32)
        wd2b_sb = sbA("wd2b_sb", [128, EJ], F32)
        convb_sb = sbA("convb_sb", [128, EJ], F32)
        convw_sb = sbA("convw_sb", [128, EJ, DCV], F32)
        wd_sb = sbA("wd_sb", [128, EJ], F32)
        selB = sbA("selB", [96, NS, 128], BF16)
        selC = sbA("selC", [96, NS, 128], BF16)
        rs_tile = sbA("rs_tile", [128, D], F32)
        sp2 = rs_tile.ap()[:, 0:C]
        sp1 = rs_tile.ap()[:, C:2 * C]
        sq_scr = sbA("sq_scr", [128, D], F32)
        ssq = sbA("ssq", [128, 4], F32)
        mcol = sbA("mcol", [128, 8], F32)
        xn_feat = sbA("xn_feat", [128, DCH, C], BF16)
        xpre_r = sbA("xpre_r", [128, 2, C + 3], BF16)
        halo = sbA("halo", [128, EJ, 3], BF16)
        x_bf = sbA("x_bf", [128, EJ, C], BF16)
        skip_s = sbA("skip_s", [128, EJ, C], BF16)
        dlrBC_bf = sbA("dlrBC_bf", [96, C], BF16)

        delta_r = sbA("delta_r", [128, 2, C], F32)
        dx_r = sbA("dx_r", [128, 2, C], BF16)
        ones_sb = sbA("ones_sb", [128, C], BF16)
        a_r = sbA("a_r", [128, 3, C], F32)
        u_r = sbA("u_r", [128, 3, C], BF16)
        h_ring = sbA("h_ring", [128, 8, C], BF16)
        g_r = sbA("g_r", [128, 3, C], BF16)

        x1w_full = None
        Gcols = sbA("Gcols", [128, NS], F32)
        Pfac = sbA("Pfac", [128, NS], F32)
        hcar = sbA("hcar", [128, 256], F32)
        Bmb_all = sbA("Bmb_all", [128, NS, C], BF16)
        Cmb_all = sbA("Cmb_all", [128, NS, C], BF16)
        y_bf = sbA("y_bf", [128, EJ, C], BF16)
        wcast_sb = y_bf.ap().rearrange("p a b -> p (a b)")[:, 0:E]
        x1w = sbA("x1w", [128, E], F32)
        Dm_r = x1w.ap()[:, D:E].rearrange("p (a b) -> p a b", a=2)

        x1b_t = sbA("x1b_t", [128, D], BF16)
        gdead = x1b_t.ap()[:, 0:C]
        fmca = sbA("fmca", [128, 4], F32)

        p_tr = attn.enter_context(nc.psum_tensor("p_tr", [128, 1024], BF16))
        p_x = attn.enter_context(nc.psum_tensor("p_x", [128, 2, 512], F32))
        p_c = attn.enter_context(nc.psum_tensor("p_c", [128, 512], F32))
        p_bc = attn.enter_context(nc.psum_tensor("p_bc", [128, 512], F32))
        p_sel = attn.enter_context(nc.psum_tensor("p_sel", [128, 2, 512], F32))
        p_y = attn.enter_context(nc.psum_tensor("p_y", [128, 512], F32))

        p.add("sync", lambda e: e.dma_start(out=fconst[:], in_=fconst_d),
              mile="fcst", dma=True)
        csts = []
        alog_sb = x1w.ap()[:, 0:EJ * NS].rearrange("p (a b) -> p a b", a=EJ)
        for i, (sb, dr) in enumerate((
                (normw_sb, normw8), (wd2b_sb, wd2b_d), (convb_sb, convb_d),
                (convw_sb, convw_d), (wd_sb, wd_d), (alog_sb, alog_d))):
            p.add("sync", lambda e, sb=sb, dr=dr: e.dma_start(
                out=sb if isinstance(sb, bass.AP) else sb[:], in_=dr),
                  mile=f"cst{i}", dma=True)
            csts.append(f"cst{i}")
        p.add("scalar", lambda e: e.activation(
            out=negA_sb[:], in_=alog_sb, func=AF.Exp), after=("cst5",),
            mile="negA")
        p.add("vector", lambda e: e.tensor_scalar(
            out=A_sb[:], in0=negA_sb[:], scalar1=-1.0, scalar2=None,
            op0=ALU.mult), after=("negA",), mile="A_")
        p.add("sync", lambda e: e.dma_start(
            out=x1w[:, 0:EJ * 96].rearrange("p (j k) -> p j k", j=EJ),
            in_=wcatTp.rearrange("(j p) k -> p j k", p=128)),
            after=("negA",), mile="wcpL", dma=True)
        p.add("scalar", lambda e: e.copy(
            out=wcat_sb[:], in_=x1w[:, 0:EJ * 96].rearrange(
                "p (j k) -> p j k", j=EJ)),
            after=("wcpL",), mile="wcpC")
        prev = "wcpC"
        p.add("sync", lambda e: e.dma_start(out=x1w[0:DR, 0:E], in_=wd2T_d),
              after=("wcpC",), mile="wd2L", dma=True)
        p.add("scalar", lambda e: e.copy(out=wd2_sb[:], in_=x1w[0:DR, 0:E]),
              after=("wd2L",), mile="wd2C")
        p.add("gpsimd", lambda e: e.memset(ident[:], 0.0))
        p.add("gpsimd", lambda e: e.affine_select(
            out=ident[:], in_=ident[:], compare_op=ALU.not_equal, fill=1.0,
            base=0, pattern=[[-1, 128]], channel_multiplier=1), mile="idI")
        p.add("gpsimd", lambda e: e.memset(selB[:], 0.0))
        p.add("gpsimd", lambda e: e.memset(selC[:], 0.0))
        for n in range(NS):
            p.add("gpsimd", lambda e, n=n: e.affine_select(
                out=selB[:, n, :], in_=selB[:, n, :],
                compare_op=ALU.not_equal, fill=1.0, base=-(64 + n),
                pattern=[[0, 128]], channel_multiplier=1),
                mile=f"selB{n}" if n == NS - 1 else None)
            p.add("gpsimd", lambda e, n=n: e.affine_select(
                out=selC[:, n, :], in_=selC[:, n, :],
                compare_op=ALU.not_equal, fill=1.0, base=-(80 + n),
                pattern=[[0, 128]], channel_multiplier=1),
                mile="selCZ" if n == NS - 1 else None)
        p.add("gpsimd", lambda e: e.memset(ones_sb[:], 1.0), mile="ones")
        p.add("gpsimd", lambda e: e.memset(hcar[:], 0.0), mile="hcar0")
        p.add("gpsimd", lambda e: e.memset(halo[:], 0.0), mile="halo0")
        for ej in range(EJ):
            for k in range(DCV):
                p.add("vector", lambda e, ej=ej, k=k: e.tensor_scalar(
                    out=cdiag[:, ej * DCV + k, :], in0=ident[:],
                    scalar1=convw_sb[:, ej, k:k + 1], scalar2=None,
                    op0=ALU.mult), after=("idI", "cst3"),
                    mile=f"cd{ej}" if k == DCV - 1 else None)
            p.add("vector", lambda e, ej=ej: e.tensor_scalar(
                out=wddiag[:, ej, :], in0=ident[:],
                scalar1=wd_sb[:, ej:ej + 1], scalar2=None, op0=ALU.mult),
                after=("idI", "cst4"), mile=f"wdd{ej}")
        prev = "wd2C"
        for nm, src, dst, fold in (("in", in_wT, in_bfd, True),
                                   ("sk", skip_wT, skip_bfd, True)):
            for dc in range(DCH):
                p.add("sync", lambda e, src=src, dc=dc: e.dma_start(
                    out=x1w[:], in_=src[dc * 128:(dc + 1) * 128, :]),
                    after=(prev,), mile=f"{nm}L{dc}", dma=True)
                p.add("vector", lambda e, dc=dc: e.tensor_scalar(
                    out=wcast_sb, in0=x1w[:],
                    scalar1=normw_sb[:, dc:dc + 1], scalar2=None,
                    op0=ALU.mult), after=(f"{nm}L{dc}", "cst0"),
                    mile=f"{nm}C{dc}")
                p.add("sync", lambda e, dst=dst, dc=dc: e.dma_start(
                    out=dst[dc], in_=wcast_sb),
                    after=(f"{nm}C{dc}",), mile=f"{nm}S{dc}", dma=True)
                prev = f"{nm}C{dc}"
        for ec in range(EJ):
            p.add("sync", lambda e, ec=ec: e.dma_start(
                out=x1w[:, 0:D], in_=out_wT[ec * 128:(ec + 1) * 128, :]),
                after=(prev,), mile=f"oL{ec}", dma=True)
            p.add("scalar", lambda e: e.copy(
                out=wcast_sb[:, 0:D], in_=x1w[:, 0:D]),
                after=(f"oL{ec}", f"oS{ec-1}" if ec else None),
                mile=f"oC{ec}")
            p.add("sync", lambda e, ec=ec: e.dma_start(
                out=out_bfd[ec], in_=wcast_sb[:, 0:D]),
                after=(f"oC{ec}",), mile=f"oS{ec}", dma=True)
            prev = f"oC{ec}"

        rs_free = [f"oS{EJ-1}"]
        x1b_free = ["wd2C"]
        xnfeat_free = [None]
        wbuf_free = [f"skS{DCH-1}"]
        xpre = Ring(2)
        pxr = Ring(2)
        pc_r = Ring(1)
        pbc_r = Ring(1)
        psel_r = Ring(2)
        py_r = Ring(1)
        ptr_r = Ring(1)
        dlt = Ring(2)
        dxr = Ring(2)
        ar = Ring(3)
        ur = Ring(3)
        hr = Ring(8)
        gr = Ring(3)
        bmb = Ring(4)
        cmb = Ring(4)
        dmr = Ring(2)
        sp1r = Ring(1)
        sp2r = Ring(1)
        xbf_free = {ej: ["A_"] for ej in range(EJ)}
        skip_free = {ej: [None] for ej in range(EJ)}
        ybf_free = {ej: [None] for ej in range(EJ)}
        dlr_free = [None]
        halo_v = {ej: "halo0" for ej in range(EJ)}
        gcols_free = ["hcar0"]
        hcar_mile = {ej: None for ej in range(EJ)}

        for ch in range(8):
            pfx = ch < 4
            t0 = (ch % 4) * C
            rsrc = resid_pfx if pfx else resid_own

            for tt in range(4):
                p.add("sync", lambda e, tt=tt, rsrc=rsrc, t0=t0: e.dma_start(
                    out=rs_tile[:], in_=rsrc[t0 + tt * 128:t0 + (tt + 1) * 128, :]),
                    after=tuple(rs_free) + ((f"dlt{ch-1}_{EJ-1}",)
                                            if tt == 0 and ch > 0 else ()),
                    mile=f"rs{ch}_{tt}", dma=True)
                p.add("scalar", lambda e, tt=tt: e.activation(
                    out=sq_scr[:], in_=rs_tile[:], func=AF.Square,
                    accum_out=ssq[:, tt:tt + 1]),
                    after=(f"rs{ch}_{tt}",), mile=f"sq{ch}_{tt}")
                p.add("scalar", lambda e, tt=tt: e.activation(
                    out=mcol[:, tt:tt + 1], in_=ssq[:, tt:tt + 1],
                    func=AF.Sqrt, scale=INV_D, bias=EPS),
                    after=(f"sq{ch}_{tt}", "fcst",
                           f"xnc{ch-1}_{tt}" if ch else None),
                    strict=True, mile=f"ms{ch}_{tt}")
                p.add("vector", lambda e, tt=tt: e.reciprocal(
                    out=mcol[:, 4 + tt:5 + tt], in_=mcol[:, tt:tt + 1]),
                    after=(f"ms{ch}_{tt}",), mile=f"rst{ch}_{tt}")
                p.add("scalar", lambda e, tt=tt: e.activation(
                    out=x1b_t[:], in_=rs_tile[:], func=AF.Copy,
                    scale=mcol[:, 4 + tt:5 + tt]),
                    after=tuple([f"rst{ch}_{tt}"] + x1b_free),
                    mile=f"xnc{ch}_{tt}")
                rs_free = [f"xnc{ch}_{tt}"]
                slot, w = ptr_r.acquire()

                def trf(e, tt=tt):
                    for j in range(DCH - 1):
                        e.transpose(out=p_tr[:, j * 128:(j + 1) * 128],
                                    in_=x1b_t[:, j * 128:(j + 1) * 128],
                                    identity=ident[:])
                    return e.transpose(out=p_tr[:, (DCH - 1) * 128:DCH * 128],
                                       in_=x1b_t[:, (DCH - 1) * 128:DCH * 128],
                                       identity=ident[:])
                p.add("tensor", trf,
                      after=tuple([f"xnc{ch}_{tt}", "idI"] + w),
                      mile=f"tr{ch}_{tt}")
                x1b_free = [f"tr{ch}_{tt}"]
                p.add("vector", lambda e, tt=tt: e.tensor_copy(
                    out=xn_feat[:, :, tt * 128:(tt + 1) * 128],
                    in_=p_tr[:].rearrange("p (a b) -> p a b", a=DCH)),
                    after=tuple([f"tr{ch}_{tt}"] + xnfeat_free),
                    mile=f"xnf{ch}_{tt}")
                ptr_r.consumed(f"xnf{ch}_{tt}")

            p.add("sync", lambda e: e.dma_start(
                out=wbuf[:], in_=in_bfd.rearrange("a p b -> p a b")),
                after=tuple(wbuf_free + [f"inS{DCH-1}"]),
                mile=f"inL{ch}", dma=True)
            for ej in range(EJ):
                slot, w = pxr.acquire()

                def inmm(e, ej=ej, slot=slot):
                    for dc in range(DCH - 1):
                        e.matmul(out=p_x[:, slot, :],
                                 lhsT=wbuf[:, dc, ej * 128:(ej + 1) * 128],
                                 rhs=xn_feat[:, dc, :],
                                 start=(dc == 0), stop=False)
                    return e.matmul(out=p_x[:, slot, :],
                                    lhsT=wbuf[:, DCH - 1, ej * 128:(ej + 1) * 128],
                                    rhs=xn_feat[:, DCH - 1, :],
                                    start=False, stop=True)
                p.add("tensor", inmm,
                      after=tuple([f"xnf{ch}_3", f"inL{ch}"] + w),
                      mile=f"Bx{ch}_{ej}")
                xslot, xw = xpre.acquire()
                p.add("scalar", lambda e, slot=slot, xslot=xslot: e.copy(
                    out=xpre_r[:, xslot, 3:3 + C], in_=p_x[:, slot, :]),
                    after=tuple([f"Bx{ch}_{ej}"] + xw), mile=f"evX{ch}_{ej}")
                pxr.consumed(f"evX{ch}_{ej}")
                p.add("vector", lambda e, ej=ej, xslot=xslot: e.tensor_copy(
                    out=xpre_r[:, xslot, 0:3], in_=halo[:, ej, :]),
                    after=(f"evX{ch}_{ej}", halo_v[ej]), mile=f"hin{ch}_{ej}")
                p.add("vector", lambda e, ej=ej, xslot=xslot: e.tensor_copy(
                    out=halo[:, ej, :], in_=xpre_r[:, xslot, C:C + 3]),
                    after=(f"evX{ch}_{ej}",), mile=f"hout{ch}_{ej}")
                halo_v[ej] = f"hout{ch}_{ej}"
                _, cw = pc_r.acquire()

                def convmm(e, ej=ej, xslot=xslot):
                    for k in range(DCV - 1):
                        e.matmul(out=p_c[:], lhsT=cdiag[:, ej * DCV + k, :],
                                 rhs=xpre_r[:, xslot, k:k + C],
                                 start=(k == 0), stop=False)
                    return e.matmul(out=p_c[:],
                                    lhsT=cdiag[:, ej * DCV + DCV - 1, :],
                                    rhs=xpre_r[:, xslot, DCV - 1:DCV - 1 + C],
                                    start=False, stop=True)
                p.add("tensor", convmm,
                      after=tuple([f"evX{ch}_{ej}", f"hin{ch}_{ej}",
                                   f"cd{ej}"] + cw),
                      mile=f"Bc{ch}_{ej}")
                xpre.consumed(f"Bc{ch}_{ej}", f"hout{ch}_{ej}")
                p.add("scalar", lambda e, ej=ej: e.activation(
                    out=x_bf[:, ej, :], in_=p_c[:], func=AF.Silu,
                    bias=convb_sb[:, ej:ej + 1]),
                    after=tuple([f"Bc{ch}_{ej}", "cst2"] + xbf_free[ej]),
                    mile=f"evC{ch}_{ej}")
                pc_r.consumed(f"evC{ch}_{ej}")
                xbf_free[ej] = []

            if not pfx:
                p.add("sync", lambda e: e.dma_start(
                    out=wbuf[:], in_=skip_bfd.rearrange("a p b -> p a b")),
                    after=(f"Bx{ch}_{EJ-1}", f"skS{DCH-1}"),
                    mile=f"skL{ch}", dma=True)
                for ej in range(EJ):
                    slot, w = pxr.acquire()

                    def skmm(e, ej=ej, slot=slot):
                        for dc in range(DCH - 1):
                            e.matmul(out=p_x[:, slot, :],
                                     lhsT=wbuf[:, dc, ej * 128:(ej + 1) * 128],
                                     rhs=xn_feat[:, dc, :],
                                     start=(dc == 0), stop=False)
                        return e.matmul(
                            out=p_x[:, slot, :],
                            lhsT=wbuf[:, DCH - 1, ej * 128:(ej + 1) * 128],
                            rhs=xn_feat[:, DCH - 1, :], start=False, stop=True)
                    p.add("tensor", skmm,
                          after=tuple([f"skL{ch}"] + w), mile=f"Bs{ch}_{ej}")
                    p.add("scalar", lambda e, ej=ej, slot=slot: e.activation(
                        out=skip_s[:, ej, :], in_=p_x[:, slot, :],
                        func=AF.Silu),
                        after=tuple([f"Bs{ch}_{ej}"] + skip_free[ej]),
                        mile=f"evS{ch}_{ej}")
                    pxr.consumed(f"evS{ch}_{ej}")
                    skip_free[ej] = []
                wbuf_free = [f"Bs{ch}_{EJ-1}"]
                xnfeat_free = [f"Bs{ch}_{EJ-1}"]
            else:
                wbuf_free = [f"Bx{ch}_{EJ-1}"]
                xnfeat_free = [f"Bx{ch}_{EJ-1}"]

            if ch == 4:
                p.add("sync", lambda e: e.dma_start(
                    out=x1w[:, 0:EJ * 96].rearrange("p (j k) -> p j k", j=EJ),
                    in_=wcatT.rearrange("(j p) k -> p j k", p=128)),
                    after=(f"Cmm{3}", f"dlt{3}_{EJ-1}"), mile="wcL", dma=True)
                p.add("scalar", lambda e: e.copy(
                    out=wcat_sb[:], in_=x1w[:, 0:EJ * 96].rearrange(
                        "p (j k) -> p j k", j=EJ)),
                    after=("wcL", f"Cmm{3}"), mile="wcC")
            wsb = wcat_sb
            _, bw = pbc_r.acquire()

            def wcmm(e, wsb=wsb):
                for kj in range(EJ - 1):
                    e.matmul(out=p_bc[0:96, :], lhsT=wsb[:, kj, 0:96],
                             rhs=x_bf[:, kj, :], start=(kj == 0), stop=False)
                return e.matmul(out=p_bc[0:96, :], lhsT=wsb[:, EJ - 1, 0:96],
                                rhs=x_bf[:, EJ - 1, :], start=False, stop=True)
            p.add("tensor", wcmm,
                  after=tuple([f"evC{ch}_{ej}" for ej in range(EJ)]
                              + ["wcC" if not pfx else "wcpC"] + bw),
                  mile=f"Cmm{ch}")
            p.add("scalar", lambda e: e.copy(
                out=dlrBC_bf[:], in_=p_bc[0:96, :]),
                after=tuple([f"Cmm{ch}"] + dlr_free), mile=f"dlrC{ch}")
            pbc_r.consumed(f"dlrC{ch}")
            dlr_free = []
            if dbg and ch == 4:
                p.add("sync", lambda e: e.dma_start(
                    out=dbg_t["dbg_dlr"], in_=dlrBC_bf[:]),
                    after=(f"dlrC{ch}",), mile="DBGdlr", dma=True)
                p.add("sync", lambda e: e.dma_start(
                    out=dbg_t["dbg_xbf"], in_=x_bf[:]),
                    after=tuple(f"evC{ch}_{ej}" for ej in range(EJ)),
                    mile="DBGxbf", dma=True)

            if ch == 0:
                bc_last = []
            elif ch <= 4:
                bc_last = [f"t{ch-1}_{EJ-1}_{NS-1}", f"t{ch-1}_{EJ-1}_{4}"]
            else:
                bc_last = [f"u{ch-1}_{EJ-1}_{NS-1}", f"u{ch-1}_{EJ-1}_{4}",
                           f"g{ch-1}_{EJ-1}_{NS-1}", f"g{ch-1}_{EJ-1}_{4}"]
            if stages >= 2:
                for n in range(NS):
                    sslot, sw = psel_r.acquire()
                    p.add("tensor", lambda e, n=n, sslot=sslot: e.matmul(
                        out=p_sel[:, sslot, :], lhsT=selB[:, n, :],
                        rhs=dlrBC_bf[:], start=True, stop=True),
                        after=tuple([f"dlrC{ch}", f"selB{NS-1}"] + sw),
                        mile=f"pB{ch}_{n}")
                    p.add("scalar", lambda e, n=n, sslot=sslot: e.copy(
                        out=Bmb_all[:, n, :], in_=p_sel[:, sslot, :]),
                        after=tuple([f"pB{ch}_{n}"] + bc_last),
                        mile=f"bm{ch}_{n}")
                    psel_r.consumed(f"bm{ch}_{n}")
                    if not pfx:
                        sslot, sw = psel_r.acquire()
                        p.add("tensor", lambda e, n=n, sslot=sslot: e.matmul(
                            out=p_sel[:, sslot, :], lhsT=selC[:, n, :],
                            rhs=dlrBC_bf[:], start=True, stop=True),
                            after=tuple([f"dlrC{ch}", "selCZ"] + sw),
                            mile=f"pC{ch}_{n}")
                        p.add("scalar", lambda e, n=n, sslot=sslot: e.copy(
                            out=Cmb_all[:, n, :], in_=p_sel[:, sslot, :]),
                            after=tuple([f"pC{ch}_{n}"] + bc_last),
                            mile=f"cm{ch}_{n}")
                        psel_r.consumed(f"cm{ch}_{n}")
            if stages < 2:
                dlr_free = [f"dlrC{ch}"]
                for ej in range(EJ):
                    xbf_free[ej] = [f"Cmm{ch}"]
                    skip_free[ej] = [f"evS{ch}_{ej}"] if not pfx else [None]
                continue

            for ej in range(EJ):
                _, bw = pbc_r.acquire()

                def wd2mm(e, ej=ej):
                    return e.matmul(out=p_bc[:],
                                    lhsT=wd2_sb[:, ej * 128:(ej + 1) * 128],
                                    rhs=dlrBC_bf[0:DR, :],
                                    start=True, stop=True)
                p.add("tensor", wd2mm,
                      after=tuple([f"dlrC{ch}", "wd2C"] + bw),
                      mile=f"Dw{ch}_{ej}")
                _, s1w = sp1r.acquire()
                p.add("scalar", lambda e, ej=ej: e.activation(
                    out=sp1, in_=p_bc[:], func=AF.Abs,
                    bias=wd2b_sb[:, ej:ej + 1]),
                    after=tuple([f"Dw{ch}_{ej}", "cst1"] + s1w))
                p.add("scalar", lambda e: e.activation(
                    out=sp1, in_=sp1, func=AF.Exp, scale=-1.0),
                    after=("fcst",))
                p.add("scalar", lambda e: e.activation(
                    out=sp1, in_=sp1, func=AF.Ln, bias=1.0))
                _, s2w = sp2r.acquire()
                p.add("scalar", lambda e, ej=ej: e.activation(
                    out=sp2, in_=p_bc[:], func=AF.Relu,
                    bias=wd2b_sb[:, ej:ej + 1]),
                    after=tuple(s2w), mile=f"spR{ch}_{ej}")
                pbc_r.consumed(f"spR{ch}_{ej}")
                dslot, dw = dlt.acquire()
                p.add("vector", lambda e, dslot=dslot: e.tensor_add(
                    out=delta_r[:, dslot, :], in0=sp1, in1=sp2),
                    after=tuple([f"spR{ch}_{ej}"] + dw), mile=f"dlt{ch}_{ej}")
                sp1r.consumed(f"dlt{ch}_{ej}")
                sp2r.consumed(f"dlt{ch}_{ej}")
                if dbg and ch == 4 and ej < 2:
                    p.add("sync", lambda e, dslot=dslot, ej=ej: e.dma_start(
                        out=dbg_t["dbg_delta"][:, ej, :],
                        in_=delta_r[:, dslot, :]),
                        after=(f"dlt{ch}_{ej}",), mile=f"DBGd{ej}", dma=True)
                    dlt.consumed(f"DBGd{ej}")
                xslot_d, dxw = dxr.acquire()
                p.add("gpsimd", lambda e, dslot=dslot, xslot_d=xslot_d, ej=ej:
                      e.tensor_tensor(out=dx_r[:, xslot_d, :],
                                      in0=delta_r[:, dslot, :],
                                      in1=x_bf[:, ej, :], op=ALU.mult),
                      after=tuple([f"dlt{ch}_{ej}"] + dxw),
                      mile=f"dx{ch}_{ej}")
                xbf_free[ej].append(f"dx{ch}_{ej}")

                if pfx:
                    aslot, aw = ar.acquire()
                    p.add("vector", lambda e, dslot=dslot, aslot=aslot:
                          e.tensor_tensor_scan(
                              out=a_r[:, aslot, :], data0=ones_sb[:],
                              data1=delta_r[:, dslot, :], initial=0.0,
                              op0=ALU.mult, op1=ALU.add),
                          after=tuple([f"dlt{ch}_{ej}", "ones"] + aw),
                          mile=f"S{ch}_{ej}")
                    dlt.consumed(f"S{ch}_{ej}", f"dx{ch}_{ej}")
                    mslot, mw = dmr.acquire()
                    p.add("gpsimd", lambda e, aslot=aslot, mslot=mslot:
                          e.tensor_scalar(
                              out=Dm_r[:, mslot, :], in0=a_r[:, aslot, :],
                              scalar1=a_r[:, aslot, C - 1:C], scalar2=None,
                              op0=ALU.subtract),
                          after=tuple([f"S{ch}_{ej}"] + mw),
                          mile=f"Dm{ch}_{ej}")
                    for n in range(NS):
                        uslot, uw = ur.acquire()
                        p.add("scalar", lambda e, mslot=mslot, uslot=uslot,
                              ej=ej, n=n: e.activation(
                                  out=u_r[:, uslot, :],
                                  in_=Dm_r[:, mslot, :], func=AF.Exp,
                                  scale=negA_sb[:, ej, n:n + 1]),
                              after=tuple([f"Dm{ch}_{ej}", "A_"] + uw),
                              mile=f"wx{ch}_{ej}_{n}")
                        gslot, gw = gr.acquire()
                        p.add("gpsimd" if n < 5 else "vector",
                              lambda e, uslot=uslot, n=n,
                              gslot=gslot: e.tensor_tensor(
                                  out=g_r[:, gslot, :],
                                  in0=u_r[:, uslot, :],
                                  in1=Bmb_all[:, n, :], op=ALU.mult),
                              after=tuple([f"wx{ch}_{ej}_{n}",
                                           f"bm{ch}_{n}"] + gw),
                              mile=f"t{ch}_{ej}_{n}")
                        ur.consumed(f"t{ch}_{ej}_{n}")
                        p.add("vector",
                              lambda e, gslot=gslot, xslot_d=xslot_d,
                              n=n: e.scalar_tensor_tensor(
                                  out=gdead, in0=g_r[:, gslot, :],
                                  scalar=1.0, in1=dx_r[:, xslot_d, :],
                                  op0=ALU.mult, op1=ALU.mult,
                                  accum_out=Gcols[:, n:n + 1]),
                              after=tuple([f"t{ch}_{ej}_{n}", f"dx{ch}_{ej}"]
                                          + (gcols_free if n == 0 else [])),
                              mile=f"sg{ch}_{ej}_{n}")
                        gr.consumed(f"sg{ch}_{ej}_{n}")
                    dxr.consumed(f"sg{ch}_{ej}_{NS-1}")
                    dmr.consumed(f"wx{ch}_{ej}_{NS-1}")
                    p.add("scalar", lambda e, ej=ej, aslot=aslot:
                          e.activation(out=Pfac[:], in_=A_sb[:, ej, :],
                                       func=AF.Exp,
                                       scale=a_r[:, aslot, C - 1:C]),
                          after=(f"S{ch}_{ej}", "A_",
                                 f"hup{ch}_{ej-1}" if ej else
                                 (f"hup{ch-1}_{EJ-1}" if ch else None)),
                          mile=f"pf{ch}_{ej}")
                    ar.consumed(f"Dm{ch}_{ej}", f"pf{ch}_{ej}")
                    p.add("vector", lambda e, ej=ej: e.tensor_mul(
                        out=hcar[:, ej * NS:(ej + 1) * NS],
                        in0=hcar[:, ej * NS:(ej + 1) * NS], in1=Pfac[:]),
                        after=(f"pf{ch}_{ej}", "hcar0")
                              + ((hcar_mile[ej],) if hcar_mile[ej] else ()),
                        strict=True, mile=f"hu1{ch}_{ej}")
                    p.add("vector", lambda e, ej=ej: e.tensor_add(
                        out=hcar[:, ej * NS:(ej + 1) * NS],
                        in0=hcar[:, ej * NS:(ej + 1) * NS], in1=Gcols[:]),
                        after=(f"sg{ch}_{ej}_{NS-1}", f"hu1{ch}_{ej}"),
                        strict=True, mile=f"hup{ch}_{ej}")
                    hcar_mile[ej] = f"hup{ch}_{ej}"
                    gcols_free = [f"hup{ch}_{ej}"]
                    if ej == EJ - 1:
                        x1b_free = [f"sg{ch}_{ej}_{NS-1}"] + x1b_free
                else:
                    for n in range(NS):
                        aslot, aw = ar.acquire()
                        p.add("scalar", lambda e, dslot=dslot, aslot=aslot,
                              ej=ej, n=n: e.activation(
                                  out=a_r[:, aslot, :],
                                  in_=delta_r[:, dslot, :], func=AF.Exp,
                                  scale=A_sb[:, ej, n:n + 1]),
                              after=tuple([f"dlt{ch}_{ej}", "A_"] + aw),
                              mile=f"ax{ch}_{ej}_{n}")
                        uslot, uw = ur.acquire()
                        p.add("gpsimd" if n >= 11 else "vector",
                              lambda e, xslot_d=xslot_d, n=n,
                              uslot=uslot: e.tensor_tensor(
                                  out=u_r[:, uslot, :],
                                  in0=dx_r[:, xslot_d, :],
                                  in1=Bmb_all[:, n, :], op=ALU.mult),
                              after=tuple([f"dx{ch}_{ej}",
                                           f"bm{ch}_{n}"] + uw),
                              mile=f"u{ch}_{ej}_{n}")
                        hslot, hw = hr.acquire()
                        p.add("vector", lambda e, aslot=aslot, uslot=uslot,
                              hslot=hslot, ej=ej, n=n: e.tensor_tensor_scan(
                                  out=h_ring[:, hslot, :],
                                  data0=a_r[:, aslot, :],
                                  data1=u_r[:, uslot, :],
                                  initial=hcar[:, ej * NS + n:ej * NS + n + 1],
                                  op0=ALU.mult, op1=ALU.add),
                              after=tuple([f"ax{ch}_{ej}_{n}",
                                           f"u{ch}_{ej}_{n}"] + hw
                                          + ([hcar_mile[ej]]
                                             if n == 0 and hcar_mile[ej]
                                             else [])),
                              strict=(n == 0), mile=f"sc{ch}_{ej}_{n}")
                        ar.consumed(f"sc{ch}_{ej}_{n}")
                        ur.consumed(f"sc{ch}_{ej}_{n}")
                        if n == 7 or n == NS - 1:
                            base = ej * NS + (0 if n == 7 else 8)
                            p.add("vector", lambda e, base=base: e.tensor_copy(
                                out=hcar[:, base:base + 8],
                                in_=h_ring[:, 0:8, C - 1:C]),
                                after=(f"sc{ch}_{ej}_{n}",), strict=True,
                                mile=f"cy{ch}_{ej}_{n}")
                            if n == NS - 1:
                                hcar_mile[ej] = f"cy{ch}_{ej}_{n}"
                        gslot, gw = gr.acquire()
                        p.add("gpsimd" if n < 5 else "vector",
                              lambda e, hslot=hslot, n=n,
                              gslot=gslot: e.tensor_tensor(
                                  out=g_r[:, gslot, :],
                                  in0=h_ring[:, hslot, :],
                                  in1=Cmb_all[:, n, :], op=ALU.mult),
                              after=tuple([f"sc{ch}_{ej}_{n}",
                                           f"cm{ch}_{n}"] + gw),
                              mile=f"g{ch}_{ej}_{n}")
                        hr.consumed(f"g{ch}_{ej}_{n}",
                                    f"cy{ch}_{ej}_{7 if n <= 7 else NS-1}"
                                    if (n == 7 or n == NS - 1) else None)
                        _, yw = (py_r.acquire() if n == 0 else (None, []))
                        p.add("tensor", lambda e, gslot=gslot, n=n: e.matmul(
                            out=p_y[:], lhsT=ident[:], rhs=g_r[:, gslot, :],
                            start=(n == 0), stop=False),
                            after=tuple([f"g{ch}_{ej}_{n}"] + yw),
                            mile=f"ya{ch}_{ej}_{n}")
                        gr.consumed(f"ya{ch}_{ej}_{n}")
                    dxr.consumed(f"u{ch}_{ej}_{NS-1}")
                    dlt.consumed(f"ax{ch}_{ej}_{NS-1}", f"dx{ch}_{ej}")
                    p.add("tensor", lambda e, ej=ej: e.matmul(
                        out=p_y[:], lhsT=wddiag[:, ej, :],
                        rhs=x_bf[:, ej, :], start=False, stop=True),
                        after=(f"wdd{ej}", f"evC{ch}_{ej}"),
                        mile=f"ywd{ch}_{ej}")
                    xbf_free[ej].append(f"ywd{ch}_{ej}")
                    p.add("vector", lambda e, ej=ej: e.tensor_tensor(
                        out=y_bf[:, ej, :], in0=p_y[:],
                        in1=skip_s[:, ej, :], op=ALU.mult),
                        after=tuple([f"ywd{ch}_{ej}", f"evS{ch}_{ej}"]
                                    + ybf_free[ej]),
                        mile=f"yev{ch}_{ej}")
                    py_r.consumed(f"yev{ch}_{ej}")
                    skip_free[ej] = [f"yev{ch}_{ej}"]
                    ybf_free[ej] = []
            dlr_free = ([f"pB{ch}_{NS-1}", f"Dw{ch}_{EJ-1}"]
                        if pfx else [f"ywd{ch}_{EJ-1}"])
            for _ej in range(EJ):
                xbf_free[_ej].append(f"Cmm{ch}")
            if dbg and ch == 3 and stages >= 2:
                p.add("sync", lambda e: e.dma_start(
                    out=dbg_t["dbg_hcar"], in_=hcar[:]),
                    after=(f"hup{ch}_{EJ-1}",), mile="DBGh", dma=True)
            if dbg and ch == 4 and stages >= 2:
                p.add("sync", lambda e: e.dma_start(
                    out=dbg_t["dbg_y"], in_=y_bf[:]),
                    after=tuple(f"yev{ch}_{ej}" for ej in range(EJ)),
                    mile="DBGy", dma=True)

            if pfx or stages < 3:
                if stages < 3 and not pfx:
                    for ej in range(EJ):
                        ybf_free[ej] = [f"yev{ch}_{ej}"]
                continue

            p.add("sync", lambda e: e.dma_start(
                out=wbuf[:], in_=out_bfd.rearrange("a p b -> p a b")),
                after=tuple(wbuf_free + [f"oS{EJ-1}"]),
                mile=f"oLd{ch}", dma=True)
            wbuf_free = []
            for lt in range(4):
                slot0, w0 = pxr.acquire()
                slot1, w1 = pxr.acquire()
                assert slot0 == 0 and slot1 == 1

                def outmm(e, lt=lt):
                    last = None
                    for half in range(2):
                        for ec in range(EJ):
                            last = e.matmul(
                                out=p_x[:, half, :],
                                lhsT=y_bf[:, ec, lt * 128:(lt + 1) * 128],
                                rhs=wbuf_o[:, ec, half * 512:(half + 1) * 512],
                                start=(ec == 0), stop=(ec == EJ - 1))
                    return last
                p.add("tensor", outmm,
                      after=tuple([f"yev{ch}_{EJ-1}", f"oLd{ch}"] + w0 + w1),
                      mile=f"Fmm{ch}_{lt}")
                p.add("sync", lambda e, lt=lt, t0=t0: e.dma_start(
                    out=rs_tile[:],
                    in_=resid_own[t0 + lt * 128:t0 + (lt + 1) * 128, :]),
                    after=tuple(rs_free) + ((f"dlt{ch}_{EJ-1}",)
                                            if lt == 0 else ()),
                    mile=f"rs2{ch}_{lt}", dma=True)
                p.add("vector", lambda e: e.scalar_tensor_tensor(
                    out=x1w[:, 0:D], in0=rs_tile[:], scalar=2.0,
                    in1=p_x[:].rearrange("p a b -> p (a b)"),
                    op0=ALU.mult, op1=ALU.add),
                    after=(f"Fmm{ch}_{lt}", f"rs2{ch}_{lt}"),
                    mile=f"x1p{ch}_{lt}")
                pxr.consumed(f"x1p{ch}_{lt}")
                pxr.consumed(f"x1p{ch}_{lt}")
                rs_free = [f"x1p{ch}_{lt}"]
                if dbg and ch == 4 and lt == 0:
                    p.add("sync", lambda e: e.dma_start(
                        out=dbg_t["dbg_x1p"], in_=x1w[:, 0:D]),
                        after=(f"x1p{ch}_{lt}",), mile="DBGxp", dma=True)
                p.add("vector", lambda e: e.reduce_sum(
                    out=fmca[:, 0:1], in_=x1w[:, 0:D], axis=AX.X),
                    mile=f"msm{ch}_{lt}")
                p.add("vector", lambda e: e.tensor_scalar(
                    out=fmca[:, 1:2], in0=fmca[:, 0:1], scalar1=INV_D,
                    scalar2=None, op0=ALU.mult),
                    after=(f"msm{ch}_{lt}",), strict=True,
                    mile=f"m2{ch}_{lt}")
                p.add("vector", lambda e: e.tensor_scalar(
                    out=x1w[:, 0:D], in0=x1w[:, 0:D],
                    scalar1=fmca[:, 1:2], scalar2=None, op0=ALU.subtract),
                    after=(f"m2{ch}_{lt}",) + (("DBGxp",)
                          if (dbg and ch == 4 and lt == 0) else ()),
                    strict=True,
                    mile=f"ctr{ch}_{lt}")
                p.add("scalar", lambda e: e.activation(
                    out=sq_scr[:], in_=x1w[:, 0:D], func=AF.Square,
                    accum_out=fmca[:, 2:3]),
                    after=(f"ctr{ch}_{lt}",), mile=f"vq{ch}_{lt}")
                p.add("scalar", lambda e: e.activation(
                    out=fmca[:, 3:4], in_=fmca[:, 2:3], func=AF.Sqrt,
                    scale=INV_D, bias=EPS),
                    after=(f"vq{ch}_{lt}",), strict=True, mile=f"vs{ch}_{lt}")
                p.add("vector", lambda e: e.reciprocal(
                    out=fmca[:, 2:3], in_=fmca[:, 3:4]),
                    after=(f"vs{ch}_{lt}",), mile=f"rc{ch}_{lt}")
                p.add("vector", lambda e: e.tensor_scalar(
                    out=x1b_t[:], in0=x1w[:, 0:D], scalar1=fmca[:, 2:3],
                    scalar2=None, op0=ALU.mult),
                    after=tuple([f"rc{ch}_{lt}"] + x1b_free), strict=True,
                    mile=f"x1c{ch}_{lt}")
                if dbg and ch == 4 and lt == 0:
                    p.add("sync", lambda e: e.dma_start(
                        out=dbg_t["dbg_st"], in_=fmca[:]),
                        after=(f"x1c{ch}_{lt}",), mile="DBGst", dma=True)
                p.add("sync", lambda e, lt=lt, t0=t0: e.dma_start(
                    out=x1td[t0 + lt * 128:t0 + (lt + 1) * 128, :],
                    in_=x1b_t[:]),
                    after=(f"x1c{ch}_{lt}",), mile=f"x1S{ch}_{lt}", dma=True)
                if dbg:
                    p.add("sync", lambda e, lt=lt, t0=t0: e.dma_start(
                        out=dbg_t["dbg_x1"][t0 + lt * 128:t0 + (lt + 1) * 128, :],
                        in_=x1b_t[:]),
                        after=(f"x1c{ch}_{lt}",), mile=f"DBGx1{ch}_{lt}",
                        dma=True)
                x1b_free = ([f"x1S{ch}_{lt}"]
                            + ([f"DBGx1{ch}_{lt}"] if dbg else []))
            for ej in range(EJ):
                ybf_free[ej] = [f"Fmm{ch}_3"]
            wbuf_free = [f"Fmm{ch}_3"]

        if stages >= 4:
            attn.close()
            p.barrier_counts()

            def sbF(name, shape, dtype):
                return top.enter_context(nc.sbuf_tensor(name, shape, dtype))

            x1f = sbF("x1f", [128, DCH, TOK], BF16)
            out_acc = sbF("out_acc", [128, 16, D], F32)
            land = sbF("land", [128, DCH * FI], F32)
            gwb = sbF("gwb", [128, 2, DCH, FI], BF16)
            uwb = sbF("uwb", [128, 2, DCH, FI], BF16)
            dwb = sbF("dwb", [128, 2, 4, D], BF16)
            acts = sbF("acts", [128, 2, 4, 512], BF16)
            gc_r = sbF("gc_r", [128, 2, 512], BF16)
            sg_r = sbF("sg_r", [128, 2, 512], BF16)
            up1_r = sbF("up1_r", [128, 2, 512], BF16)
            up2_r = sbF("up2_r", [128, 2, 512], BF16)
            glu_r = sbF("glu_r", [128, 2, 512], BF16)
            gc_bf = sbF("gc_bf", [128, 512], BF16)
            sg_bf = sbF("sg_bf", [128, 512], BF16)
            up1_bf = sbF("up1_bf", [128, 512], BF16)
            up2_bf = sbF("up2_bf", [128, 512], BF16)
            glu_bf = sbF("glu_bf", [128, 512], BF16)
            x1tok = sbF("x1tok", [128, D], BF16)
            lgt = sbF("lgt", [128, EXP], F32)
            fm1 = sbF("fm1", [128, EXP], F32)
            fl2 = sbF("fl2", [128, EXP], F32)
            fmk2 = sbF("fmk2", [128, EXP], F32)
            scores = sbF("scores", [128, 16, EXP], F32)
            sc_bf = sbF("sc_bf", [128, EXP], BF16)
            fcol = sbF("fcol", [128, 8], F32)
            srT = sbF("srT", [16, TOK], BF16)
            identF = sbF("identF", [128, 128], BF16)
            router_bf = sbF("router_bf", [128, DCH, EXP], BF16)
            downb_bf = sbF("downb_bf", [16, D], BF16)
            gb_sb = sbF("gb_sb", [128, EXP, 4], F32)
            ub_sb = sbF("ub_sb", [128, EXP, 4], F32)
            flw = sbF("flw", [128, D], F32)
            fmc = sbF("fmc", [128, 4], F32)
            fsq = sbF("fsq", [128, 512], F32)

            p_ftr = top.enter_context(nc.psum_tensor("p_ftr", [128, 1024], BF16))
            p_ftr2 = top.enter_context(nc.psum_tensor("p_ftr2", [128, 512], F32))
            p_fg = top.enter_context(nc.psum_tensor("p_fg", [128, 2, 512], F32))
            p_fu = top.enter_context(nc.psum_tensor("p_fu", [128, 2, 512], F32))
            p_fd = top.enter_context(nc.psum_tensor("p_fd", [128, 1024], F32))


            p.add("gpsimd", lambda e: e.memset(identF[:], 0.0))
            p.add("gpsimd", lambda e: e.affine_select(
                out=identF[:], in_=identF[:], compare_op=ALU.not_equal,
                fill=1.0, base=0, pattern=[[-1, 128]], channel_multiplier=1),
                mile="FidI")
            p.add("sync", lambda e: e.dma_start(
                out=flw[:, 0:DCH * EXP].rearrange("p (a b) -> p a b", a=DCH),
                in_=router_d.rearrange("(a p) b -> p a b", p=128)),
                mile="rLd", dma=True)
            p.add("scalar", lambda e: e.copy(
                out=router_bf[:],
                in_=flw[:, 0:DCH * EXP].rearrange("p (a b) -> p a b", a=DCH)),
                after=("rLd",), mile="rC")
            p.add("sync", lambda e: e.dma_start(
                out=flw[0:16, 0:D], in_=downb_d), after=("rC",),
                mile="dbLd", dma=True)
            p.add("scalar", lambda e: e.copy(
                out=downb_bf[:], in_=flw[0:16, 0:D]), after=("dbLd",),
                mile="dbC")
            p.add("sync", lambda e: e.dma_start(out=gb_sb[:], in_=gateb_d),
                  mile="gbL", dma=True)
            p.add("sync", lambda e: e.dma_start(out=ub_sb[:], in_=upb_d),
                  mile="ubL", dma=True)

            x1tok_free = ["dbC"]
            for tt in range(16):
                p.add("sync", lambda e, tt=tt: e.dma_start(
                    out=x1tok[:], in_=x1td[tt * 128:(tt + 1) * 128, :]),
                    after=tuple(x1tok_free), mile=f"fxL{tt}", dma=True)

                def ftr(e, tt=tt):
                    for j in range(DCH - 1):
                        e.transpose(out=p_ftr[:, j * 128:(j + 1) * 128],
                                    in_=x1tok[:, j * 128:(j + 1) * 128],
                                    identity=identF[:])
                    return e.transpose(
                        out=p_ftr[:, (DCH - 1) * 128:DCH * 128],
                        in_=x1tok[:, (DCH - 1) * 128:DCH * 128],
                        identity=identF[:])
                p.add("tensor", ftr,
                      after=(f"fxL{tt}", "FidI",
                             f"fxE{tt-1}" if tt else None),
                      mile=f"fxT{tt}")
                x1tok_free = [f"fxT{tt}"]
                p.add("vector", lambda e, tt=tt: e.tensor_copy(
                    out=x1f[:, :, tt * 128:(tt + 1) * 128],
                    in_=p_ftr[:].rearrange("p (a b) -> p a b", a=DCH)),
                    after=(f"fxT{tt}",), mile=f"fxE{tt}")

            for tt in range(16):
                def rmm(e, tt=tt):
                    for dc in range(DCH - 1):
                        e.matmul(out=p_ftr2[:, 0:EXP],
                                 lhsT=x1f[:, dc, tt * 128:(tt + 1) * 128],
                                 rhs=router_bf[:, dc, :],
                                 start=(dc == 0), stop=False)
                    return e.matmul(
                        out=p_ftr2[:, 0:EXP],
                        lhsT=x1f[:, DCH - 1, tt * 128:(tt + 1) * 128],
                        rhs=router_bf[:, DCH - 1, :], start=False, stop=True)
                p.add("tensor", rmm,
                      after=(f"fxE{tt}", "rC",
                             f"rcp{tt-1}" if tt else None),
                      mile=f"rmm{tt}")
                p.add("vector", lambda e: e.tensor_copy(
                    out=lgt[:], in_=p_ftr2[:, 0:EXP]),
                    after=(f"rmm{tt}",), mile=f"rcp{tt}")
                p.add("vector", lambda e: e.reduce_max(
                    out=fcol[:, 0:1], in_=lgt[:], axis=AX.X),
                    after=(f"rcp{tt}",), strict=True, mile=f"rm1{tt}")
                p.add("vector", lambda e: e.tensor_scalar(
                    out=fm1[:], in0=lgt[:], scalar1=fcol[:, 0:1],
                    scalar2=None, op0=ALU.is_equal),
                    after=(f"rm1{tt}",), strict=True, mile=f"fm1m{tt}")

                p.add("vector", lambda e: e.scalar_tensor_tensor(
                    out=fl2[:], in0=fm1[:], scalar=-1e30, in1=lgt[:],
                    op0=ALU.mult, op1=ALU.add),
                    after=(f"fm1m{tt}",), strict=True, mile=f"fl2m{tt}")
                p.add("vector", lambda e: e.reduce_max(
                    out=fcol[:, 1:2], in_=fl2[:], axis=AX.X),
                    after=(f"fl2m{tt}",), strict=True, mile=f"rm2{tt}")
                p.add("vector", lambda e: e.tensor_scalar(
                    out=fmk2[:], in0=fl2[:], scalar1=fcol[:, 1:2],
                    scalar2=None, op0=ALU.is_equal),
                    after=(f"rm2{tt}",), strict=True, mile=f"mk2{tt}")
                p.add("vector", lambda e: e.tensor_scalar(
                    out=fcol[:, 2:3], in0=fcol[:, 1:2],
                    scalar1=fcol[:, 0:1], scalar2=None, op0=ALU.subtract),
                    after=(f"rm2{tt}",), strict=True, mile=f"d21{tt}")
                p.add("scalar", lambda e: e.activation(
                    out=fcol[:, 3:4], in_=fcol[:, 2:3], func=AF.Sigmoid),
                    after=(f"d21{tt}",), mile=f"w2{tt}")
                p.add("vector", lambda e: e.tensor_scalar(
                    out=fcol[:, 4:5], in0=fcol[:, 3:4], scalar1=-1.0,
                    scalar2=1.0, op0=ALU.mult, op1=ALU.add),
                    after=(f"w2{tt}", f"mk2{tt}"), strict=True,
                    mile=f"w1{tt}")
                p.add("vector", lambda e, tt=tt: e.tensor_scalar(
                    out=scores[:, tt, :], in0=fm1[:],
                    scalar1=fcol[:, 4:5], scalar2=None, op0=ALU.mult),
                    after=(f"w1{tt}",), strict=True, mile=f"sa{tt}")
                p.add("vector", lambda e, tt=tt: e.scalar_tensor_tensor(
                    out=scores[:, tt, :], in0=fmk2[:],
                    scalar=fcol[:, 3:4], in1=scores[:, tt, :],
                    op0=ALU.mult, op1=ALU.add),
                    after=(f"sa{tt}", f"mk2{tt}"), strict=True,
                    mile=f"scs{tt}")
                p.add("vector", lambda e, tt=tt: e.tensor_copy(
                    out=sc_bf[:], in_=scores[:, tt, :]),
                    after=(f"scs{tt}", f"sTe{tt-1}" if tt else None),
                    strict=True, mile=f"scb{tt}")
                p.add("tensor", lambda e: e.transpose(
                    out=p_ftr[0:EXP, 0:128], in_=sc_bf[:], identity=identF[:]),
                    after=(f"scb{tt}", f"fxE{15}",
                           f"sTe{tt-1}" if tt else None),
                    mile=f"sT{tt}")
                p.add("scalar", lambda e, tt=tt: e.copy(
                    out=srT[:, tt * 128:(tt + 1) * 128],
                    in_=p_ftr[0:EXP, 0:128]),
                    after=(f"sT{tt}",), mile=f"sTe{tt}")
                if dbg and tt == 0:
                    p.add("sync", lambda e: e.dma_start(
                        out=dbg_t["dbg_scores"], in_=scores[:, 0, :]),
                        after=("scs0",), mile="DBGsc", dma=True)
                    p.add("sync", lambda e: e.dma_start(
                        out=dbg_t["dbg_rt"][:, 0:16], in_=lgt[:]),
                        after=("scs0",), mile="DBGr1", dma=True)
                    p.add("sync", lambda e: e.dma_start(
                        out=dbg_t["dbg_rt"][:, 16:32], in_=fmk2[:]),
                        after=("scs0",), mile="DBGr2", dma=True)
                    p.add("sync", lambda e: e.dma_start(
                        out=dbg_t["dbg_rt"][:, 32:48], in_=fm1[:]),
                        after=("scs0",), mile="DBGr3", dma=True)
                    p.add("sync", lambda e: e.dma_start(
                        out=dbg_t["dbg_rt"][:, 48:56], in_=fcol[:]),
                        after=("scs0",), mile="DBGr4", dma=True)
                    p.add("vector", lambda e: e.engine_nop(),
                          after=("DBGr1", "DBGr2", "DBGr3", "DBGr4"),
                          mile="DBGrwait")

            for tt in range(16):
                def bmm(e, tt=tt):
                    e.matmul(out=p_fd[:, 0:512],
                             lhsT=srT[:, tt * 128:(tt + 1) * 128],
                             rhs=downb_bf[:, 0:512], start=True, stop=True)
                    return e.matmul(out=p_fd[:, 512:1024],
                                    lhsT=srT[:, tt * 128:(tt + 1) * 128],
                                    rhs=downb_bf[:, 512:1024],
                                    start=True, stop=True)
                p.add("tensor", bmm,
                      after=(f"sTe{tt}", "dbC", f"oaI{tt-1}" if tt else None),
                      mile=f"bmm{tt}")
                p.add("sync", lambda e, tt=tt: e.dma_start(
                    out=x1tok[:], in_=x1td[tt * 128:(tt + 1) * 128, :]),
                    after=tuple([f"bmm{tt}"] + x1tok_free),
                    mile=f"fx2L{tt}", dma=True)
                p.add("vector", lambda e, tt=tt: e.tensor_add(
                    out=out_acc[:, tt, :],
                    in0=p_fd[:], in1=x1tok[:]),
                    after=(f"bmm{tt}", f"fx2L{tt}"), mile=f"oaI{tt}")
                if dbg and tt == 0:
                    p.add("sync", lambda e: e.dma_start(
                        out=dbg_t["dbg_oa0"], in_=out_acc[:, 0, :]),
                        after=("oaI0",), mile="DBGoa0", dma=True)
                x1tok_free = [f"oaI{tt}"]

            pfg_r = Ring(2)
            pfu_r = Ring(2)
            gcr = Ring(2)
            sgr = Ring(2)
            u1r = Ring(2)
            u2r = Ring(2)
            glr = Ring(2)
            actr = Ring(2)
            acts_slot = {}
            prevload = "ubL"
            for ex in range(EXP):
                es = ex % 2
                p.add("sync", lambda e, ex=ex: e.dma_start(
                    out=land[:].rearrange("p (a b) -> p a b", a=DCH),
                    in_=gatew_d[ex].rearrange("(a p) b -> p a b", p=128)),
                    after=(prevload, f"gmmE{ex-2}" if ex >= 2 else None),
                    mile=f"gL{ex}", dma=True)
                p.add("scalar", lambda e, es=es: e.copy(
                    out=gwb[:, es, :, :],
                    in_=land[:].rearrange("p (a b) -> p a b", a=DCH)),
                    after=(f"gL{ex}",), mile=f"gC{ex}")
                p.add("sync", lambda e, ex=ex: e.dma_start(
                    out=land[:].rearrange("p (a b) -> p a b", a=DCH),
                    in_=upw_d[ex].rearrange("(a p) b -> p a b", p=128)),
                    after=(f"gC{ex}", f"ummE{ex-2}" if ex >= 2 else None),
                    mile=f"uL{ex}", dma=True)
                p.add("scalar", lambda e, es=es: e.copy(
                    out=uwb[:, es, :, :],
                    in_=land[:].rearrange("p (a b) -> p a b", a=DCH)),
                    after=(f"uL{ex}",), mile=f"uC{ex}")
                p.add("sync", lambda e, ex=ex: e.dma_start(
                    out=land[:, 0:4 * D].rearrange("p (a b) -> p a b", a=4),
                    in_=downw_d[ex].rearrange("(a p) b -> p a b", p=128)),
                    after=(f"uC{ex}", f"dmmE{ex-2}" if ex >= 2 else None),
                    mile=f"dL{ex}", dma=True)
                p.add("scalar", lambda e, es=es: e.copy(
                    out=dwb[:, es, :, :],
                    in_=land[:, 0:4 * D].rearrange("p (a b) -> p a b", a=4)),
                    after=(f"dL{ex}",), mile=f"dC{ex}")
                prevload = f"dC{ex}"
                for tsb in range(4):
                    acs_, acw_ = actr.acquire()
                    acts_slot[(ex, tsb)] = acs_
                    for fc in range(4):
                        gk = f"{ex}_{tsb}_{fc}"
                        gslot_, gw_ = pfg_r.acquire()

                        def gmm(e, es=es, tsb=tsb, fc=fc, s=gslot_):
                            last = None
                            for dc in range(DCH):
                                last = e.matmul(
                                    out=p_fg[:, s, :],
                                    lhsT=gwb[:, es, dc, fc * 128:(fc + 1) * 128],
                                    rhs=x1f[:, dc, tsb * 512:(tsb + 1) * 512],
                                    start=(dc == 0), stop=(dc == DCH - 1))
                            return last
                        p.add("tensor", gmm,
                              after=tuple([f"gC{ex}", f"fxE{15}"] + gw_),
                              mile=f"gmm{gk}" + ("E" if tsb == 3 and fc == 3
                                                 else ""))
                        gmm_m = f"gmm{gk}" + ("E" if tsb == 3 and fc == 3 else "")
                        gcs_, gcw_ = gcr.acquire()
                        p.add("vector", lambda e, ex=ex, fc=fc, s=gslot_,
                              o=gcs_: e.tensor_scalar(
                                  out=gc_r[:, o, :], in0=p_fg[:, s, :],
                                  scalar1=gb_sb[:, ex, fc:fc + 1],
                                  scalar2=7.0, op0=ALU.add, op1=ALU.min),
                              after=tuple([gmm_m, "gbL"] + gcw_),
                              mile=f"gev{gk}")
                        pfg_r.consumed(f"gev{gk}")
                        sgs_, sgw_ = sgr.acquire()
                        p.add("scalar", lambda e, i=gcs_, o=sgs_: e.activation(
                            out=sg_r[:, o, :], in_=gc_r[:, i, :],
                            func=AF.Sigmoid, scale=1.702),
                            after=tuple([f"gev{gk}"] + sgw_),
                            mile=f"sig{gk}")
                        uslot_, uw_ = pfu_r.acquire()

                        def umm(e, es=es, tsb=tsb, fc=fc, s=uslot_):
                            last = None
                            for dc in range(DCH):
                                last = e.matmul(
                                    out=p_fu[:, s, :],
                                    lhsT=uwb[:, es, dc, fc * 128:(fc + 1) * 128],
                                    rhs=x1f[:, dc, tsb * 512:(tsb + 1) * 512],
                                    start=(dc == 0), stop=(dc == DCH - 1))
                            return last
                        p.add("tensor", umm,
                              after=tuple([f"uC{ex}"] + uw_),
                              mile=f"umm{gk}" + ("E" if tsb == 3 and fc == 3
                                                 else ""))
                        umm_m = f"umm{gk}" + ("E" if tsb == 3 and fc == 3 else "")
                        u1s_, u1w_ = u1r.acquire()
                        p.add("vector", lambda e, ex=ex, fc=fc, s=uslot_,
                              o=u1s_: e.tensor_scalar(
                                  out=up1_r[:, o, :], in0=p_fu[:, s, :],
                                  scalar1=ub_sb[:, ex, fc:fc + 1],
                                  scalar2=7.0, op0=ALU.add, op1=ALU.min),
                              after=tuple([umm_m, "ubL"] + u1w_),
                              mile=f"uev{gk}")
                        pfu_r.consumed(f"uev{gk}")
                        u2s_, u2w_ = u2r.acquire()
                        p.add("vector", lambda e, i=u1s_, o=u2s_: e.tensor_scalar(
                            out=up2_r[:, o, :], in0=up1_r[:, i, :],
                            scalar1=-7.0, scalar2=1.0, op0=ALU.max,
                            op1=ALU.add),
                            after=tuple([f"uev{gk}"] + u2w_),
                            mile=f"up2{gk}")
                        u1r.consumed(f"up2{gk}")
                        gls_, glw_ = glr.acquire()
                        p.add("vector", lambda e, i1=gcs_, i2=sgs_, o=gls_:
                              e.tensor_tensor(
                                  out=glu_r[:, o, :], in0=gc_r[:, i1, :],
                                  in1=sg_r[:, i2, :], op=ALU.mult),
                              after=tuple([f"sig{gk}", f"gev{gk}"] + glw_),
                              mile=f"glu{gk}")
                        gcr.consumed(f"glu{gk}")
                        sgr.consumed(f"glu{gk}")
                        p.add("vector", lambda e, i1=u2s_, i2=gls_, o=acs_,
                              fc=fc: e.tensor_tensor(
                                  out=acts[:, o, fc, :],
                                  in0=up2_r[:, i1, :],
                                  in1=glu_r[:, i2, :], op=ALU.mult),
                              after=tuple(
                                  [f"up2{gk}", f"glu{gk}",
                                   "DBGact" if dbg and ex == 0 and tsb == 1
                                   and fc == 0 else None]
                                  + (acw_ if fc == 0 else [])),
                              mile=f"act{gk}")
                        u2r.consumed(f"act{gk}")
                        glr.consumed(f"act{gk}")
                    if dbg and ex == 0 and tsb == 0:
                        p.add("sync", lambda e: e.dma_start(
                            out=dbg_t["dbg_act"],
                            in_=acts[:, acts_slot[(0, 0)], :, :]),
                            after=("act0_0_3",), mile="DBGact", dma=True)
                    for tt2 in range(4):
                        aslot2 = acts_slot[(ex, tsb)]

                        def dmm(e, es=es, tsb=tsb, tt2=tt2, s=aslot2):
                            last = None
                            for half in range(2):
                                for ic in range(4):
                                    last = e.matmul(
                                        out=p_fd[:, half * 512:(half + 1) * 512],
                                        lhsT=acts[:, s, ic,
                                                  tt2 * 128:(tt2 + 1) * 128],
                                        rhs=dwb[:, es, ic,
                                                half * 512:(half + 1) * 512],
                                        start=(ic == 0), stop=(ic == 3))
                            return last
                        dk = f"{ex}_{tsb}_{tt2}"
                        p.add("tensor", dmm,
                              after=(f"act{ex}_{tsb}_3", f"dC{ex}",
                                     f"oaI{15}",
                                     f"dev{ex}_{tsb}_{tt2-1}" if tt2 else
                                     (f"dev{ex}_{tsb-1}_3" if tsb else
                                      (f"dev{ex-1}_3_3" if ex else None))),
                              mile=f"dmm{dk}")
                        if tt2 == 3:
                            actr.consumed(f"dmm{dk}")
                        tg = tsb * 4 + tt2
                        p.add("vector", lambda e, ex=ex, tg=tg: (
                            e.scalar_tensor_tensor(
                                out=out_acc[:, tg, 0:512],
                                in0=p_fd[:, 0:512],
                                scalar=scores[:, tg, ex:ex + 1],
                                in1=out_acc[:, tg, 0:512],
                                op0=ALU.mult, op1=ALU.add),
                            e.scalar_tensor_tensor(
                                out=out_acc[:, tg, 512:1024],
                                in0=p_fd[:, 512:1024],
                                scalar=scores[:, tg, ex:ex + 1],
                                in1=out_acc[:, tg, 512:1024],
                                op0=ALU.mult, op1=ALU.add))[-1],
                            after=(f"dmm{dk}", f"oaI{tg}", f"scs{tg}"),
                            strict=True, mile=f"dev{ex}_{tsb}_{tt2}")
                p.ms[f"gmmE{ex}"] = p.ms[f"gmm{ex}_3_3E"]
                p.ms[f"ummE{ex}"] = p.ms[f"umm{ex}_3_3E"]
                p.ms[f"dmmE{ex}"] = p.ms[f"dmm{ex}_3_3"]

            if dbg:
                p.add("sync", lambda e: e.dma_start(
                    out=dbg_t["dbg_oaf"], in_=out_acc[:, 0, :]),
                    after=(f"dev{EXP-1}_0_0",), mile="DBGoaf", dma=True)
            outw = []
            for tt in range(16):
                p.add("vector", lambda e, tt=tt: e.reduce_sum(
                    out=fmc[:, 0:1], in_=out_acc[:, tt, :], axis=AX.X),
                    after=(f"dev{EXP-1}_{tt//4}_{tt%4}",), mile=f"Fms{tt}")
                p.add("vector", lambda e: e.tensor_scalar(
                    out=fmc[:, 1:2], in0=fmc[:, 0:1], scalar1=INV_D,
                    scalar2=None, op0=ALU.mult),
                    after=(f"Fms{tt}",), strict=True, mile=f"Fm2{tt}")
                p.add("vector", lambda e, tt=tt: e.tensor_scalar(
                    out=flw[:], in0=out_acc[:, tt, :], scalar1=fmc[:, 1:2],
                    scalar2=None, op0=ALU.subtract),
                    after=(f"Fm2{tt}", f"OUT{tt-1}" if tt else "dbC"),
                    strict=True, mile=f"Fctr{tt}")
                p.add("scalar", lambda e: e.activation(
                    out=fsq[:], in_=flw[:, 0:512], func=AF.Square,
                    accum_out=fmc[:, 2:3]),
                    after=(f"Fctr{tt}",))
                p.add("scalar", lambda e: e.activation(
                    out=fsq[:], in_=flw[:, 512:1024], func=AF.Square,
                    accum_out=fmc[:, 3:4]), mile=f"Fvq{tt}")
                p.add("vector", lambda e: e.tensor_add(
                    out=fmc[:, 2:3], in0=fmc[:, 2:3], in1=fmc[:, 3:4]),
                    after=(f"Fvq{tt}",), strict=True, mile=f"Fva{tt}")
                p.add("vector", lambda e: e.tensor_scalar(
                    out=fmc[:, 2:3], in0=fmc[:, 2:3], scalar1=INV_D,
                    scalar2=EPS, op0=ALU.mult, op1=ALU.add),
                    after=(f"Fva{tt}",), strict=True, mile=f"Fv{tt}")
                p.add("scalar", lambda e: e.activation(
                    out=fmc[:, 3:4], in_=fmc[:, 2:3], func=AF.Sqrt),
                    after=(f"Fv{tt}",), mile=f"Fsd{tt}")
                p.add("vector", lambda e: e.reciprocal(
                    out=fmc[:, 2:3], in_=fmc[:, 3:4]),
                    after=(f"Fsd{tt}",), mile=f"Frc{tt}")
                p.add("vector", lambda e: e.tensor_scalar(
                    out=flw[:], in0=flw[:], scalar1=fmc[:, 2:3],
                    scalar2=None, op0=ALU.mult),
                    after=(f"Frc{tt}",), strict=True, mile=f"Fsc{tt}")
                p.add("sync", lambda e, tt=tt: e.dma_start(
                    out=out_d[tt * 128:(tt + 1) * 128, :], in_=flw[:]),
                    after=(f"Fsc{tt}",), mile=f"OUT{tt}", dma=True)
                outw.append(f"OUT{tt}")

        if stages < 4:
            attn.close()

        qs_total = p.cnt["qS"]
        qg_total = p.cnt["qG"]
        p.ops["sync"].append(([("qS", qs_total)], lambda e: e.nop(), None))
        if qg_total:
            p.ops["gpsimd"].append(
                ([("qG", qg_total)], lambda e: e.nop(), None))

        with nc.Block() as block:
            p.emit_all(block, semh)

    return nc


_NC_CACHE = {}


def _get_nc(stages=5, dbg=False):
    key = (stages, dbg)
    if key not in _NC_CACHE:
        _NC_CACHE[key] = build(stages, dbg)
    return _NC_CACHE[key]


def make_in_maps(inp):
    f32 = np.float32
    resid = np.asarray(inp["L0_attn_resid"], f32)
    in_wT = np.ascontiguousarray(np.asarray(inp["L0_attn_in_w"], f32).T)
    skip_wT = np.ascontiguousarray(np.asarray(inp["L0_attn_skip_w"], f32).T)
    normw8 = np.ascontiguousarray(
        np.asarray(inp["L0_attn_norm_w"], f32).reshape(DCH, 128).T)
    wd1T = np.asarray(inp["L0_attn_wd1"], f32).T
    wBT = np.asarray(inp["L0_attn_wB"], f32).T
    wCT = np.asarray(inp["L0_attn_wC"], f32).T
    wcat = np.ascontiguousarray(np.concatenate([wd1T, wBT, wCT], axis=1))
    wcatp0 = wcat.copy()
    wcatp0[:, 64:80] = 0.0
    wd2T = np.ascontiguousarray(np.asarray(inp["L0_attn_wd2"], f32).T)
    wd2b = np.ascontiguousarray(
        np.asarray(inp["L0_attn_wd2_b"], f32).reshape(EJ, 128).T)
    convw = np.ascontiguousarray(
        np.asarray(inp["L0_attn_conv_w"], f32).reshape(EJ, 128, DCV)
        .transpose(1, 0, 2))
    convb = np.ascontiguousarray(
        np.asarray(inp["L0_attn_conv_b"], f32).reshape(EJ, 128).T)
    alog = np.ascontiguousarray(
        np.asarray(inp["L0_attn_A_log"], f32).reshape(EJ, 128, NS)
        .transpose(1, 0, 2))
    wdv = np.ascontiguousarray(
        np.asarray(inp["L0_attn_W_D"], f32).reshape(EJ, 128).T)
    out_wT = np.ascontiguousarray(np.asarray(inp["L0_attn_out_w"], f32).T)
    router = np.ascontiguousarray(np.asarray(inp["L0_ffn_router_weight"], f32))
    gup = np.asarray(inp["L0_ffn_gate_up_proj"], f32)
    gatew = np.ascontiguousarray(gup[:, :, 0::2])
    upw = np.ascontiguousarray(gup[:, :, 1::2])
    gupb = np.asarray(inp["L0_ffn_gate_up_proj_bias"], f32)
    gateb = np.ascontiguousarray(
        gupb[:, 0::2].reshape(EXP, 4, 128).transpose(2, 0, 1))
    upb = np.ascontiguousarray(
        gupb[:, 1::2].reshape(EXP, 4, 128).transpose(2, 0, 1))
    downw = np.ascontiguousarray(np.asarray(inp["L0_ffn_down_proj"], f32))
    downb = np.ascontiguousarray(np.asarray(inp["L0_ffn_down_proj_bias"], f32))
    zeros = np.zeros((TOK, D), f32)
    fconsts = np.tile(np.array(CONST_VALS + (0.0,) * (8 - len(CONST_VALS)),
                               f32), (128, 1))

    shared = dict(in_wT=in_wT, skip_wT=skip_wT, normw8=normw8, wcatT=wcat,
                  wd2T=wd2T, wd2b=wd2b, convw=convw, convb=convb, alog=alog,
                  wdvec=wdv, out_wT=out_wT, router=router, gatew=gatew,
                  fconsts=fconsts,
                  upw=upw, gateb=gateb, upb=upb, downw=downw, downb=downb)
    in_maps = []
    for c in range(8):
        b, r = c // 2, c % 2
        m = dict(shared)
        m["resid_own"] = np.ascontiguousarray(resid[b, r * TOK:(r + 1) * TOK])
        m["resid_pfx"] = zeros if r == 0 else np.ascontiguousarray(
            resid[b, 0:TOK])
        m["wcatTp"] = wcatp0 if r == 0 else wcat
        in_maps.append(m)
    return in_maps


def kernel(**inp):
    nc = _get_nc(5, False)
    in_maps = make_in_maps(inp)
    res = run_bass_kernel_spmd(nc, in_maps, core_ids=list(range(8)))
    out = np.empty((B, L, D), np.float32)
    for c in range(8):
        b, r = c // 2, c % 2
        out[b, r * TOK:(r + 1) * TOK] = res.results[c]["out"]
    return out
